# version 20
# baseline (speedup 1.0000x reference)
import contextlib
import numpy as np
import concourse.bass as bass
import concourse.mybir as mybir
from concourse.bass_utils import run_bass_kernel_spmd

F32 = mybir.dt.float32
BF16 = mybir.dt.bfloat16
I32 = mybir.dt.int32
AF = mybir.ActivationFunctionType
ALU = mybir.AluOpType
AX = mybir.AxisListType

ENGS = ("tensor", "vector", "scalar", "sync", "gpsimd")

S_LEN = 2048
D = 1024
NSEQ = 2
T = NSEQ * S_LEN
H = 16
DH = 64
NEG_BIG = -1.0e30
LN_EPS = 1.0e-5



class Sem:
    def __init__(self, handle):
        self.h = handle
        self.n = 0

    def bump(self, k):
        self.n += k
        return (self, self.n)


class Prog:

    def __init__(self, nc, stack):
        self.nc = nc
        self.stack = stack
        self.ops = {e: [] for e in ENGS}
        self._lastwait = {e: {} for e in ENGS}
        self._sems = {}

    def sem(self, name):
        if name not in self._sems:
            self._sems[name] = Sem(self.stack.enter_context(self.nc.semaphore(name)))
        return self._sems[name]

    def _wait(self, eng, tok):
        if tok is None:
            return
        sem, val = tok
        if val <= self._lastwait[eng].get(sem, 0):
            return
        self._lastwait[eng][sem] = val
        self.ops[eng].append(lambda E, s=sem.h, v=val: E.wait_ge(s, v))

    def on(self, eng, build, waits=(), inc=None, k=1):
        for t in waits:
            self._wait(eng, t)
        tok = None
        if inc is not None:
            tok = inc.bump(k)

            def fn(E, b=build, s=inc.h, kk=k):
                b(E).then_inc(s, kk)
        else:
            def fn(E, b=build):
                b(E)
        self.ops[eng].append(fn)
        return tok

    def raw(self, eng, fn):
        self.ops[eng].append(fn)

    def barrier(self, name):
        s = self.sem(name)
        target = s.n + len(ENGS)
        for e in ENGS:
            self.ops[e].append(lambda E, ss=s.h: E.sem_inc(ss, 1))
        s.n = target
        for e in ENGS:
            self._wait(e, (s, target))

    def finalize(self):
        with self.nc.Block() as block:
            for eng in ENGS:
                getattr(block, eng)(
                    lambda E, ops=self.ops[eng]: [fn(E) for fn in ops])


class PsumTracker:
    def __init__(self, nbanks=8):
        self.writer = [None] * nbanks
        self.readers = [[] for _ in range(nbanks)]

    def write_waits(self, banks):
        w = []
        for b in banks:
            w.extend(self.readers[b])
            if self.writer[b] is not None:
                w.append(self.writer[b])
        return w

    def wrote(self, banks, tok):
        for b in banks:
            self.writer[b] = tok
            self.readers[b] = []

    def read_waits(self, banks):
        return [self.writer[b] for b in banks if self.writer[b] is not None]

    def read(self, banks, tok):
        for b in banks:
            self.readers[b].append(tok)


def ts(i, n):
    return bass.ts(i, n)


def load_w_bf16(p, dst, src, sem, waits=()):
    c = dst.shape[1]
    srcv = src.rearrange("(c p) n -> p c n", p=128, c=c)
    return p.on("gpsimd", lambda E, d=dst, s=srcv: E.dma_start(d[:], s),
                waits=waits, inc=sem, k=16)



def emit_tile_transpose(p, pst, ps, cast_slot_ap, dst_ap, banks, ident_b,
                        sem_te, sem_sc, first_wait):
    tok = None
    for dc in range(8):
        bank = banks[dc // 4]
        col = (dc % 4) * 128
        def mk(E, b=bank, c=col, dcc=dc):
            return E.matmul(ps[:, b, c:c + 128], cast_slot_ap[:, ts(dcc, 128)],
                            ident_b[:], start=True, stop=True)
        w = []
        if dc == 0:
            w = list(first_wait) + pst.write_waits([banks[0]])
        elif dc == 4:
            w = pst.write_waits([banks[1]])
        tok = p.on("tensor", mk, waits=w, inc=sem_te if dc == 7 else None)
    pst.wrote(list(banks), tok)
    def ev(E):
        return E.copy(dst_ap,
                      ps[:, banks[0]:banks[0] + 2, :].rearrange(
                          "p c (d n) -> p (c d) n", d=4, n=128))
    t_ev = p.on("scalar", ev, waits=[tok], inc=sem_sc)
    pst.read(list(banks), t_ev)
    return tok, t_ev



def emit_residual_ln(p, pst, ps, banks, res_tile, sm, out_hbm_ap, sems,
                     mm_tok, res_tok, extra_add=None, extra_tok=None, slot=0,
                     store_user=None):
    sem_v, sem_sc, sem_st = sems
    def mkpre(E, b0=banks[0], s2=res_tile):
        return E.tensor_tensor(sm["pretile"][:],
                               ps[:, b0:b0 + 2, :].rearrange("p c n -> p (c n)"),
                               s2, ALU.add)
    t_pre = p.on("vector", mkpre, waits=[mm_tok, res_tok], inc=sem_v)
    pst.read(banks, t_pre)
    if extra_add is not None:
        t_pre = p.on("vector",
                     lambda E: E.tensor_tensor(sm["pretile"][:], sm["pretile"][:],
                                               extra_add[:], ALU.add),
                     waits=[t_pre, extra_tok], inc=sem_v)
    t_s1 = p.on("vector",
                lambda E: E.reduce_sum(sm["negs1"][:], sm["pretile"][:], AX.X, negate=True),
                waits=[t_pre], inc=sem_v)
    t_mu = p.on("vector",
                lambda E: E.tensor_scalar(sm["negmu"][:], sm["negs1"][:],
                                          1.0 / D, None, ALU.mult),
                waits=[t_s1], inc=sem_v)
    t_d = p.on("vector",
               lambda E: E.tensor_scalar(sm["pretile"][:], sm["pretile"][:],
                                         sm["negmu"][:], None, ALU.add),
               waits=[t_mu], inc=sem_v)
    t_sq = p.on("scalar",
                lambda E: E.activation(sm["sqtile"][:], sm["pretile"][:],
                                       AF.Square, accum_out=sm["s2"][:]),
                waits=[t_d], inc=sem_sc)
    t_ln = p.on("scalar",
                lambda E: E.activation(sm["lnv"][:], sm["s2"][:],
                                       AF.Ln, bias=sm["eps"][:], scale=1.0 / D),
                waits=[t_sq], inc=sem_sc)
    t_rs = p.on("scalar",
                lambda E: E.activation(sm["rstd"][:], sm["lnv"][:],
                                       AF.Exp, scale=-0.5),
                waits=[t_ln], inc=sem_sc)
    ot = sm["otile"][:, 0, :]
    w = [t_rs]
    if store_user is not None:
        w.extend(t for t in (store_user if isinstance(store_user, list) else [store_user])
                 if t is not None)
    t_o = p.on("vector",
               lambda E, o=ot: E.tensor_scalar(o, sm["pretile"][:],
                                               sm["rstd"][:], None, ALU.mult),
               waits=w, inc=sem_v)
    t_st = p.on("sync", lambda E, d=out_hbm_ap, o=ot: E.dma_start(d, o),
                waits=[t_o], inc=sem_st, k=16)
    return t_st


def emit_ln_sbuf(p, sm, src_ap, out_hbm_ap, sems, src_tok, slot=0,
                 store_user=None):
    sem_v, sem_sc, sem_st = sems
    t_s1 = p.on("vector",
                lambda E: E.reduce_sum(sm["negs1"][:], src_ap, AX.X, negate=True),
                waits=[src_tok], inc=sem_v)
    t_mu = p.on("vector",
                lambda E: E.tensor_scalar(sm["negmu"][:], sm["negs1"][:],
                                          1.0 / D, None, ALU.mult),
                waits=[t_s1], inc=sem_v)
    t_d = p.on("vector",
               lambda E: E.tensor_scalar(src_ap, src_ap,
                                         sm["negmu"][:], None, ALU.add),
               waits=[t_mu], inc=sem_v)
    t_sq = p.on("scalar",
                lambda E: E.activation(sm["sqtile"][:], src_ap,
                                       AF.Square, accum_out=sm["s2"][:]),
                waits=[t_d], inc=sem_sc)
    t_ln = p.on("scalar",
                lambda E: E.activation(sm["lnv"][:], sm["s2"][:],
                                       AF.Ln, bias=sm["eps"][:], scale=1.0 / D),
                waits=[t_sq], inc=sem_sc)
    t_rs = p.on("scalar",
                lambda E: E.activation(sm["rstd"][:], sm["lnv"][:],
                                       AF.Exp, scale=-0.5),
                waits=[t_ln], inc=sem_sc)
    ot = sm["otile"][:, slot % sm["otile"].shape[1], :]
    w = [t_rs]
    if store_user is not None:
        w.extend(t for t in (store_user if isinstance(store_user, list) else [store_user])
                 if t is not None)
    t_o = p.on("vector",
               lambda E, o=ot: E.tensor_scalar(o, src_ap,
                                               sm["rstd"][:], None, ALU.mult),
               waits=w, inc=sem_v)
    t_st = p.on("sync", lambda E, d=out_hbm_ap, o=ot: E.dma_start(d, o),
                waits=[t_o], inc=sem_st, k=16)
    return t_o, t_st



def build_attn_phase(p, bufs, xin, wqt, wkt, wvt, wot, bout, xout, tag):
    nc = p.nc
    B = bufs
    ps = B["ps"]
    sm = B["sm"]
    sem_te = p.sem(tag + "_te")
    sem_v = p.sem(tag + "_v")
    sem_sc = p.sem(tag + "_sc")
    sem_wA = p.sem(tag + "_wA")
    sem_wB = p.sem(tag + "_wB")
    sem_x = [p.sem(tag + "_x0"), p.sem(tag + "_x1")]
    sem_xb = p.sem(tag + "_xb")
    sem_st = [p.sem(tag + "_st0"), p.sem(tag + "_st1")]
    pst = PsumTracker()

    xtile_user = [None, None, None, None]
    cast_user = [None, None, None, None]
    store_user = [None, None]

    bout_tok = None
    if bout is not None:
        brow = sm["otile"][0:1, 0, :]
        browb = sm["cast_b"][0:1, 0:1024]
        t_ld = p.on("sync", lambda E: E.dma_start(brow, bout[None, :]),
                    inc=sem_xb, k=16)
        t_c = p.on("vector", lambda E: E.tensor_copy(browb, brow),
                   waits=[t_ld], inc=sem_v)
        toks = []
        for half in range(2):
            def mk(E, hh=half):
                return E.matmul(ps[:, hh, :], sm["ones_row"][:],
                                sm["cast_b"][0:1, hh * 512:hh * 512 + 512],
                                start=True, stop=True)
            t_mm = p.on("tensor", mk, waits=[t_c] + pst.write_waits([half]), inc=sem_te)
            pst.wrote([half], t_mm)
            t_ev = p.on("scalar",
                        lambda E, hh=half, t=None: E.copy(B["boutbc"][:, ts(hh, 512)], ps[:, hh, :]),
                        waits=[t_mm], inc=sem_sc)
            pst.read([half], t_ev)
            toks.append(t_ev)
            last_mm = t_mm
        bout_tok = toks[-1]
        cast_user[0] = last_mm
        xtile_user[0] = last_mm

    for s in range(NSEQ):
        xin_s = xin[s * S_LEN:(s + 1) * S_LEN, :]
        xout_s = xout[s * S_LEN:(s + 1) * S_LEN, :]

        te_now = (sem_te, sem_te.n)
        tok_wk = load_w_bf16(p, B["wbufA"], wkt, sem_wA, waits=[te_now])
        tok_wv = load_w_bf16(p, B["wbufB"], wvt, sem_wB)

        xb_tok = [None] * 16
        for tt in range(16):
            sl = B["xtile"][:, tt % 2, :]
            t_ld = p.on("sync", lambda E, dd=sl, sv=xin_s[ts(tt, 128), :]: E.dma_start(dd, sv),
                        waits=[xtile_user[tt % 2]], inc=sem_x[tt % 2], k=16)
            cs = sm["cast_b"][:, ts(tt % 2, D)]
            t_c = p.on("vector", lambda E, dd=cs, s2=sl: E.tensor_copy(dd, s2),
                       waits=[t_ld, cast_user[tt % 2]], inc=sem_v)
            xtile_user[tt % 2] = t_c
            banks = (0, 1) if tt % 2 == 0 else (2, 3)
            tok_mm, t_ev = emit_tile_transpose(
                p, pst, ps, cs, B["xb"][:, 0:8, ts(tt, 128)], banks,
                B["ident_b"], sem_te, sem_sc, [t_c])
            cast_user[tt % 2] = tok_mm
            xb_tok[tt] = t_ev
        xb_ready = (sem_sc, sem_sc.n)

        for dc in range(8):
            banks = [0, 1, 2, 3] if dc % 2 == 0 else [4, 5, 6, 7]
            for dinc in range(8):
                for tg in range(4):
                    def mk(E, b=banks[tg], dcc=dc, di=dinc, tgg=tg):
                        return E.matmul(ps[:, b, :], B["wbufA"][:, di, ts(dcc, 128)],
                                        B["xb"][:, di, ts(tgg, 512)],
                                        start=(di == 0), stop=(di == 7))
                    w = []
                    if dinc == 0:
                        w = pst.write_waits([banks[tg]])
                        if dc == 0:
                            w = w + [tok_wk, xb_tok[4 * tg + 3]]
                    tok = p.on("tensor", mk, waits=w,
                               inc=sem_te if (dinc == 7 and tg == 3) else None)
            pst.wrote(banks, tok)
            for tg in range(4):
                t_ev = p.on("scalar",
                            lambda E, dcc=dc, tgg=tg, b=banks[tg]:
                            E.copy(B["kt"][:, dcc, ts(tgg, 512)], ps[:, b, :]),
                            waits=[tok], inc=sem_sc)
                pst.read([banks[tg]], t_ev)
        kt_ready = (sem_sc, sem_sc.n)

        for tt in range(16):
            bk = [(0, 1), (2, 3), (4, 5), (6, 7)][tt % 4]
            for dinc in range(8):
                for half in range(2):
                    def mk(E, b=bk[half], di=dinc, hh=half, ttt=tt):
                        return E.matmul(ps[:, b, :], B["xb"][:, di, ts(ttt, 128)],
                                        B["wbufB"][:, di, ts(hh, 512)],
                                        start=(di == 0), stop=(di == 7))
                    w = []
                    if dinc == 0:
                        w = pst.write_waits([bk[half]])
                        if half == 0:
                            w = w + [tok_wv, xb_tok[tt]]
                    tok = p.on("tensor", mk, waits=w,
                               inc=sem_te if (dinc == 7 and half == 1) else None)
            pst.wrote(list(bk), tok)
            t_ev = p.on("scalar",
                        lambda E, ttt=tt, b0=bk[0]:
                        E.copy(B["vv"][:, ttt, :],
                               ps[:, b0:b0 + 2, :].rearrange("p c n -> p (c n)")),
                        waits=[tok], inc=sem_sc)
            pst.read(list(bk), t_ev)
        v_ready = (sem_sc, sem_sc.n)

        te_now = (sem_te, sem_te.n)
        tok_wq = load_w_bf16(p, B["wbufA"], wqt, sem_wA, waits=[te_now])
        tok_wo = load_w_bf16(p, B["wbufB"], wot, sem_wB, waits=[te_now])

        pq_free = [None, None, None, None]
        ao_prev = None
        for qg in range(4):
            for dc in range(8):
                bank = dc
                for dinc in range(8):
                    def mk(E, b=bank, di=dinc, dcc=dc, q=qg):
                        return E.matmul(ps[:, b, :], B["wbufA"][:, di, ts(dcc, 128)],
                                        B["xb"][:, di, ts(q, 512)],
                                        start=(di == 0), stop=(di == 7))
                    w = []
                    if dinc == 0:
                        w = pst.write_waits([bank]) + ([tok_wq] if dc == 0 else [])
                    tok = p.on("tensor", mk, waits=w, inc=sem_te if dinc == 7 else None)
                pst.wrote([bank], tok)
                t_ev = p.on("scalar",
                            lambda E, dcc=dc, b=bank: E.activation(
                                B["qt"][:, dcc, :], ps[:, b, :], AF.Copy, scale=0.125),
                            waits=[tok], inc=sem_sc)
                pst.read([bank], t_ev)
            qt_ready = (sem_sc, sem_sc.n)

            for h in range(H):
                c = h // 2
                off = (h % 2) * 64
                qt_state = {}

                def pq_slot(qt, qg=qg):
                    if qg == 0 and qt >= 2:
                        return ("pqs", qt - 2, qt)
                    sid = qt if qg == 0 else qt % 2
                    return ("pq", sid, sid)

                def emit_S(qt, c=c, off=off, qg=qg, h=h):
                    k_hi = qg * 512 + qt * 128 + 128
                    nb = (k_hi + 511) // 512
                    if qg == 0:
                        sbanks = [qt]
                    elif qg == 1:
                        sbanks = [0, 1] if qt % 2 == 0 else [4, 5]
                    else:
                        sbanks = [0, 1, 2, 3] if qt % 2 == 0 else [4, 5, 6, 7]
                    sb_used = sbanks[:nb]
                    kb_toks = []
                    for kb in range(nb):
                        n = min(512, k_hi - kb * 512)
                        def mk(E, b=sbanks[kb], kbb=kb, nn=n, cc=c, oo=off, qtt=qt):
                            return E.matmul(ps[:, b, 0:nn],
                                            B["qt"][oo:oo + 64, cc, ts(qtt, 128)],
                                            B["kt"][oo:oo + 64, cc, kbb * 512:kbb * 512 + nn],
                                            start=True, stop=True)
                        w = pst.write_waits([sbanks[kb]])
                        if h == 0 and qt == 0 and kb == 0:
                            w = w + [qt_ready, kt_ready]
                        kb_toks.append(p.on("tensor", mk, waits=w, inc=sem_te))
                    db = sbanks[(k_hi - 128) // 512]
                    dcol = (k_hi - 128) % 512
                    t_m = p.on("vector",
                               lambda E, b=db, col=dcol: E.tensor_tensor(
                                   ps[:, b, col:col + 128], ps[:, b, col:col + 128],
                                   B["mask_ut"][:], ALU.add),
                               waits=[kb_toks[-1]], inc=sem_v)
                    pst.wrote(sb_used, t_m)
                    t_pm = None
                    for kb in range(nb):
                        n = min(512, k_hi - kb * 512)
                        wv = [kb_toks[kb]] if kb < nb - 1 else [t_m]
                        t_pm = p.on("vector",
                                    lambda E, b=sbanks[kb], nn=n, kbb=kb, q=qt:
                                    E.reduce_max(sm["pmax"][:, q * 4 + kbb:q * 4 + kbb + 1],
                                                 ps[:, b, 0:nn], AX.X),
                                    waits=wv, inc=sem_v)
                        pst.read([sbanks[kb]], t_pm)
                    t_nm = p.on("vector",
                                lambda E, q=qt, nbb=nb: E.reduce_max(
                                    sm["negm"][:, q:q + 1],
                                    sm["pmax"][:, q * 4:q * 4 + nbb],
                                    AX.X, negate=True),
                                waits=[t_pm], inc=sem_v)
                    qt_state[qt] = (k_hi, sbanks, t_nm)

                def emit_expinv(qt, h=h):
                    k_hi, sbanks, t_nm = qt_state[qt]
                    nb = (k_hi + 511) // 512
                    if nb >= 3:
                        t_exp = None
                        for kb in range(nb):
                            n = min(512, k_hi - kb * 512)
                            buf, sid, tid = pq_slot(qt)
                            def mkexp(E, b=sbanks[kb], kbb=kb, nn=n, q=qt, bu=buf, si=sid):
                                return E.activation(
                                    B[bu][:, si, kbb * 512:kbb * 512 + nn],
                                    ps[:, b, 0:nn],
                                    AF.Exp, bias=sm["negm"][:, q:q + 1], scale=1.0,
                                    accum_out=sm["lpart"][:, q * 4 + kbb:q * 4 + kbb + 1])
                            w = [t_nm]
                            if kb == 0 and pq_free[tid] is not None:
                                w = w + [pq_free[tid]]
                            t_exp = p.on("scalar", mkexp, waits=w, inc=sem_sc)
                            pst.read([sbanks[kb]], t_exp)
                        t_l = p.on("vector",
                                   lambda E, q=qt, nbb=nb: E.reduce_sum(
                                       sm["l"][:, q:q + 1], sm["lpart"][:, q * 4:q * 4 + nbb],
                                       AX.X),
                                   waits=[t_exp], inc=sem_v)
                        t_inv = p.on("vector",
                                     lambda E, q=qt: E.reciprocal(sm["invl"][:, q:q + 1],
                                                                  sm["l"][:, q:q + 1]),
                                     waits=[t_l], inc=sem_v)
                    else:
                        buf, sid, tid = pq_slot(qt)
                        def mkexp(E, b0=sbanks[0], kh=k_hi, q=qt, bu=buf, si=sid):
                            return E.activation(
                                B[bu][:, si, 0:kh],
                                ps[:, b0:b0 + nb, :].rearrange("p c n -> p (c n)")[:, 0:kh]
                                if nb > 1 else ps[:, b0, 0:kh],
                                AF.Exp, bias=sm["negm"][:, q:q + 1], scale=1.0,
                                accum_out=sm["l"][:, q:q + 1])
                        w = [t_nm]
                        if pq_free[tid] is not None:
                            w = w + [pq_free[tid]]
                        t_exp = p.on("scalar", mkexp, waits=w, inc=sem_sc)
                        pst.read(sbanks[:nb], t_exp)
                        t_inv = p.on("vector",
                                     lambda E, q=qt: E.reciprocal(sm["invl"][:, q:q + 1],
                                                                  sm["l"][:, q:q + 1]),
                                     waits=[t_exp], inc=sem_v)
                    t_dg = p.on("vector",
                                lambda E, q=qt: E.tensor_scalar(
                                    sm["diag_b"][:, q, :], B["ident_b"][:],
                                    sm["invl"][:, q:q + 1], None, ALU.mult),
                                waits=[t_inv], inc=sem_v)
                    qt_state[qt] = (k_hi, sbanks, t_dg)

                def emit_PT(qt, h=h):
                    k_hi, sbanks, t_dg = qt_state[qt]
                    kc_n = k_hi // 128
                    ev_banks = []
                    tok = None
                    buf, sid, tid = pq_slot(qt)
                    for kc in range(kc_n):
                        bank = sbanks[min(kc // 4, len(sbanks) - 1)]
                        col = (kc % 4) * 128
                        def mk(E, b=bank, c2=col, kcc=kc, q=qt, bu=buf, si=sid):
                            return E.matmul(ps[:, b, c2:c2 + 128],
                                            B[bu][:, si, ts(kcc, 128)],
                                            sm["diag_b"][:, q, :],
                                            start=True, stop=True)
                        w = []
                        if kc == 0:
                            w = [t_dg]
                        if kc % 4 == 0:
                            w = w + pst.write_waits([bank])
                            ev_banks.append(bank)
                        tok = p.on("tensor", mk, waits=w,
                                   inc=sem_te if kc == kc_n - 1 else None)
                    pst.wrote(ev_banks, tok)
                    pq_free[tid] = tok
                    for bi, bank in enumerate(ev_banks):
                        n_kc = min(4, kc_n - bi * 4)
                        eng = "vector" if (h + bi) % 2 == 0 else "scalar"
                        isem = sem_v if eng == "vector" else sem_sc
                        if eng == "vector":
                            def ev(E, b=bank, bi2=bi, nk=n_kc, q=qt):
                                return E.tensor_copy(
                                    B["pt"][:, bi2 * 4:bi2 * 4 + nk, ts(q, 128)],
                                    ps[:, b, 0:nk * 128].rearrange("p (c n) -> p c n", n=128))
                        else:
                            def ev(E, b=bank, bi2=bi, nk=n_kc, q=qt):
                                return E.copy(
                                    B["pt"][:, bi2 * 4:bi2 * 4 + nk, ts(q, 128)],
                                    ps[:, b, 0:nk * 128].rearrange("p (c n) -> p c n", n=128))
                        t_ev = p.on(eng, ev, waits=[tok], inc=isem)
                        pst.read([bank], t_ev)

                if qg == 0:
                    for qt in range(4):
                        emit_S(qt)
                    for qt in range(4):
                        emit_expinv(qt)
                    for qt in range(4):
                        emit_PT(qt)
                else:
                    emit_S(0)
                    emit_S(1)
                    emit_expinv(0)
                    emit_PT(0)
                    emit_expinv(1)
                    emit_S(2)
                    emit_PT(1)
                    emit_expinv(2)
                    emit_S(3)
                    emit_PT(2)
                    emit_expinv(3)
                    emit_PT(3)
                pt_ready = (sem_v, sem_v.n)
                pt_ready_sc = (sem_sc, sem_sc.n)

                kc_all = (qg + 1) * 4
                pv_bank = {0: 6, 1: 6, 2: 3, 3: 3}[qg]
                for kc in range(kc_all):
                    qs = max(0, kc * 128 - qg * 512)
                    def mk(E, kcc=kc, qss=qs, hh=h, ka=kc_all, pvb=pv_bank):
                        return E.matmul(ps[0:64, pvb, qss:512],
                                        B["vv"][:, kcc, hh * 64:hh * 64 + 64],
                                        B["pt"][:, kcc, qss:512],
                                        start=(kcc == 0), stop=(kcc == ka - 1))
                    w = []
                    if kc == 0:
                        w = [pt_ready, pt_ready_sc] + pst.write_waits([pv_bank])
                        if h == 0 and qg == 0:
                            w = w + [v_ready]
                    tok = p.on("tensor", mk, waits=w, inc=sem_te if kc == kc_all - 1 else None)
                pst.wrote([pv_bank], tok)
                t_ev = p.on("scalar",
                            lambda E, cc=c, oo=off, pvb=pv_bank: E.copy(
                                B["obuf"][oo:oo + 64, cc, :], ps[0:64, pvb, :]),
                            waits=[tok], inc=sem_sc)
                pst.read([pv_bank], t_ev)
            obuf_ready = (sem_sc, sem_sc.n)

            for tq in range(4):
                tt = qg * 4 + tq
                banks = [2, 3] if tq % 2 == 0 else [4, 5]
                for dinc in range(8):
                    for half in range(2):
                        def mk(E, b=banks[half], di=dinc, hh=half, tqq=tq):
                            return E.matmul(ps[:, b, :], B["obuf"][:, di, ts(tqq, 128)],
                                            B["wbufB"][:, di, ts(hh, 512)],
                                            start=(di == 0), stop=(di == 7))
                        w = []
                        if dinc == 0:
                            w = pst.write_waits([banks[half]])
                            if half == 0 and tq == 0:
                                w = w + [obuf_ready]
                                if qg == 0:
                                    w = w + [tok_wo]
                        tok = p.on("tensor", mk, waits=w,
                                   inc=sem_te if (dinc == 7 and half == 1) else None)
                pst.wrote(banks, tok)
                sl = B["xtile"][:, tt % 2, :]
                t_ld = p.on("sync", lambda E, dd=sl, sv=xin_s[ts(tt, 128), :]: E.dma_start(dd, sv),
                            waits=[xtile_user[tt % 2]], inc=sem_x[tt % 2], k=16)
                t_st = emit_residual_ln(
                    p, pst, ps, banks, sl, sm, xout_s[ts(tt, 128), :],
                    (sem_v, sem_sc, sem_st[tt % 2]), tok, t_ld,
                    extra_add=B["boutbc"] if bout is not None else None,
                    extra_tok=bout_tok, slot=tt % 2, store_user=list(store_user))
                store_user[tt % 2] = t_st
                xtile_user[tt % 2] = t_st

    for ss in sem_st:
        p.raw("sync", lambda E, s2=ss.h, v=ss.n: E.wait_ge(s2, v))


def alloc_common(nc, stack):
    sb = lambda name, shape, dt: stack.enter_context(nc.sbuf_tensor(name, shape, dt))
    return {
        "ident_b": sb("c_identb", [128, 128], BF16),
        "ident_f": sb("c_identf", [128, 128], F32),
        "mask_ut": sb("c_mask", [128, 128], F32),
        "ones_row": sb("c_ones", [1, 128], BF16),
        "eps": sb("c_eps", [128, 1], F32),
    }


def alloc_ln_smalls(nc, stack, pre, common, with_pretile=True, otile_slots=1):
    sb = lambda name, shape, dt: stack.enter_context(nc.sbuf_tensor(name, shape, dt))
    d = {
        "pretile": sb(pre + "_pre", [128, 1024], F32) if with_pretile else None,
        "negs1": sb(pre + "_negs1", [128, 1], F32),
        "negmu": sb(pre + "_negmu", [128, 1], F32),
        "sqtile": sb(pre + "_sq", [128, 1024], F32),
        "s2": sb(pre + "_s2", [128, 1], F32),
        "lnv": sb(pre + "_lnv", [128, 1], F32),
        "rstd": sb(pre + "_rstd", [128, 1], F32),
        "otile": sb(pre + "_otile", [128, otile_slots, 1024], F32),
        "eps": common["eps"],
    }
    return d


def alloc_attn_bufs(nc, stack, ps, common, pre="A"):
    sb = lambda name, shape, dt: stack.enter_context(nc.sbuf_tensor(name, shape, dt))
    B = {
        "ps": ps,
        "ident_b": common["ident_b"],
        "mask_ut": common["mask_ut"],
        "xb": sb(pre + "_xb", [128, 8, 2048], BF16),
        "kt": sb(pre + "_kt", [128, 8, 2048], BF16),
        "vv": sb(pre + "_v", [128, 16, 1024], BF16),
        "qt": sb(pre + "_qt", [128, 8, 512], BF16),
        "pt": sb(pre + "_pt", [128, 16, 512], BF16),
        "pq": sb(pre + "_pq", [128, 2, 2048], BF16),
        "pqs": sb(pre + "_pqs", [128, 2, 512], BF16),
        "obuf": sb(pre + "_obuf", [128, 8, 512], BF16),
        "wbufA": sb(pre + "_wA", [128, 8, 1024], BF16),
        "wbufB": sb(pre + "_wB", [128, 8, 1024], BF16),
        "xtile": sb(pre + "_xtile", [128, 2, 1024], F32),
        "boutbc": sb(pre + "_boutbc", [128, 1024], F32),
    }
    B["sm"] = alloc_ln_smalls(nc, stack, pre + "ln", common, otile_slots=2)
    B["sm"].update({
        "cast_b": sb(pre + "_cast", [128, 2 * 1024], BF16),
        "negm": sb(pre + "_negm", [128, 4], F32),
        "pmax": sb(pre + "_pmax", [128, 16], F32),
        "lpart": sb(pre + "_lpart", [128, 16], F32),
        "l": sb(pre + "_l", [128, 4], F32),
        "invl": sb(pre + "_invl", [128, 4], F32),
        "diag_b": sb(pre + "_diagb", [128, 4, 128], BF16),
        "ones_row": common["ones_row"],
    })
    return B



def alloc_moe0_bufs(nc, stack, ps, common):
    sb = lambda name, shape, dt: stack.enter_context(nc.sbuf_tensor(name, shape, dt))
    B = {
        "ps": ps,
        "ident_b": common["ident_b"],
        "ident_f": common["ident_f"],
        "ones_row": common["ones_row"],
        "x1b": sb("B_x1b", [128, 8, 1024], BF16),
        "yacc": sb("B_yacc", [128, 8, 1024], F32),
        "hh": sb("B_h", [128, 16, 1024], BF16),
        "sg": sb("B_sg", [128, 2, 1024], BF16),
        "gate": sb("B_gate", [128, 2, 8, 512], BF16),
        "up": sb("B_up", [128, 2, 8, 512], BF16),
        "down": sb("B_down", [128, 16, 1024], BF16),
        "rwb": sb("B_rwb", [128, 8, 8], BF16),
        "rbbc": sb("B_rbbc", [128, 8], F32),
        "rbrow": sb("B_rbrow", [1, 8], F32),
        "rbrowb": sb("B_rbrowb", [1, 8], BF16),
        "dbb": sb("B_dbb", [8, 1024], BF16),
        "lg": sb("B_lg", [128, 8], F32),
        "p8": sb("B_p8", [128, 8], F32),
        "probs": sb("B_probs", [128, 8], F32),
        "eqm": sb("B_eqm", [128, 8], F32),
        "masked": sb("B_masked", [128, 8], F32),
        "keep": sb("B_keep", [128, 8], F32),
        "m1": sb("B_m1", [128, 1], F32),
        "m2": sb("B_m2", [128, 1], F32),
        "negm8": sb("B_negm8", [128, 1], F32),
        "se": sb("B_se", [128, 1], F32),
        "inv": sb("B_inv", [128, 1], F32),
        "s12": sb("B_s12", [128, 1], F32),
        "inv12": sb("B_inv12", [128, 1], F32),
        "wall": sb("B_wall", [128, 8, 8], F32),
        "wallb": sb("B_wallb", [128, 8, 8], BF16),
        "wT": sb("B_wT", [8, 8, 128], BF16),
        "gbt": sb("B_gbt", [16, 128], F32),
        "gbcol": sb("B_gbcol", [128, 16], F32),
        "ubt": sb("B_ubt", [16, 128], F32),
        "ubcol": sb("B_ubcol", [128, 16], F32),
    }
    B["sm"] = alloc_ln_smalls(nc, stack, "B", common, with_pretile=False, otile_slots=2)
    B["sm"]["cast_b"] = sb("B_cast", [128, 2 * 1024], BF16)
    return B


def build_moe0_phase(p, B, xin, xout, w, tag):
    ps = B["ps"]
    sm = B["sm"]
    sem_te = p.sem(tag + "_te")
    sem_v = p.sem(tag + "_v")
    sem_sc = p.sem(tag + "_sc")
    sem_yl = [p.sem(tag + "_yl%d" % i) for i in range(8)]
    sem_gs = [p.sem(tag + "_gs%d" % i) for i in range(2)]
    sem_us = [p.sem(tag + "_us%d" % i) for i in range(2)]
    sem_dn = p.sem(tag + "_dn")
    sem_gb = p.sem(tag + "_gb")
    sem_ub = p.sem(tag + "_ub")
    sem_rw = p.sem(tag + "_rw")
    sem_rb = p.sem(tag + "_rb")
    sem_db = p.sem(tag + "_db")
    sem_st = [p.sem(tag + "_st0"), p.sem(tag + "_st1")]
    pst = PsumTracker()

    t_rw = p.on("gpsimd", lambda E: E.dma_start(
        B["rwb"][:], w["rw"].rearrange("(c p) e -> p c e", p=128, c=8)),
        inc=sem_rw, k=16)
    t_rb = p.on("sync", lambda E: E.dma_start(B["rbrow"][:], w["rb"][None, :]),
                inc=sem_rb, k=16)
    t_rbb = p.on("vector", lambda E: E.tensor_copy(B["rbrowb"][:], B["rbrow"][:]),
                 waits=[t_rb], inc=sem_v)
    t_mm = p.on("tensor", lambda E: E.matmul(ps[:, 0, 0:8], B["ones_row"][:],
                                             B["rbrowb"][:], start=True, stop=True),
                waits=[t_rbb] + pst.write_waits([0]), inc=sem_te)
    pst.wrote([0], t_mm)
    t_rbbc = p.on("scalar", lambda E: E.copy(B["rbbc"][:], ps[:, 0, 0:8]),
                  waits=[t_mm], inc=sem_sc)
    pst.read([0], t_rbbc)
    t_db = p.on("gpsimd", lambda E: E.dma_start(B["dbb"][:], w["db"][:, :]),
                inc=sem_db, k=16)

    store_user = [None, None]
    yacc_user = [None] * 8
    wallb_user = [None] * 8
    sg_user = [None, None]
    gu_user = [None, None]
    down_user = None
    gbt_user = None
    ubt_user = None
    cast_user = [None, None]

    for swp in range(4):
        acc_tok = [None] * 8
        for tt in range(8):
            gt = swp * 8 + tt
            t_ld = p.on("sync",
                        lambda E, d=B["yacc"][:, tt, :], sv=xin[ts(gt, 128), :]:
                        E.dma_start(d, sv),
                        waits=[yacc_user[tt]], inc=sem_yl[tt], k=16)
            cs = sm["cast_b"][:, ts(tt % 2, D)]
            t_c = p.on("vector", lambda E, d=cs, s2=B["yacc"][:, tt, :]:
                       E.tensor_copy(d, s2),
                       waits=[t_ld, cast_user[tt % 2]], inc=sem_v)
            banks = (0, 1) if tt % 2 == 0 else (2, 3)
            tok_mm, t_ev = emit_tile_transpose(
                p, pst, ps, cs, B["x1b"][:, 0:8, ts(tt, 128)], banks,
                B["ident_b"], sem_te, sem_sc, [t_c])
            cast_user[tt % 2] = tok_mm
            for dinc in range(8):
                def mk(E, di=dinc, ttt=tt):
                    return E.matmul(ps[:, 4, 0:8], B["x1b"][:, di, ts(ttt, 128)],
                                    B["rwb"][:, di, :], start=(di == 0), stop=(di == 7))
                wts = []
                if dinc == 0:
                    wts = pst.write_waits([4]) + [t_ev, t_rw]
                t_r = p.on("tensor", mk, waits=wts, inc=sem_te if dinc == 7 else None)
            pst.wrote([4], t_r)
            t_lg = p.on("vector", lambda E: E.tensor_tensor(
                B["lg"][:], ps[:, 4, 0:8], B["rbbc"][:], ALU.add),
                waits=[t_r, t_rbbc], inc=sem_v)
            pst.read([4], t_lg)
            t_nm = p.on("vector", lambda E: E.reduce_max(
                B["negm8"][:], B["lg"][:], AX.X, negate=True), waits=[t_lg], inc=sem_v)
            t_p8 = p.on("scalar", lambda E: E.activation(
                B["p8"][:], B["lg"][:], AF.Exp, bias=B["negm8"][:],
                accum_out=B["se"][:]), waits=[t_nm], inc=sem_sc)
            t_inv = p.on("vector", lambda E: E.reciprocal(B["inv"][:], B["se"][:]),
                         waits=[t_p8], inc=sem_v)
            t_pr = p.on("vector", lambda E: E.tensor_scalar(
                B["probs"][:], B["p8"][:], B["inv"][:], None, ALU.mult),
                waits=[t_inv], inc=sem_v)
            t_m1 = p.on("vector", lambda E: E.reduce_max(B["m1"][:], B["probs"][:], AX.X),
                        waits=[t_pr], inc=sem_v)
            t_eq = p.on("vector", lambda E: E.tensor_scalar(
                B["eqm"][:], B["probs"][:], B["m1"][:], None, ALU.is_ge),
                waits=[t_m1], inc=sem_v)
            t_ms = p.on("vector", lambda E: E.scalar_tensor_tensor(
                B["masked"][:], B["eqm"][:], NEG_BIG, B["probs"][:],
                ALU.mult, ALU.add), waits=[t_eq], inc=sem_v)
            t_m2 = p.on("vector", lambda E: E.reduce_max(B["m2"][:], B["masked"][:], AX.X),
                        waits=[t_ms], inc=sem_v)
            t_kp = p.on("vector", lambda E: E.tensor_scalar(
                B["keep"][:], B["probs"][:], B["m2"][:], None, ALU.is_ge),
                waits=[t_m2], inc=sem_v)
            t_s12 = p.on("vector", lambda E: E.tensor_tensor(
                B["s12"][:], B["m1"][:], B["m2"][:], ALU.add), waits=[t_kp], inc=sem_v)
            t_i12 = p.on("vector", lambda E: E.reciprocal(B["inv12"][:], B["s12"][:]),
                         waits=[t_s12], inc=sem_v)
            t_w = p.on("vector", lambda E, ttt=tt: E.scalar_tensor_tensor(
                B["wall"][:, ttt, :], B["probs"][:], B["inv12"][:], B["keep"][:],
                ALU.mult, ALU.mult), waits=[t_i12], inc=sem_v)
            t_wb = p.on("vector", lambda E, ttt=tt: E.tensor_copy(
                B["wallb"][:, ttt, :], B["wall"][:, ttt, :]),
                waits=[t_w, wallb_user[tt]], inc=sem_v)
            t_wt = p.on("tensor", lambda E, ttt=tt: E.matmul(
                ps[0:8, 5, 0:128], B["wallb"][:, ttt, :], B["ident_b"][:],
                start=True, stop=True),
                waits=[t_wb] + pst.write_waits([5]), inc=sem_te)
            pst.wrote([5], t_wt)
            t_wte = p.on("scalar", lambda E, ttt=tt: E.copy(
                B["wT"][0:8, ttt, :], ps[0:8, 5, 0:128]), waits=[t_wt], inc=sem_sc)
            pst.read([5], t_wte)
            for half in range(2):
                def mk(E, hh=half, ttt=tt):
                    return E.matmul(ps[:, 6 + hh, :], B["wT"][0:8, ttt, :],
                                    B["dbb"][0:8, ts(hh, 512)], start=True, stop=True)
                wts = [t_wte] + pst.write_waits([6 + half]) + ([t_db] if tt == 0 and swp == 0 else [])
                t_ini = p.on("tensor", mk, waits=wts, inc=sem_te if half == 1 else None)
            pst.wrote([6, 7], t_ini)
            t_acc0 = p.on("vector", lambda E, ttt=tt: E.tensor_tensor(
                B["yacc"][:, ttt, :],
                ps[:, 6:8, :].rearrange("p c n -> p (c n)"),
                B["yacc"][:, ttt, :], ALU.add),
                waits=[t_ini, t_ld], inc=sem_v)
            pst.read([6, 7], t_acc0)
            acc_tok[tt] = t_acc0
            wallb_user[tt] = t_wt
        x1b_ready = (sem_sc, sem_sc.n)

        for e in range(8):
            t_gb = p.on("sync", lambda E, ee=e: E.dma_start(
                B["gbt"][:], w["gb"][ee].rearrange("(c p) -> c p", p=128)),
                waits=[gbt_user], inc=sem_gb, k=16)
            t_gbm = p.on("tensor", lambda E: E.matmul(
                ps[:, 4, 0:16], B["gbt"][0:16, :], B["ident_f"][0:16, 0:16],
                start=True, stop=True), waits=[t_gb] + pst.write_waits([4]), inc=sem_te)
            pst.wrote([4], t_gbm)
            t_gbc = p.on("scalar", lambda E: E.copy(B["gbcol"][:], ps[:, 4, 0:16]),
                         waits=[t_gbm], inc=sem_sc)
            pst.read([4], t_gbc)
            t_ub = p.on("sync", lambda E, ee=e: E.dma_start(
                B["ubt"][:], w["ub"][ee].rearrange("(c p) -> c p", p=128)),
                waits=[ubt_user], inc=sem_ub, k=16)
            t_ubm = p.on("tensor", lambda E: E.matmul(
                ps[:, 5, 0:16], B["ubt"][0:16, :], B["ident_f"][0:16, 0:16],
                start=True, stop=True), waits=[t_ub] + pst.write_waits([5]), inc=sem_te)
            pst.wrote([5], t_ubm)
            t_ubc = p.on("scalar", lambda E: E.copy(B["ubcol"][:], ps[:, 5, 0:16]),
                         waits=[t_ubm], inc=sem_sc)
            pst.read([5], t_ubc)
            gbt_user = t_gbm
            ubt_user = t_ubm

            slice_tok = {}
            for isl in range(2):
                slice_tok[isl] = (
                    p.on("gpsimd", lambda E, ee=e, ii=isl: E.dma_start(
                        B["gate"][:, ii % 2, :, :],
                        w["gw"][ee][:, ts(ii, 512)].rearrange("(c p) i -> p c i", p=128, c=8)),
                        waits=[gu_user[isl % 2]], inc=sem_gs[isl % 2], k=16),
                    p.on("gpsimd", lambda E, ee=e, ii=isl: E.dma_start(
                        B["up"][:, ii % 2, :, :],
                        w["uw"][ee][:, ts(ii, 512)].rearrange("(c p) i -> p c i", p=128, c=8)),
                        inc=sem_us[isl % 2], k=16),
                )

            t_dw = p.on("gpsimd", lambda E, ee=e: E.dma_start(
                B["down"][:], w["dw"][ee].rearrange("(c p) n -> p c n", p=128, c=16)),
                waits=[down_user], inc=sem_dn, k=16)

            for ic in range(16):
                isl = ic // 4
                il = ic % 4
                slot = isl % 2
                gb_banks = (0, 1) if ic % 2 == 0 else (4, 5)
                ub_banks = (2, 3) if ic % 2 == 0 else (6, 7)
                gtok, utok = slice_tok[isl]
                for dinc in range(8):
                    for tg in range(2):
                        def mk(E, b=gb_banks[tg], di=dinc, sl=slot, ill=il, tgg=tg):
                            return E.matmul(ps[:, b, :],
                                            B["gate"][:, sl, di, ts(ill, 128)][:, 0:128],
                                            B["x1b"][:, di, ts(tgg, 512)],
                                            start=(di == 0), stop=(di == 7))
                        wts = []
                        if dinc == 0:
                            wts = pst.write_waits([gb_banks[tg]])
                            if tg == 0:
                                wts = wts + [gtok, x1b_ready]
                        t_g = p.on("tensor", mk, waits=wts,
                                   inc=sem_te if (dinc == 7 and tg == 1) else None)
                pst.wrote(list(gb_banks), t_g)
                for tg in range(2):
                    def mksilu(E, b=gb_banks[tg], icc=ic, tgg=tg):
                        return E.activation(B["sg"][:, icc % 2, ts(tgg, 512)],
                                            ps[:, b, :], AF.Silu,
                                            bias=B["gbcol"][:, icc:icc + 1])
                    t_sg = p.on("scalar", mksilu,
                                waits=[t_g, t_gbc] + ([sg_user[ic % 2]] if tg == 0 else []),
                                inc=sem_sc)
                    pst.read([gb_banks[tg]], t_sg)
                for dinc in range(8):
                    for tg in range(2):
                        def mk(E, b=ub_banks[tg], di=dinc, sl=slot, ill=il, tgg=tg):
                            return E.matmul(ps[:, b, :],
                                            B["up"][:, sl, di, ts(ill, 128)][:, 0:128],
                                            B["x1b"][:, di, ts(tgg, 512)],
                                            start=(di == 0), stop=(di == 7))
                        wts = []
                        if dinc == 0:
                            wts = pst.write_waits([ub_banks[tg]])
                            if tg == 0:
                                wts = wts + [utok]
                        t_u = p.on("tensor", mk, waits=wts,
                                   inc=sem_te if (dinc == 7 and tg == 1) else None)
                pst.wrote(list(ub_banks), t_u)
                sg_done = None
                for tg in range(2):
                    def mkh(E, b=ub_banks[tg], icc=ic, tgg=tg):
                        return E.scalar_tensor_tensor(
                            B["hh"][:, icc, ts(tgg, 512)], ps[:, b, :],
                            B["ubcol"][:, icc:icc + 1], B["sg"][:, icc % 2, ts(tgg, 512)],
                            ALU.add, ALU.mult)
                    t_h = p.on("vector", mkh, waits=[t_u, t_ubc, t_sg], inc=sem_v)
                    pst.read([ub_banks[tg]], t_h)
                sg_user[ic % 2] = t_h
                if il == 3 and isl + 2 < 4:
                    te_now = (sem_te, sem_te.n)
                    gu_user[slot] = te_now
                    slice_tok[isl + 2] = (
                        p.on("gpsimd", lambda E, ee=e, ii=isl + 2: E.dma_start(
                            B["gate"][:, ii % 2, :, :],
                            w["gw"][ee][:, ts(ii, 512)].rearrange("(c p) i -> p c i", p=128, c=8)),
                            waits=[te_now], inc=sem_gs[(isl + 2) % 2], k=16),
                        p.on("gpsimd", lambda E, ee=e, ii=isl + 2: E.dma_start(
                            B["up"][:, ii % 2, :, :],
                            w["uw"][ee][:, ts(ii, 512)].rearrange("(c p) i -> p c i", p=128, c=8)),
                            inc=sem_us[(isl + 2) % 2], k=16),
                    )
                elif il == 3:
                    gu_user[slot] = (sem_te, sem_te.n)
            h_ready = (sem_v, sem_v.n)

            for tt in range(8):
                bk = [(0, 1), (2, 3), (4, 5), (6, 7)][tt % 4]
                for ic in range(16):
                    for half in range(2):
                        def mk(E, b=bk[half], icc=ic, hh=half, ttt=tt):
                            return E.matmul(ps[:, b, :], B["hh"][:, icc, ts(ttt, 128)],
                                            B["down"][:, icc, ts(hh, 512)],
                                            start=(icc == 0), stop=(icc == 15))
                        wts = []
                        if ic == 0:
                            wts = pst.write_waits([bk[half]])
                            if half == 0:
                                wts = wts + ([h_ready, t_dw] if tt == 0 else [])
                        t_dn = p.on("tensor", mk, waits=wts,
                                    inc=sem_te if (ic == 15 and half == 1) else None)
                pst.wrote(list(bk), t_dn)
                t_acc = p.on("vector", lambda E, ttt=tt, ee=e, b0=bk[0]:
                             E.scalar_tensor_tensor(
                                 B["yacc"][:, ttt, :],
                                 ps[:, b0:b0 + 2, :].rearrange("p c n -> p (c n)"),
                                 B["wall"][:, ttt, ee:ee + 1],
                                 B["yacc"][:, ttt, :], ALU.mult, ALU.add),
                             waits=[t_dn, acc_tok[tt]], inc=sem_v)
                pst.read(list(bk), t_acc)
                acc_tok[tt] = t_acc
            down_user = (sem_te, sem_te.n)

        for tt in range(8):
            gt = swp * 8 + tt
            t_o, t_st = emit_ln_sbuf(p, sm, B["yacc"][:, tt, :], xout[ts(gt, 128), :],
                                     (sem_v, sem_sc, sem_st[tt % 2]), acc_tok[tt],
                                     slot=tt % 2, store_user=list(store_user))
            store_user[tt % 2] = t_st
            yacc_user[tt] = t_o

    for ss in sem_st:
        p.raw("sync", lambda E, s2=ss.h, v=ss.n: E.wait_ge(s2, v))



def alloc_ffn1_bufs(nc, stack, ps, common):
    sb = lambda name, shape, dt: stack.enter_context(nc.sbuf_tensor(name, shape, dt))
    B = {
        "ps": ps,
        "ident_b": common["ident_b"],
        "ones_row": common["ones_row"],
        "x3b": sb("D_x3b", [128, 8, 1024], BF16),
        "yacc": sb("D_yacc", [128, 8, 1024], F32),
        "hh": sb("D_h", [128, 32, 1024], BF16),
        "w1": sb("D_w1", [128, 2, 8, 256], BF16),
        "w2": sb("D_w2", [128, 4, 4, 1024], BF16),
        "b1t": sb("D_b1t", [32, 128], BF16),
        "b1col": sb("D_b1col", [128, 32], F32),
        "b2b": sb("D_b2b", [8, 1024], BF16),
        "sall": sb("D_sall", [128, 8], F32),
        "habs": sb("D_habs", [128, 8], F32),
        "hint": sb("D_hint", [128, 8], I32),
        "hcf": sb("D_hcf", [128, 8], F32),
        "hfl": sb("D_hfl", [128, 8], F32),
        "hsgn": sb("D_hsgn", [128, 8], F32),
        "htr": sb("D_htr", [128, 8], F32),
        "hall": sb("D_hall", [128, 8], F32),
        "eqt": sb("D_eqt", [128, 8], F32),
        "eq2": sb("D_eq2", [128, 8], F32),
        "wall": sb("D_wall", [128, 8, 8], F32),
        "wallb": sb("D_wallb", [128, 8, 8], BF16),
        "wT": sb("D_wT", [8, 8, 128], BF16),
    }
    B["sm"] = alloc_ln_smalls(nc, stack, "D", common, with_pretile=False, otile_slots=2)
    B["sm"]["cast_b"] = sb("D_cast", [128, 2 * 1024], BF16)
    return B


def build_ffn1_phase(p, B, xin, xout, w, tag):
    ps = B["ps"]
    sm = B["sm"]
    sem_te = p.sem(tag + "_te")
    sem_v = p.sem(tag + "_v")
    sem_sc = p.sem(tag + "_sc")
    sem_yl = [p.sem(tag + "_yl%d" % i) for i in range(8)]
    sem_w1 = [p.sem(tag + "_w1s%d" % i) for i in range(4)]
    sem_w2 = [p.sem(tag + "_w2s%d" % i) for i in range(4)]
    sem_b1 = p.sem(tag + "_b1")
    sem_b2 = p.sem(tag + "_b2")
    sem_st = [p.sem(tag + "_st0"), p.sem(tag + "_st1")]
    pst = PsumTracker()

    t_b2 = p.on("gpsimd", lambda E: E.dma_start(B["b2b"][:], w["b2"][:, :]),
                inc=sem_b2, k=16)

    store_user = [None, None]
    yacc_user = [None] * 8
    wallb_user = [None] * 8
    w1_user = [None, None, None, None]
    w2s_user = [None, None, None, None]
    b1t_user = None
    cast_user = [None, None]

    for swp in range(4):
        acc_tok = [None] * 8
        ld_toks = [None] * 8
        for tt in range(8):
            gt = swp * 8 + tt
            t_ld = p.on("sync",
                        lambda E, d=B["yacc"][:, tt, :], sv=xin[ts(gt, 128), :]:
                        E.dma_start(d, sv),
                        waits=[yacc_user[tt]], inc=sem_yl[tt], k=16)
            ld_toks[tt] = t_ld
            cs = sm["cast_b"][:, ts(tt % 2, D)]
            t_c = p.on("vector", lambda E, d=cs, s2=B["yacc"][:, tt, :]:
                       E.tensor_copy(d, s2),
                       waits=[t_ld, cast_user[tt % 2]], inc=sem_v)
            banks = (0, 1) if tt % 2 == 0 else (2, 3)
            tok_mm, t_ev = emit_tile_transpose(
                p, pst, ps, cs, B["x3b"][:, 0:8, ts(tt, 128)], banks,
                B["ident_b"], sem_te, sem_sc, [t_c])
            cast_user[tt % 2] = tok_mm
            t_s = p.on("vector", lambda E, ttt=tt: E.tensor_tensor(
                B["sall"][:, ttt:ttt + 1], B["yacc"][:, ttt, 0:1],
                B["yacc"][:, ttt, 1:2], ALU.add), waits=[t_ld], inc=sem_v)
        x3b_ready = (sem_sc, sem_sc.n)

        t = p.on("vector", lambda E: E.tensor_scalar(
            B["habs"][:], B["sall"][:], -1.0, None, ALU.mult), waits=[t_s], inc=sem_v)
        t = p.on("vector", lambda E: E.tensor_tensor(
            B["habs"][:], B["habs"][:], B["sall"][:], ALU.max), waits=[t], inc=sem_v)
        t = p.on("vector", lambda E: E.tensor_copy(B["hint"][:], B["habs"][:]),
                 waits=[t], inc=sem_v)
        t = p.on("vector", lambda E: E.tensor_copy(B["hcf"][:], B["hint"][:]),
                 waits=[t], inc=sem_v)
        t = p.on("vector", lambda E: E.tensor_tensor(
            B["hfl"][:], B["hcf"][:], B["habs"][:], ALU.is_gt), waits=[t], inc=sem_v)
        t = p.on("vector", lambda E: E.tensor_tensor(
            B["hfl"][:], B["hcf"][:], B["hfl"][:], ALU.subtract), waits=[t], inc=sem_v)
        t = p.on("vector", lambda E: E.tensor_scalar(
            B["hsgn"][:], B["sall"][:], 0.0, None, ALU.is_ge), waits=[t], inc=sem_v)
        t = p.on("vector", lambda E: E.tensor_scalar(
            B["hsgn"][:], B["hsgn"][:], 2.0, -1.0, ALU.mult, ALU.add), waits=[t], inc=sem_v)
        t = p.on("vector", lambda E: E.tensor_tensor(
            B["htr"][:], B["hfl"][:], B["hsgn"][:], ALU.mult), waits=[t], inc=sem_v)
        t = p.on("vector", lambda E: E.tensor_scalar(
            B["htr"][:], B["htr"][:], 800.0, None, ALU.add), waits=[t], inc=sem_v)
        t = p.on("vector", lambda E: E.tensor_copy(B["hint"][:], B["htr"][:]),
                 waits=[t], inc=sem_v)
        t = p.on("vector", lambda E: E.tensor_scalar(
            B["hint"][:], B["hint"][:], 7, None, ALU.bitwise_and), waits=[t], inc=sem_v)
        t_hall = p.on("vector", lambda E: E.tensor_copy(B["hall"][:], B["hint"][:]),
                      waits=[t], inc=sem_v)
        prev_mask_tok = None
        for e in range(8):
            t1 = p.on("vector", lambda E, ee=e: E.tensor_scalar(
                B["eqt"][:], B["hall"][:], float(ee), None, ALU.is_equal),
                waits=[t_hall, prev_mask_tok], inc=sem_v)
            t2 = p.on("vector", lambda E, ee=e: E.tensor_scalar(
                B["eq2"][:], B["hall"][:], float((ee + 7) % 8), None, ALU.is_equal),
                waits=[t1], inc=sem_v)
            t3 = p.on("vector", lambda E: E.tensor_tensor(
                B["eqt"][:], B["eqt"][:], B["eq2"][:], ALU.add), waits=[t2], inc=sem_v)
            prev_mask_tok = p.on("vector", lambda E, ee=e: E.tensor_scalar(
                B["wall"][:, 0:8, ee], B["eqt"][:], 0.5, None, ALU.mult),
                waits=[t3], inc=sem_v)
        w_ready_v = (sem_v, sem_v.n)

        for tt in range(8):
            t_wb = p.on("vector", lambda E, ttt=tt: E.tensor_copy(
                B["wallb"][:, ttt, :], B["wall"][:, ttt, :]),
                waits=[w_ready_v, wallb_user[tt]], inc=sem_v)
            t_wt = p.on("tensor", lambda E, ttt=tt: E.matmul(
                ps[0:8, 4, 0:128], B["wallb"][:, ttt, :], B["ident_b"][:],
                start=True, stop=True),
                waits=[t_wb] + pst.write_waits([4]), inc=sem_te)
            pst.wrote([4], t_wt)
            t_wte = p.on("scalar", lambda E, ttt=tt: E.copy(
                B["wT"][0:8, ttt, :], ps[0:8, 4, 0:128]), waits=[t_wt], inc=sem_sc)
            pst.read([4], t_wte)
            for half in range(2):
                def mk(E, hh=half, ttt=tt):
                    return E.matmul(ps[:, 6 + hh, :], B["wT"][0:8, ttt, :],
                                    B["b2b"][0:8, ts(hh, 512)], start=True, stop=True)
                wts = [t_wte] + pst.write_waits([6 + half]) + ([t_b2] if tt == 0 and swp == 0 else [])
                t_ini = p.on("tensor", mk, waits=wts, inc=sem_te if half == 1 else None)
            pst.wrote([6, 7], t_ini)
            t_acc0 = p.on("vector", lambda E, ttt=tt: E.tensor_tensor(
                B["yacc"][:, ttt, :],
                ps[:, 6:8, :].rearrange("p c n -> p (c n)"),
                B["yacc"][:, ttt, :], ALU.add),
                waits=[t_ini, ld_toks[tt]], inc=sem_v)
            pst.read([6, 7], t_acc0)
            acc_tok[tt] = t_acc0
            wallb_user[tt] = t_wt

        for e in range(8):
            t_b1 = p.on("gpsimd", lambda E, ee=e: E.dma_start(
                B["b1t"][:], w["b1"][ee].rearrange("(c p) -> c p", p=128)),
                waits=[b1t_user], inc=sem_b1, k=16)
            t_b1m = p.on("tensor", lambda E: E.matmul(
                ps[:, 4, 0:32], B["b1t"][0:32, :], B["ident_b"][0:32, 0:32],
                start=True, stop=True), waits=[t_b1] + pst.write_waits([4]), inc=sem_te)
            pst.wrote([4], t_b1m)
            t_b1c = p.on("scalar", lambda E: E.copy(B["b1col"][:], ps[:, 4, 0:32]),
                         waits=[t_b1m], inc=sem_sc)
            pst.read([4], t_b1c)
            b1t_user = t_b1m

            slice_tok = {}
            for isl in range(2):
                slice_tok[isl] = p.on("gpsimd", lambda E, ee=e, ii=isl: E.dma_start(
                    B["w1"][:, ii % 2, :, :],
                    w["w1"][ee][:, ts(ii, 256)].rearrange("(c p) i -> p c i", p=128, c=8)),
                    waits=[w1_user[isl % 2]], inc=sem_w1[isl % 2], k=16)

            for ic in range(32):
                isl = ic // 2
                il = ic % 2
                slot = isl % 2
                bk = [(0, 1), (2, 3), (4, 5), (6, 7)][ic % 4]
                gtok = slice_tok[isl]
                for dinc in range(8):
                    for tg in range(2):
                        def mk(E, b=bk[tg], di=dinc, sl=slot, ill=il, tgg=tg):
                            return E.matmul(ps[:, b, :],
                                            B["w1"][:, sl, di, ts(ill, 128)],
                                            B["x3b"][:, di, ts(tgg, 512)],
                                            start=(di == 0), stop=(di == 7))
                        wts = []
                        if dinc == 0:
                            wts = pst.write_waits([bk[tg]])
                            if tg == 0:
                                wts = wts + [gtok, x3b_ready]
                        t_g = p.on("tensor", mk, waits=wts,
                                   inc=sem_te if (dinc == 7 and tg == 1) else None)
                pst.wrote(list(bk), t_g)
                for tg in range(2):
                    def mkrelu(E, b=bk[tg], icc=ic, tgg=tg):
                        return E.activation(B["hh"][:, icc, ts(tgg, 512)],
                                            ps[:, b, :], AF.Relu,
                                            bias=B["b1col"][:, icc:icc + 1])
                    t_r = p.on("scalar", mkrelu, waits=[t_g, t_b1c], inc=sem_sc)
                    pst.read([bk[tg]], t_r)
                if il == 1 and isl + 2 < 16:
                    te_now = (sem_te, sem_te.n)
                    w1_user[slot] = te_now
                    slice_tok[isl + 2] = p.on("gpsimd", lambda E, ee=e, ii=isl + 2: E.dma_start(
                        B["w1"][:, ii % 2, :, :],
                        w["w1"][ee][:, ts(ii, 256)].rearrange("(c p) i -> p c i", p=128, c=8)),
                        waits=[te_now], inc=sem_w1[(isl + 2) % 2], k=16)
                elif il == 1:
                    w1_user[slot] = (sem_te, sem_te.n)
            h_ready = (sem_sc, sem_sc.n)

            for ttg in range(2):
                w2tok = {}
                for icg in range(4):
                    w2tok[icg] = p.on("gpsimd", lambda E, ee=e, ig=icg: E.dma_start(
                        B["w2"][:, ig % 4, :, :],
                        w["w2"][ee][ts(ig, 512), :].rearrange("(c p) n -> p c n", p=128, c=4)),
                        waits=[w2s_user[icg % 4]], inc=sem_w2[icg % 4], k=16)
                dn_tok = [None] * 4
                for icg in range(8):
                    for ic in range(icg * 4, icg * 4 + 4):
                        for ti in range(4):
                            tt = ttg * 4 + ti
                            bkp = [(0, 1), (2, 3), (4, 5), (6, 7)][ti]
                            for half in range(2):
                                def mk(E, b=bkp[half], icc=ic, hh=half, ttt=tt, sl=icg % 4):
                                    return E.matmul(ps[:, b, :],
                                                    B["hh"][:, icc, ts(ttt, 128)],
                                                    B["w2"][:, sl, icc % 4, ts(hh, 512)],
                                                    start=(icc == 0), stop=(icc == 31))
                                wts = []
                                if ic == 0:
                                    wts = pst.write_waits([bkp[half]])
                                    if half == 0 and ti == 0:
                                        wts = wts + [h_ready, w2tok[0]]
                                elif ic % 4 == 0 and ti == 0 and half == 0:
                                    wts = [w2tok[icg]]
                                do_inc = (ic == 31 and half == 1) or (
                                    ic % 4 == 3 and ti == 3 and half == 1)
                                t_dn = p.on("tensor", mk, waits=wts,
                                            inc=sem_te if do_inc else None)
                            if ic == 31:
                                dn_tok[ti] = t_dn
                    if icg + 4 < 8:
                        te_now = (sem_te, sem_te.n)
                        w2s_user[icg % 4] = te_now
                        w2tok[icg + 4] = p.on("gpsimd", lambda E, ee=e, ig=icg + 4: E.dma_start(
                            B["w2"][:, ig % 4, :, :],
                            w["w2"][ee][ts(ig, 512), :].rearrange("(c p) n -> p c n", p=128, c=4)),
                            waits=[te_now], inc=sem_w2[(icg + 4) % 4], k=16)
                    else:
                        w2s_user[icg % 4] = (sem_te, sem_te.n)
                for ti in range(4):
                    tt = ttg * 4 + ti
                    bkp = [(0, 1), (2, 3), (4, 5), (6, 7)][ti]
                    pst.wrote(list(bkp), dn_tok[ti])
                    t_acc = p.on("vector", lambda E, ttt=tt, ee=e, b0=bkp[0]:
                                 E.scalar_tensor_tensor(
                                     B["yacc"][:, ttt, :],
                                     ps[:, b0:b0 + 2, :].rearrange("p c n -> p (c n)"),
                                     B["wall"][:, ttt, ee:ee + 1],
                                     B["yacc"][:, ttt, :], ALU.mult, ALU.add),
                                 waits=[dn_tok[ti], acc_tok[tt]], inc=sem_v)
                    pst.read(list(bkp), t_acc)
                    acc_tok[tt] = t_acc

        for tt in range(8):
            gt = swp * 8 + tt
            t_o, t_st = emit_ln_sbuf(p, sm, B["yacc"][:, tt, :], xout[ts(gt, 128), :],
                                     (sem_v, sem_sc, sem_st[tt % 2]), acc_tok[tt],
                                     slot=tt % 2, store_user=list(store_user))
            store_user[tt % 2] = t_st
            yacc_user[tt] = t_o

    for ss in sem_st:
        p.raw("sync", lambda E, s2=ss.h, v=ss.n: E.wait_ge(s2, v))



def build_program(phases=("A", "B", "C", "D")):
    nc = bass.Bass("TRN2", target_bir_lowering=False)
    stack = contextlib.ExitStack()
    p = Prog(nc, stack)
    have = list(phases)

    def chain(name, prev_ph, next_ph):
        if prev_ph not in have and next_ph not in have:
            return None
        if prev_ph in have and next_ph in have:
            kind = "Internal"
        elif prev_ph in have:
            kind = "ExternalOutput"
        else:
            kind = "ExternalInput"
        return nc.dram_tensor(name, [T, D], F32, kind=kind).ap()

    x0 = (nc.dram_tensor("x0", [T, D], F32, kind="ExternalInput").ap()
          if "A" in have else None)
    x1 = chain("x1", "A", "B")
    x2 = chain("x2", "B", "C")
    x3 = chain("x3", "C", "D")
    y = (nc.dram_tensor("y", [T, D], F32, kind="ExternalOutput").ap()
         if "D" in have else None)

    consts_id = nc.dram_tensor("c_ident", [128, 128], F32, kind="ExternalInput").ap()
    consts_mask = nc.dram_tensor("c_maskut", [128, 128], F32, kind="ExternalInput").ap()

    ps = stack.enter_context(nc.psum_tensor("ps", [128, 8, 512], F32))

    with stack:
        common = alloc_common(nc, stack)
        sem_cg = p.sem("constsg")
        sem_cs = p.sem("constss")
        sem_c = p.sem("consts")
        p.on("gpsimd", lambda E: E.dma_start(common["ident_b"][:], consts_id[:]),
             inc=sem_cg, k=16)
        p.on("sync", lambda E: E.dma_start(common["ident_f"][:], consts_id[:]),
             inc=sem_cs, k=16)
        p.on("sync", lambda E: E.dma_start(common["mask_ut"][:], consts_mask[:]),
             inc=sem_cs, k=16)
        p.on("vector", lambda E: E.memset(common["ones_row"][:], 1.0), inc=sem_c)
        p.on("vector", lambda E: E.memset(common["eps"][:], LN_EPS), inc=sem_c)
        for e in ENGS:
            p._wait(e, (sem_cg, sem_cg.n))
            p._wait(e, (sem_cs, sem_cs.n))
            p._wait(e, (sem_c, sem_c.n))

        if "A" in have:
            wq = nc.dram_tensor("L0_WQT", [D, D], F32, kind="ExternalInput").ap()
            wk = nc.dram_tensor("L0_WKT", [D, D], F32, kind="ExternalInput").ap()
            wv = nc.dram_tensor("L0_WVT", [D, D], F32, kind="ExternalInput").ap()
            wo = nc.dram_tensor("L0_WOT", [D, D], F32, kind="ExternalInput").ap()
            with contextlib.ExitStack() as phs:
                bufs = alloc_attn_bufs(nc, phs, ps, common, "A")
                build_attn_phase(p, bufs, x0, wq, wk, wv, wo, None, x1, "A")
                p.barrier("ph_A")
        if "B" in have:
            wB = {
                "rw": nc.dram_tensor("L0_rw", [D, 8], F32, kind="ExternalInput").ap(),
                "rb": nc.dram_tensor("L0_rb", [8], F32, kind="ExternalInput").ap(),
                "gw": nc.dram_tensor("L0_gw", [8, D, 2048], F32, kind="ExternalInput").ap(),
                "gb": nc.dram_tensor("L0_gb", [8, 2048], F32, kind="ExternalInput").ap(),
                "uw": nc.dram_tensor("L0_uw", [8, D, 2048], F32, kind="ExternalInput").ap(),
                "ub": nc.dram_tensor("L0_ub", [8, 2048], F32, kind="ExternalInput").ap(),
                "dw": nc.dram_tensor("L0_dw", [8, 2048, D], F32, kind="ExternalInput").ap(),
                "db": nc.dram_tensor("L0_db", [8, D], F32, kind="ExternalInput").ap(),
            }
            with contextlib.ExitStack() as phs:
                bufs = alloc_moe0_bufs(nc, phs, ps, common)
                build_moe0_phase(p, bufs, x1, x2, wB, "B")
                p.barrier("ph_B")
        if "C" in have:
            wqkv = nc.dram_tensor("L1_WqkvT", [D, 3 * D], F32, kind="ExternalInput").ap()
            wout = nc.dram_tensor("L1_WoutT", [D, D], F32, kind="ExternalInput").ap()
            bo = nc.dram_tensor("L1_bout", [D], F32, kind="ExternalInput").ap()
            with contextlib.ExitStack() as phs:
                bufs = alloc_attn_bufs(nc, phs, ps, common, "C")
                build_attn_phase(p, bufs, x2, wqkv[:, 0:D], wqkv[:, D:2 * D],
                                 wqkv[:, 2 * D:3 * D], wout, bo, x3, "C")
                p.barrier("ph_C")
        if "D" in have:
            wD = {
                "w1": nc.dram_tensor("L1_w1", [8, D, 4096], F32, kind="ExternalInput").ap(),
                "b1": nc.dram_tensor("L1_b1", [8, 4096], F32, kind="ExternalInput").ap(),
                "w2": nc.dram_tensor("L1_w2", [8, 4096, D], F32, kind="ExternalInput").ap(),
                "b2": nc.dram_tensor("L1_b2", [8, D], F32, kind="ExternalInput").ap(),
            }
            with contextlib.ExitStack() as phs:
                bufs = alloc_ffn1_bufs(nc, phs, ps, common)
                build_ffn1_phase(p, bufs, x3, y, wD, "D")
                p.barrier("ph_D")

        p.finalize()
    return nc



def _consts():
    return {
        "c_ident": np.eye(128, dtype=np.float32),
        "c_maskut": np.triu(np.full((128, 128), NEG_BIG, dtype=np.float32), 1),
    }


def _prep_attn_weights(inputs):
    f = np.float32
    return {
        "L0_WQT": np.ascontiguousarray(np.asarray(inputs["L0_attn_W_Q"], dtype=f).T),
        "L0_WKT": np.ascontiguousarray(np.asarray(inputs["L0_attn_W_K"], dtype=f).T),
        "L0_WVT": np.ascontiguousarray(np.asarray(inputs["L0_attn_W_V"], dtype=f).T),
        "L0_WOT": np.ascontiguousarray(np.asarray(inputs["L0_attn_W_O"], dtype=f).T),
        "L1_WqkvT": np.ascontiguousarray(np.asarray(inputs["L1_attn_Wqkv"], dtype=f).T),
        "L1_WoutT": np.ascontiguousarray(np.asarray(inputs["L1_attn_Wout"], dtype=f).T),
        "L1_bout": np.ascontiguousarray(np.asarray(inputs["L1_attn_bout"], dtype=f)),
    }


def _f32(a):
    return np.ascontiguousarray(np.asarray(a, dtype=np.float32))


def kernel(**inputs):
    x = _f32(inputs["L0_attn_x"])
    n_cores = 8
    shards = [np.ascontiguousarray(x[2 * c:2 * c + 2].reshape(T, D))
              for c in range(n_cores)]
    consts = _consts()
    aw = _prep_attn_weights(inputs)

    wA = {k: aw[k] for k in ("L0_WQT", "L0_WKT", "L0_WVT", "L0_WOT")}
    wB = {
        "L0_rw": _f32(inputs["L0_ffn_router_w"]),
        "L0_rb": _f32(inputs["L0_ffn_router_b"]),
        "L0_gw": _f32(inputs["L0_ffn_gate_w"]),
        "L0_gb": _f32(inputs["L0_ffn_gate_b"]),
        "L0_uw": _f32(inputs["L0_ffn_up_w"]),
        "L0_ub": _f32(inputs["L0_ffn_up_b"]),
        "L0_dw": _f32(inputs["L0_ffn_down_w"]),
        "L0_db": _f32(inputs["L0_ffn_down_b"]),
    }
    wC = {k: aw[k] for k in ("L1_WqkvT", "L1_WoutT", "L1_bout")}
    wD = {
        "L1_w1": _f32(inputs["L1_ffn_W1"]),
        "L1_b1": _f32(inputs["L1_ffn_b1"]),
        "L1_w2": _f32(inputs["L1_ffn_W2"]),
        "L1_b2": _f32(inputs["L1_ffn_b2"]),
    }

    cores = list(range(n_cores))

    nc = build_program(("A", "B", "C", "D"))
    res = run_bass_kernel_spmd(
        nc,
        [dict(x0=shards[c], **wA, **wB, **wC, **wD, **consts) for c in cores],
        core_ids=cores)

    out = np.concatenate(
        [r["y"].reshape(2, S_LEN, D) for r in res.results], axis=0)
    return np.ascontiguousarray(out, dtype=np.float32)


# version 21
# speedup vs baseline: 1.1105x; 1.1105x over previous
import contextlib
import numpy as np
import concourse.bass as bass
import concourse.mybir as mybir
from concourse.bass_utils import run_bass_kernel_spmd

F32 = mybir.dt.float32
BF16 = mybir.dt.bfloat16
I32 = mybir.dt.int32
AF = mybir.ActivationFunctionType
ALU = mybir.AluOpType
AX = mybir.AxisListType

ENGS = ("tensor", "vector", "scalar", "sync", "gpsimd")

S_LEN = 2048
D = 1024
NSEQ = 2
T = NSEQ * S_LEN
H = 16
DH = 64
NEG_BIG = -1.0e30
LN_EPS = 1.0e-5



class Sem:
    def __init__(self, handle):
        self.h = handle
        self.n = 0

    def bump(self, k):
        self.n += k
        return (self, self.n)


class Prog:

    def __init__(self, nc, stack):
        self.nc = nc
        self.stack = stack
        self.ops = {e: [] for e in ENGS}
        self._lastwait = {e: {} for e in ENGS}
        self._sems = {}

    def sem(self, name):
        if name not in self._sems:
            self._sems[name] = Sem(self.stack.enter_context(self.nc.semaphore(name)))
        return self._sems[name]

    def _wait(self, eng, tok):
        if tok is None:
            return
        sem, val = tok
        if val <= self._lastwait[eng].get(sem, 0):
            return
        self._lastwait[eng][sem] = val
        self.ops[eng].append(lambda E, s=sem.h, v=val: E.wait_ge(s, v))

    def on(self, eng, build, waits=(), inc=None, k=1):
        for t in waits:
            self._wait(eng, t)
        tok = None
        if inc is not None:
            tok = inc.bump(k)

            def fn(E, b=build, s=inc.h, kk=k):
                b(E).then_inc(s, kk)
        else:
            def fn(E, b=build):
                b(E)
        self.ops[eng].append(fn)
        return tok

    def raw(self, eng, fn):
        self.ops[eng].append(fn)

    def barrier(self, name):
        s = self.sem(name)
        target = s.n + len(ENGS)
        for e in ENGS:
            self.ops[e].append(lambda E, ss=s.h: E.sem_inc(ss, 1))
        s.n = target
        for e in ENGS:
            self._wait(e, (s, target))

    def finalize(self):
        with self.nc.Block() as block:
            for eng in ENGS:
                getattr(block, eng)(
                    lambda E, ops=self.ops[eng]: [fn(E) for fn in ops])


class PsumTracker:
    def __init__(self, nbanks=8):
        self.writer = [None] * nbanks
        self.readers = [[] for _ in range(nbanks)]

    def write_waits(self, banks):
        w = []
        for b in banks:
            w.extend(self.readers[b])
            if self.writer[b] is not None:
                w.append(self.writer[b])
        return w

    def wrote(self, banks, tok):
        for b in banks:
            self.writer[b] = tok
            self.readers[b] = []

    def read_waits(self, banks):
        return [self.writer[b] for b in banks if self.writer[b] is not None]

    def read(self, banks, tok):
        for b in banks:
            self.readers[b].append(tok)


def ts(i, n):
    return bass.ts(i, n)


def load_w_bf16(p, dst, src, sem, waits=()):
    c = dst.shape[1]
    srcv = src.rearrange("(c p) n -> p c n", p=128, c=c)
    return p.on("gpsimd", lambda E, d=dst, s=srcv: E.dma_start(d[:], s),
                waits=waits, inc=sem, k=16)



def emit_tile_transpose(p, pst, ps, cast_slot_ap, dst_ap, banks, ident_b,
                        sem_te, sem_sc, first_wait):
    tok = None
    for dc in range(8):
        bank = banks[dc // 4]
        col = (dc % 4) * 128
        def mk(E, b=bank, c=col, dcc=dc):
            return E.matmul(ps[:, b, c:c + 128], cast_slot_ap[:, ts(dcc, 128)],
                            ident_b[:], start=True, stop=True)
        w = []
        if dc == 0:
            w = list(first_wait) + pst.write_waits([banks[0]])
        elif dc == 4:
            w = pst.write_waits([banks[1]])
        tok = p.on("tensor", mk, waits=w, inc=sem_te if dc == 7 else None)
    pst.wrote(list(banks), tok)
    def ev(E):
        return E.copy(dst_ap,
                      ps[:, banks[0]:banks[0] + 2, :].rearrange(
                          "p c (d n) -> p (c d) n", d=4, n=128))
    t_ev = p.on("scalar", ev, waits=[tok], inc=sem_sc)
    pst.read(list(banks), t_ev)
    return tok, t_ev



def emit_residual_ln(p, pst, ps, banks, res_tile, sm, out_hbm_ap, sems,
                     mm_tok, res_tok, extra_add=None, extra_tok=None, slot=0,
                     store_user=None):
    sem_v, sem_sc, sem_st = sems
    def mkpre(E, b0=banks[0], s2=res_tile):
        return E.tensor_tensor(sm["pretile"][:],
                               ps[:, b0:b0 + 2, :].rearrange("p c n -> p (c n)"),
                               s2, ALU.add)
    t_pre = p.on("vector", mkpre, waits=[mm_tok, res_tok], inc=sem_v)
    pst.read(banks, t_pre)
    if extra_add is not None:
        t_pre = p.on("vector",
                     lambda E: E.tensor_tensor(sm["pretile"][:], sm["pretile"][:],
                                               extra_add[:], ALU.add),
                     waits=[t_pre, extra_tok], inc=sem_v)
    t_s1 = p.on("vector",
                lambda E: E.reduce_sum(sm["negs1"][:], sm["pretile"][:], AX.X, negate=True),
                waits=[t_pre], inc=sem_v)
    t_mu = p.on("vector",
                lambda E: E.tensor_scalar(sm["negmu"][:], sm["negs1"][:],
                                          1.0 / D, None, ALU.mult),
                waits=[t_s1], inc=sem_v)
    t_d = p.on("vector",
               lambda E: E.tensor_scalar(sm["pretile"][:], sm["pretile"][:],
                                         sm["negmu"][:], None, ALU.add),
               waits=[t_mu], inc=sem_v)
    t_sq = p.on("scalar",
                lambda E: E.activation(sm["sqtile"][:], sm["pretile"][:],
                                       AF.Square, accum_out=sm["s2"][:]),
                waits=[t_d], inc=sem_sc)
    t_ln = p.on("scalar",
                lambda E: E.activation(sm["lnv"][:], sm["s2"][:],
                                       AF.Ln, bias=sm["eps"][:], scale=1.0 / D),
                waits=[t_sq], inc=sem_sc)
    t_rs = p.on("scalar",
                lambda E: E.activation(sm["rstd"][:], sm["lnv"][:],
                                       AF.Exp, scale=-0.5),
                waits=[t_ln], inc=sem_sc)
    ot = sm["otile"][:, 0, :]
    w = [t_rs]
    if store_user is not None:
        w.extend(t for t in (store_user if isinstance(store_user, list) else [store_user])
                 if t is not None)
    t_o = p.on("vector",
               lambda E, o=ot: E.tensor_scalar(o, sm["pretile"][:],
                                               sm["rstd"][:], None, ALU.mult),
               waits=w, inc=sem_v)
    t_st = p.on("sync", lambda E, d=out_hbm_ap, o=ot: E.dma_start(d, o),
                waits=[t_o], inc=sem_st, k=16)
    return t_st


def emit_ln_sbuf(p, sm, src_ap, out_hbm_ap, sems, src_tok, slot=0,
                 store_user=None):
    sem_v, sem_sc, sem_st = sems
    t_s1 = p.on("vector",
                lambda E: E.reduce_sum(sm["negs1"][:], src_ap, AX.X, negate=True),
                waits=[src_tok], inc=sem_v)
    t_mu = p.on("vector",
                lambda E: E.tensor_scalar(sm["negmu"][:], sm["negs1"][:],
                                          1.0 / D, None, ALU.mult),
                waits=[t_s1], inc=sem_v)
    t_d = p.on("vector",
               lambda E: E.tensor_scalar(src_ap, src_ap,
                                         sm["negmu"][:], None, ALU.add),
               waits=[t_mu], inc=sem_v)
    t_sq = p.on("scalar",
                lambda E: E.activation(sm["sqtile"][:], src_ap,
                                       AF.Square, accum_out=sm["s2"][:]),
                waits=[t_d], inc=sem_sc)
    t_ln = p.on("scalar",
                lambda E: E.activation(sm["lnv"][:], sm["s2"][:],
                                       AF.Ln, bias=sm["eps"][:], scale=1.0 / D),
                waits=[t_sq], inc=sem_sc)
    t_rs = p.on("scalar",
                lambda E: E.activation(sm["rstd"][:], sm["lnv"][:],
                                       AF.Exp, scale=-0.5),
                waits=[t_ln], inc=sem_sc)
    ot = sm["otile"][:, slot % sm["otile"].shape[1], :]
    w = [t_rs]
    if store_user is not None:
        w.extend(t for t in (store_user if isinstance(store_user, list) else [store_user])
                 if t is not None)
    t_o = p.on("vector",
               lambda E, o=ot: E.tensor_scalar(o, src_ap,
                                               sm["rstd"][:], None, ALU.mult),
               waits=w, inc=sem_v)
    t_st = p.on("sync", lambda E, d=out_hbm_ap, o=ot: E.dma_start(d, o),
                waits=[t_o], inc=sem_st, k=16)
    return t_o, t_st



def build_attn_phase(p, bufs, xin, wqt, wkt, wvt, wot, bout, xout, tag):
    nc = p.nc
    B = bufs
    ps = B["ps"]
    sm = B["sm"]
    sem_te = p.sem(tag + "_te")
    sem_v = p.sem(tag + "_v")
    sem_sc = p.sem(tag + "_sc")
    sem_wA = p.sem(tag + "_wA")
    sem_wB = p.sem(tag + "_wB")
    sem_x = [p.sem(tag + "_x0"), p.sem(tag + "_x1")]
    sem_xb = p.sem(tag + "_xb")
    sem_st = [p.sem(tag + "_st0"), p.sem(tag + "_st1")]
    pst = PsumTracker()

    xtile_user = [None, None, None, None]
    cast_user = [None, None, None, None]
    store_user = [None, None]

    bout_tok = None
    if bout is not None:
        brow = sm["otile"][0:1, 0, :]
        browb = sm["cast_b"][0:1, 0:1024]
        t_ld = p.on("sync", lambda E: E.dma_start(brow, bout[None, :]),
                    inc=sem_xb, k=16)
        t_c = p.on("vector", lambda E: E.tensor_copy(browb, brow),
                   waits=[t_ld], inc=sem_v)
        toks = []
        for half in range(2):
            def mk(E, hh=half):
                return E.matmul(ps[:, hh, :], sm["ones_row"][:],
                                sm["cast_b"][0:1, hh * 512:hh * 512 + 512],
                                start=True, stop=True)
            t_mm = p.on("tensor", mk, waits=[t_c] + pst.write_waits([half]), inc=sem_te)
            pst.wrote([half], t_mm)
            t_ev = p.on("scalar",
                        lambda E, hh=half, t=None: E.copy(B["boutbc"][:, ts(hh, 512)], ps[:, hh, :]),
                        waits=[t_mm], inc=sem_sc)
            pst.read([half], t_ev)
            toks.append(t_ev)
            last_mm = t_mm
        bout_tok = toks[-1]
        cast_user[0] = last_mm
        xtile_user[0] = last_mm

    for s in range(NSEQ):
        xin_s = xin[s * S_LEN:(s + 1) * S_LEN, :]
        xout_s = xout[s * S_LEN:(s + 1) * S_LEN, :]

        te_now = (sem_te, sem_te.n)
        tok_wk = load_w_bf16(p, B["wbufA"], wkt, sem_wA, waits=[te_now])
        tok_wv = load_w_bf16(p, B["wbufB"], wvt, sem_wB)

        xb_tok = [None] * 16
        for tt in range(16):
            sl = B["xtile"][:, tt % 2, :]
            t_ld = p.on("sync", lambda E, dd=sl, sv=xin_s[ts(tt, 128), :]: E.dma_start(dd, sv),
                        waits=[xtile_user[tt % 2]], inc=sem_x[tt % 2], k=16)
            cs = sm["cast_b"][:, ts(tt % 2, D)]
            t_c = p.on("vector", lambda E, dd=cs, s2=sl: E.tensor_copy(dd, s2),
                       waits=[t_ld, cast_user[tt % 2]], inc=sem_v)
            xtile_user[tt % 2] = t_c
            banks = (0, 1) if tt % 2 == 0 else (2, 3)
            tok_mm, t_ev = emit_tile_transpose(
                p, pst, ps, cs, B["xb"][:, 0:8, ts(tt, 128)], banks,
                B["ident_b"], sem_te, sem_sc, [t_c])
            cast_user[tt % 2] = tok_mm
            xb_tok[tt] = t_ev
        xb_ready = (sem_sc, sem_sc.n)

        for dc in range(8):
            banks = [0, 1, 2, 3] if dc % 2 == 0 else [4, 5, 6, 7]
            for dinc in range(8):
                for tg in range(4):
                    def mk(E, b=banks[tg], dcc=dc, di=dinc, tgg=tg):
                        return E.matmul(ps[:, b, :], B["wbufA"][:, di, ts(dcc, 128)],
                                        B["xb"][:, di, ts(tgg, 512)],
                                        start=(di == 0), stop=(di == 7))
                    w = []
                    if dinc == 0:
                        w = pst.write_waits([banks[tg]])
                        if dc == 0:
                            w = w + [tok_wk, xb_tok[4 * tg + 3]]
                    tok = p.on("tensor", mk, waits=w,
                               inc=sem_te if (dinc == 7 and tg == 3) else None)
            pst.wrote(banks, tok)
            for tg in range(4):
                t_ev = p.on("scalar",
                            lambda E, dcc=dc, tgg=tg, b=banks[tg]:
                            E.copy(B["kt"][:, dcc, ts(tgg, 512)], ps[:, b, :]),
                            waits=[tok], inc=sem_sc)
                pst.read([banks[tg]], t_ev)
        kt_ready = (sem_sc, sem_sc.n)

        for tt in range(16):
            bk = [(0, 1), (2, 3), (4, 5), (6, 7)][tt % 4]
            for dinc in range(8):
                for half in range(2):
                    def mk(E, b=bk[half], di=dinc, hh=half, ttt=tt):
                        return E.matmul(ps[:, b, :], B["xb"][:, di, ts(ttt, 128)],
                                        B["wbufB"][:, di, ts(hh, 512)],
                                        start=(di == 0), stop=(di == 7))
                    w = []
                    if dinc == 0:
                        w = pst.write_waits([bk[half]])
                        if half == 0:
                            w = w + [tok_wv, xb_tok[tt]]
                    tok = p.on("tensor", mk, waits=w,
                               inc=sem_te if (dinc == 7 and half == 1) else None)
            pst.wrote(list(bk), tok)
            t_ev = p.on("scalar",
                        lambda E, ttt=tt, b0=bk[0]:
                        E.copy(B["vv"][:, ttt, :],
                               ps[:, b0:b0 + 2, :].rearrange("p c n -> p (c n)")),
                        waits=[tok], inc=sem_sc)
            pst.read(list(bk), t_ev)
        v_ready = (sem_sc, sem_sc.n)

        te_now = (sem_te, sem_te.n)
        tok_wq = load_w_bf16(p, B["wbufA"], wqt, sem_wA, waits=[te_now])
        tok_wo = load_w_bf16(p, B["wbufB"], wot, sem_wB, waits=[te_now])

        pq_free = [None, None, None, None]
        ao_prev = None
        for qg in range(4):
            for dc in (0, 1, 6, 7, 2, 3, 4, 5):
                bank = dc
                for dinc in range(8):
                    def mk(E, b=bank, di=dinc, dcc=dc, q=qg):
                        return E.matmul(ps[:, b, :], B["wbufA"][:, di, ts(dcc, 128)],
                                        B["xb"][:, di, ts(q, 512)],
                                        start=(di == 0), stop=(di == 7))
                    w = []
                    if dinc == 0:
                        w = pst.write_waits([bank]) + ([tok_wq] if dc == 0 else [])
                    tok = p.on("tensor", mk, waits=w, inc=sem_te if dinc == 7 else None)
                pst.wrote([bank], tok)
                t_ev = p.on("scalar",
                            lambda E, dcc=dc, b=bank: E.activation(
                                B["qt"][:, dcc, :], ps[:, b, :], AF.Copy, scale=0.125),
                            waits=[tok], inc=sem_sc)
                pst.read([bank], t_ev)
            qt_ready = (sem_sc, sem_sc.n)

            for h in range(H):
                c = h // 2
                off = (h % 2) * 64
                qt_state = {}

                def pq_slot(qt, qg=qg):
                    if qg == 0 and qt >= 2:
                        return ("pqs", qt - 2, qt)
                    sid = qt if qg == 0 else qt % 2
                    return ("pq", sid, sid)

                def emit_S(qt, c=c, off=off, qg=qg, h=h):
                    k_hi = qg * 512 + qt * 128 + 128
                    nb = (k_hi + 511) // 512
                    if qg == 0:
                        sbanks = [qt]
                    elif qg == 1:
                        sbanks = [0, 1] if qt % 2 == 0 else [4, 5]
                    else:
                        sbanks = [0, 1, 2, 3] if qt % 2 == 0 else [4, 5, 6, 7]
                    sb_used = sbanks[:nb]
                    kb_toks = []
                    for kb in range(nb):
                        n = min(512, k_hi - kb * 512)
                        def mk(E, b=sbanks[kb], kbb=kb, nn=n, cc=c, oo=off, qtt=qt):
                            return E.matmul(ps[:, b, 0:nn],
                                            B["qt"][oo:oo + 64, cc, ts(qtt, 128)],
                                            B["kt"][oo:oo + 64, cc, kbb * 512:kbb * 512 + nn],
                                            start=True, stop=True)
                        w = pst.write_waits([sbanks[kb]])
                        if h == 0 and qt == 0 and kb == 0:
                            w = w + [qt_ready, kt_ready]
                        kb_toks.append(p.on("tensor", mk, waits=w, inc=sem_te))
                    db = sbanks[(k_hi - 128) // 512]
                    dcol = (k_hi - 128) % 512
                    t_m = p.on("vector",
                               lambda E, b=db, col=dcol: E.tensor_tensor(
                                   ps[:, b, col:col + 128], ps[:, b, col:col + 128],
                                   B["mask_ut"][:], ALU.add),
                               waits=[kb_toks[-1]], inc=sem_v)
                    pst.wrote(sb_used, t_m)
                    t_pm = None
                    for kb in range(nb):
                        n = min(512, k_hi - kb * 512)
                        wv = [kb_toks[kb]] if kb < nb - 1 else [t_m]
                        t_pm = p.on("vector",
                                    lambda E, b=sbanks[kb], nn=n, kbb=kb, q=qt:
                                    E.reduce_max(sm["pmax"][:, q * 4 + kbb:q * 4 + kbb + 1],
                                                 ps[:, b, 0:nn], AX.X),
                                    waits=wv, inc=sem_v)
                        pst.read([sbanks[kb]], t_pm)
                    t_nm = p.on("vector",
                                lambda E, q=qt, nbb=nb: E.reduce_max(
                                    sm["negm"][:, q:q + 1],
                                    sm["pmax"][:, q * 4:q * 4 + nbb],
                                    AX.X, negate=True),
                                waits=[t_pm], inc=sem_v)
                    qt_state[qt] = (k_hi, sbanks, t_nm)

                def emit_expinv(qt, h=h):
                    k_hi, sbanks, t_nm = qt_state[qt]
                    nb = (k_hi + 511) // 512
                    if nb >= 3:
                        t_exp = None
                        for kb in range(nb):
                            n = min(512, k_hi - kb * 512)
                            buf, sid, tid = pq_slot(qt)
                            def mkexp(E, b=sbanks[kb], kbb=kb, nn=n, q=qt, bu=buf, si=sid):
                                return E.activation(
                                    B[bu][:, si, kbb * 512:kbb * 512 + nn],
                                    ps[:, b, 0:nn],
                                    AF.Exp, bias=sm["negm"][:, q:q + 1], scale=1.0,
                                    accum_out=sm["lpart"][:, q * 4 + kbb:q * 4 + kbb + 1])
                            w = [t_nm]
                            if kb == 0 and pq_free[tid] is not None:
                                w = w + [pq_free[tid]]
                            t_exp = p.on("scalar", mkexp, waits=w, inc=sem_sc)
                            pst.read([sbanks[kb]], t_exp)
                        t_l = p.on("vector",
                                   lambda E, q=qt, nbb=nb: E.reduce_sum(
                                       sm["l"][:, q:q + 1], sm["lpart"][:, q * 4:q * 4 + nbb],
                                       AX.X),
                                   waits=[t_exp], inc=sem_v)
                        t_inv = p.on("vector",
                                     lambda E, q=qt: E.reciprocal(sm["invl"][:, q:q + 1],
                                                                  sm["l"][:, q:q + 1]),
                                     waits=[t_l], inc=sem_v)
                    else:
                        buf, sid, tid = pq_slot(qt)
                        def mkexp(E, b0=sbanks[0], kh=k_hi, q=qt, bu=buf, si=sid):
                            return E.activation(
                                B[bu][:, si, 0:kh],
                                ps[:, b0:b0 + nb, :].rearrange("p c n -> p (c n)")[:, 0:kh]
                                if nb > 1 else ps[:, b0, 0:kh],
                                AF.Exp, bias=sm["negm"][:, q:q + 1], scale=1.0,
                                accum_out=sm["l"][:, q:q + 1])
                        w = [t_nm]
                        if pq_free[tid] is not None:
                            w = w + [pq_free[tid]]
                        t_exp = p.on("scalar", mkexp, waits=w, inc=sem_sc)
                        pst.read(sbanks[:nb], t_exp)
                        t_inv = p.on("vector",
                                     lambda E, q=qt: E.reciprocal(sm["invl"][:, q:q + 1],
                                                                  sm["l"][:, q:q + 1]),
                                     waits=[t_exp], inc=sem_v)
                    t_dg = p.on("vector",
                                lambda E, q=qt: E.tensor_scalar(
                                    sm["diag_b"][:, q, :], B["ident_b"][:],
                                    sm["invl"][:, q:q + 1], None, ALU.mult),
                                waits=[t_inv], inc=sem_v)
                    qt_state[qt] = (k_hi, sbanks, t_dg)

                def emit_PT(qt, h=h):
                    k_hi, sbanks, t_dg = qt_state[qt]
                    kc_n = k_hi // 128
                    ev_banks = []
                    tok = None
                    buf, sid, tid = pq_slot(qt)
                    for kc in range(kc_n):
                        bank = sbanks[min(kc // 4, len(sbanks) - 1)]
                        col = (kc % 4) * 128
                        def mk(E, b=bank, c2=col, kcc=kc, q=qt, bu=buf, si=sid):
                            return E.matmul(ps[:, b, c2:c2 + 128],
                                            B[bu][:, si, ts(kcc, 128)],
                                            sm["diag_b"][:, q, :],
                                            start=True, stop=True)
                        w = []
                        if kc == 0:
                            w = [t_dg]
                        if kc % 4 == 0:
                            w = w + pst.write_waits([bank])
                            ev_banks.append(bank)
                        tok = p.on("tensor", mk, waits=w,
                                   inc=sem_te if kc == kc_n - 1 else None)
                    pst.wrote(ev_banks, tok)
                    pq_free[tid] = tok
                    for bi, bank in enumerate(ev_banks):
                        n_kc = min(4, kc_n - bi * 4)
                        eng = "vector" if (h + bi) % 2 == 0 else "scalar"
                        isem = sem_v if eng == "vector" else sem_sc
                        if eng == "vector":
                            def ev(E, b=bank, bi2=bi, nk=n_kc, q=qt):
                                return E.tensor_copy(
                                    B["pt"][:, bi2 * 4:bi2 * 4 + nk, ts(q, 128)],
                                    ps[:, b, 0:nk * 128].rearrange("p (c n) -> p c n", n=128))
                        else:
                            def ev(E, b=bank, bi2=bi, nk=n_kc, q=qt):
                                return E.copy(
                                    B["pt"][:, bi2 * 4:bi2 * 4 + nk, ts(q, 128)],
                                    ps[:, b, 0:nk * 128].rearrange("p (c n) -> p c n", n=128))
                        t_ev = p.on(eng, ev, waits=[tok], inc=isem)
                        pst.read([bank], t_ev)

                if qg == 0:
                    for qt in range(4):
                        emit_S(qt)
                    for qt in range(4):
                        emit_expinv(qt)
                    for qt in range(4):
                        emit_PT(qt)
                else:
                    emit_S(0)
                    emit_S(1)
                    emit_expinv(0)
                    emit_PT(0)
                    emit_expinv(1)
                    emit_S(2)
                    emit_PT(1)
                    emit_expinv(2)
                    emit_S(3)
                    emit_PT(2)
                    emit_expinv(3)
                    emit_PT(3)
                pt_ready = (sem_v, sem_v.n)
                pt_ready_sc = (sem_sc, sem_sc.n)

                kc_all = (qg + 1) * 4
                pv_bank = {0: 6, 1: 6, 2: 3, 3: 3}[qg]
                for kc in range(kc_all):
                    qs = max(0, kc * 128 - qg * 512)
                    def mk(E, kcc=kc, qss=qs, hh=h, ka=kc_all, pvb=pv_bank):
                        return E.matmul(ps[0:64, pvb, qss:512],
                                        B["vv"][:, kcc, hh * 64:hh * 64 + 64],
                                        B["pt"][:, kcc, qss:512],
                                        start=(kcc == 0), stop=(kcc == ka - 1))
                    w = []
                    if kc == 0:
                        w = [pt_ready, pt_ready_sc] + pst.write_waits([pv_bank])
                        if h == 0 and qg == 0:
                            w = w + [v_ready]
                    tok = p.on("tensor", mk, waits=w, inc=sem_te if kc == kc_all - 1 else None)
                pst.wrote([pv_bank], tok)
                t_ev = p.on("scalar",
                            lambda E, cc=c, oo=off, pvb=pv_bank: E.copy(
                                B["obuf"][oo:oo + 64, cc, :], ps[0:64, pvb, :]),
                            waits=[tok], inc=sem_sc)
                pst.read([pv_bank], t_ev)
            obuf_ready = (sem_sc, sem_sc.n)

            for tq in range(4):
                tt = qg * 4 + tq
                banks = [2, 3] if tq % 2 == 0 else [4, 5]
                for dinc in range(8):
                    for half in range(2):
                        def mk(E, b=banks[half], di=dinc, hh=half, tqq=tq):
                            return E.matmul(ps[:, b, :], B["obuf"][:, di, ts(tqq, 128)],
                                            B["wbufB"][:, di, ts(hh, 512)],
                                            start=(di == 0), stop=(di == 7))
                        w = []
                        if dinc == 0:
                            w = pst.write_waits([banks[half]])
                            if half == 0 and tq == 0:
                                w = w + [obuf_ready]
                                if qg == 0:
                                    w = w + [tok_wo]
                        tok = p.on("tensor", mk, waits=w,
                                   inc=sem_te if (dinc == 7 and half == 1) else None)
                pst.wrote(banks, tok)
                sl = B["xtile"][:, tt % 2, :]
                t_ld = p.on("sync", lambda E, dd=sl, sv=xin_s[ts(tt, 128), :]: E.dma_start(dd, sv),
                            waits=[xtile_user[tt % 2]], inc=sem_x[tt % 2], k=16)
                t_st = emit_residual_ln(
                    p, pst, ps, banks, sl, sm, xout_s[ts(tt, 128), :],
                    (sem_v, sem_sc, sem_st[tt % 2]), tok, t_ld,
                    extra_add=B["boutbc"] if bout is not None else None,
                    extra_tok=bout_tok, slot=tt % 2, store_user=list(store_user))
                store_user[tt % 2] = t_st
                xtile_user[tt % 2] = t_st

    for ss in sem_st:
        p.raw("sync", lambda E, s2=ss.h, v=ss.n: E.wait_ge(s2, v))


def alloc_common(nc, stack):
    sb = lambda name, shape, dt: stack.enter_context(nc.sbuf_tensor(name, shape, dt))
    return {
        "ident_b": sb("c_identb", [128, 128], BF16),
        "ident_f": sb("c_identf", [128, 128], F32),
        "mask_ut": sb("c_mask", [128, 128], F32),
        "ones_row": sb("c_ones", [1, 128], BF16),
        "eps": sb("c_eps", [128, 1], F32),
    }


def alloc_ln_smalls(nc, stack, pre, common, with_pretile=True, otile_slots=1):
    sb = lambda name, shape, dt: stack.enter_context(nc.sbuf_tensor(name, shape, dt))
    d = {
        "pretile": sb(pre + "_pre", [128, 1024], F32) if with_pretile else None,
        "negs1": sb(pre + "_negs1", [128, 1], F32),
        "negmu": sb(pre + "_negmu", [128, 1], F32),
        "sqtile": sb(pre + "_sq", [128, 1024], F32),
        "s2": sb(pre + "_s2", [128, 1], F32),
        "lnv": sb(pre + "_lnv", [128, 1], F32),
        "rstd": sb(pre + "_rstd", [128, 1], F32),
        "otile": sb(pre + "_otile", [128, otile_slots, 1024], F32),
        "eps": common["eps"],
    }
    return d


def alloc_attn_bufs(nc, stack, ps, common, pre="A"):
    sb = lambda name, shape, dt: stack.enter_context(nc.sbuf_tensor(name, shape, dt))
    B = {
        "ps": ps,
        "ident_b": common["ident_b"],
        "mask_ut": common["mask_ut"],
        "xb": sb(pre + "_xb", [128, 8, 2048], BF16),
        "kt": sb(pre + "_kt", [128, 8, 2048], BF16),
        "vv": sb(pre + "_v", [128, 16, 1024], BF16),
        "qt": sb(pre + "_qt", [128, 8, 512], BF16),
        "pt": sb(pre + "_pt", [128, 16, 512], BF16),
        "pq": sb(pre + "_pq", [128, 2, 2048], BF16),
        "pqs": sb(pre + "_pqs", [128, 2, 512], BF16),
        "obuf": sb(pre + "_obuf", [128, 8, 512], BF16),
        "wbufA": sb(pre + "_wA", [128, 8, 1024], BF16),
        "wbufB": sb(pre + "_wB", [128, 8, 1024], BF16),
        "xtile": sb(pre + "_xtile", [128, 2, 1024], F32),
        "boutbc": sb(pre + "_boutbc", [128, 1024], F32),
    }
    B["sm"] = alloc_ln_smalls(nc, stack, pre + "ln", common, otile_slots=2)
    B["sm"].update({
        "cast_b": sb(pre + "_cast", [128, 2 * 1024], BF16),
        "negm": sb(pre + "_negm", [128, 4], F32),
        "pmax": sb(pre + "_pmax", [128, 16], F32),
        "lpart": sb(pre + "_lpart", [128, 16], F32),
        "l": sb(pre + "_l", [128, 4], F32),
        "invl": sb(pre + "_invl", [128, 4], F32),
        "diag_b": sb(pre + "_diagb", [128, 4, 128], BF16),
        "ones_row": common["ones_row"],
    })
    return B



def alloc_moe0_bufs(nc, stack, ps, common):
    sb = lambda name, shape, dt: stack.enter_context(nc.sbuf_tensor(name, shape, dt))
    B = {
        "ps": ps,
        "ident_b": common["ident_b"],
        "ident_f": common["ident_f"],
        "ones_row": common["ones_row"],
        "x1b": sb("B_x1b", [128, 8, 1024], BF16),
        "yacc": sb("B_yacc", [128, 8, 1024], F32),
        "hh": sb("B_h", [128, 16, 1024], BF16),
        "sg": sb("B_sg", [128, 2, 1024], BF16),
        "gate": sb("B_gate", [128, 2, 8, 512], BF16),
        "up": sb("B_up", [128, 2, 8, 512], BF16),
        "down": sb("B_down", [128, 16, 1024], BF16),
        "rwb": sb("B_rwb", [128, 8, 8], BF16),
        "rbbc": sb("B_rbbc", [128, 8], F32),
        "rbrow": sb("B_rbrow", [1, 8], F32),
        "rbrowb": sb("B_rbrowb", [1, 8], BF16),
        "dbb": sb("B_dbb", [8, 1024], BF16),
        "lg": sb("B_lg", [128, 8], F32),
        "p8": sb("B_p8", [128, 8], F32),
        "probs": sb("B_probs", [128, 8], F32),
        "eqm": sb("B_eqm", [128, 8], F32),
        "masked": sb("B_masked", [128, 8], F32),
        "keep": sb("B_keep", [128, 8], F32),
        "m1": sb("B_m1", [128, 1], F32),
        "m2": sb("B_m2", [128, 1], F32),
        "negm8": sb("B_negm8", [128, 1], F32),
        "se": sb("B_se", [128, 1], F32),
        "inv": sb("B_inv", [128, 1], F32),
        "s12": sb("B_s12", [128, 1], F32),
        "inv12": sb("B_inv12", [128, 1], F32),
        "wall": sb("B_wall", [128, 8, 8], F32),
        "wallb": sb("B_wallb", [128, 8, 8], BF16),
        "wT": sb("B_wT", [8, 8, 128], BF16),
        "gbt": sb("B_gbt", [16, 128], F32),
        "gbcol": sb("B_gbcol", [128, 16], F32),
        "ubt": sb("B_ubt", [16, 128], F32),
        "ubcol": sb("B_ubcol", [128, 16], F32),
    }
    B["sm"] = alloc_ln_smalls(nc, stack, "B", common, with_pretile=False, otile_slots=2)
    B["sm"]["cast_b"] = sb("B_cast", [128, 2 * 1024], BF16)
    return B


def build_moe0_phase(p, B, xin, xout, w, tag):
    ps = B["ps"]
    sm = B["sm"]
    sem_te = p.sem(tag + "_te")
    sem_v = p.sem(tag + "_v")
    sem_sc = p.sem(tag + "_sc")
    sem_yl = [p.sem(tag + "_yl%d" % i) for i in range(8)]
    sem_gs = [p.sem(tag + "_gs%d" % i) for i in range(2)]
    sem_us = [p.sem(tag + "_us%d" % i) for i in range(2)]
    sem_dn = p.sem(tag + "_dn")
    sem_gb = p.sem(tag + "_gb")
    sem_ub = p.sem(tag + "_ub")
    sem_rw = p.sem(tag + "_rw")
    sem_rb = p.sem(tag + "_rb")
    sem_db = p.sem(tag + "_db")
    sem_st = [p.sem(tag + "_st0"), p.sem(tag + "_st1")]
    pst = PsumTracker()

    t_rw = p.on("gpsimd", lambda E: E.dma_start(
        B["rwb"][:], w["rw"].rearrange("(c p) e -> p c e", p=128, c=8)),
        inc=sem_rw, k=16)
    t_rb = p.on("sync", lambda E: E.dma_start(B["rbrow"][:], w["rb"][None, :]),
                inc=sem_rb, k=16)
    t_rbb = p.on("vector", lambda E: E.tensor_copy(B["rbrowb"][:], B["rbrow"][:]),
                 waits=[t_rb], inc=sem_v)
    t_mm = p.on("tensor", lambda E: E.matmul(ps[:, 0, 0:8], B["ones_row"][:],
                                             B["rbrowb"][:], start=True, stop=True),
                waits=[t_rbb] + pst.write_waits([0]), inc=sem_te)
    pst.wrote([0], t_mm)
    t_rbbc = p.on("scalar", lambda E: E.copy(B["rbbc"][:], ps[:, 0, 0:8]),
                  waits=[t_mm], inc=sem_sc)
    pst.read([0], t_rbbc)
    t_db = p.on("gpsimd", lambda E: E.dma_start(B["dbb"][:], w["db"][:, :]),
                inc=sem_db, k=16)

    store_user = [None, None]
    yacc_user = [None] * 8
    wallb_user = [None] * 8
    sg_user = [None, None]
    gu_user = [None, None]
    down_user = None
    gbt_user = None
    ubt_user = None
    cast_user = [None, None]

    for swp in range(4):
        acc_tok = [None] * 8
        for tt in range(8):
            gt = swp * 8 + tt
            t_ld = p.on("sync",
                        lambda E, d=B["yacc"][:, tt, :], sv=xin[ts(gt, 128), :]:
                        E.dma_start(d, sv),
                        waits=[yacc_user[tt]], inc=sem_yl[tt], k=16)
            cs = sm["cast_b"][:, ts(tt % 2, D)]
            t_c = p.on("vector", lambda E, d=cs, s2=B["yacc"][:, tt, :]:
                       E.tensor_copy(d, s2),
                       waits=[t_ld, cast_user[tt % 2]], inc=sem_v)
            banks = (0, 1) if tt % 2 == 0 else (2, 3)
            tok_mm, t_ev = emit_tile_transpose(
                p, pst, ps, cs, B["x1b"][:, 0:8, ts(tt, 128)], banks,
                B["ident_b"], sem_te, sem_sc, [t_c])
            cast_user[tt % 2] = tok_mm
            for dinc in range(8):
                def mk(E, di=dinc, ttt=tt):
                    return E.matmul(ps[:, 4, 0:8], B["x1b"][:, di, ts(ttt, 128)],
                                    B["rwb"][:, di, :], start=(di == 0), stop=(di == 7))
                wts = []
                if dinc == 0:
                    wts = pst.write_waits([4]) + [t_ev, t_rw]
                t_r = p.on("tensor", mk, waits=wts, inc=sem_te if dinc == 7 else None)
            pst.wrote([4], t_r)
            t_lg = p.on("vector", lambda E: E.tensor_tensor(
                B["lg"][:], ps[:, 4, 0:8], B["rbbc"][:], ALU.add),
                waits=[t_r, t_rbbc], inc=sem_v)
            pst.read([4], t_lg)
            t_nm = p.on("vector", lambda E: E.reduce_max(
                B["negm8"][:], B["lg"][:], AX.X, negate=True), waits=[t_lg], inc=sem_v)
            t_p8 = p.on("scalar", lambda E: E.activation(
                B["p8"][:], B["lg"][:], AF.Exp, bias=B["negm8"][:],
                accum_out=B["se"][:]), waits=[t_nm], inc=sem_sc)
            t_inv = p.on("vector", lambda E: E.reciprocal(B["inv"][:], B["se"][:]),
                         waits=[t_p8], inc=sem_v)
            t_pr = p.on("vector", lambda E: E.tensor_scalar(
                B["probs"][:], B["p8"][:], B["inv"][:], None, ALU.mult),
                waits=[t_inv], inc=sem_v)
            t_m1 = p.on("vector", lambda E: E.reduce_max(B["m1"][:], B["probs"][:], AX.X),
                        waits=[t_pr], inc=sem_v)
            t_eq = p.on("vector", lambda E: E.tensor_scalar(
                B["eqm"][:], B["probs"][:], B["m1"][:], None, ALU.is_ge),
                waits=[t_m1], inc=sem_v)
            t_ms = p.on("vector", lambda E: E.scalar_tensor_tensor(
                B["masked"][:], B["eqm"][:], NEG_BIG, B["probs"][:],
                ALU.mult, ALU.add), waits=[t_eq], inc=sem_v)
            t_m2 = p.on("vector", lambda E: E.reduce_max(B["m2"][:], B["masked"][:], AX.X),
                        waits=[t_ms], inc=sem_v)
            t_kp = p.on("vector", lambda E: E.tensor_scalar(
                B["keep"][:], B["probs"][:], B["m2"][:], None, ALU.is_ge),
                waits=[t_m2], inc=sem_v)
            t_s12 = p.on("vector", lambda E: E.tensor_tensor(
                B["s12"][:], B["m1"][:], B["m2"][:], ALU.add), waits=[t_kp], inc=sem_v)
            t_i12 = p.on("vector", lambda E: E.reciprocal(B["inv12"][:], B["s12"][:]),
                         waits=[t_s12], inc=sem_v)
            t_w = p.on("vector", lambda E, ttt=tt: E.scalar_tensor_tensor(
                B["wall"][:, ttt, :], B["probs"][:], B["inv12"][:], B["keep"][:],
                ALU.mult, ALU.mult), waits=[t_i12], inc=sem_v)
            t_wb = p.on("vector", lambda E, ttt=tt: E.tensor_copy(
                B["wallb"][:, ttt, :], B["wall"][:, ttt, :]),
                waits=[t_w, wallb_user[tt]], inc=sem_v)
            t_wt = p.on("tensor", lambda E, ttt=tt: E.matmul(
                ps[0:8, 5, 0:128], B["wallb"][:, ttt, :], B["ident_b"][:],
                start=True, stop=True),
                waits=[t_wb] + pst.write_waits([5]), inc=sem_te)
            pst.wrote([5], t_wt)
            t_wte = p.on("scalar", lambda E, ttt=tt: E.copy(
                B["wT"][0:8, ttt, :], ps[0:8, 5, 0:128]), waits=[t_wt], inc=sem_sc)
            pst.read([5], t_wte)
            for half in range(2):
                def mk(E, hh=half, ttt=tt):
                    return E.matmul(ps[:, 6 + hh, :], B["wT"][0:8, ttt, :],
                                    B["dbb"][0:8, ts(hh, 512)], start=True, stop=True)
                wts = [t_wte] + pst.write_waits([6 + half]) + ([t_db] if tt == 0 and swp == 0 else [])
                t_ini = p.on("tensor", mk, waits=wts, inc=sem_te if half == 1 else None)
            pst.wrote([6, 7], t_ini)
            t_acc0 = p.on("vector", lambda E, ttt=tt: E.tensor_tensor(
                B["yacc"][:, ttt, :],
                ps[:, 6:8, :].rearrange("p c n -> p (c n)"),
                B["yacc"][:, ttt, :], ALU.add),
                waits=[t_ini, t_ld], inc=sem_v)
            pst.read([6, 7], t_acc0)
            acc_tok[tt] = t_acc0
            wallb_user[tt] = t_wt
        x1b_ready = (sem_sc, sem_sc.n)

        for e in range(8):
            t_gb = p.on("sync", lambda E, ee=e: E.dma_start(
                B["gbt"][:], w["gb"][ee].rearrange("(c p) -> c p", p=128)),
                waits=[gbt_user], inc=sem_gb, k=16)
            t_gbm = p.on("tensor", lambda E: E.matmul(
                ps[:, 4, 0:16], B["gbt"][0:16, :], B["ident_f"][0:16, 0:16],
                start=True, stop=True), waits=[t_gb] + pst.write_waits([4]), inc=sem_te)
            pst.wrote([4], t_gbm)
            t_gbc = p.on("scalar", lambda E: E.copy(B["gbcol"][:], ps[:, 4, 0:16]),
                         waits=[t_gbm], inc=sem_sc)
            pst.read([4], t_gbc)
            t_ub = p.on("sync", lambda E, ee=e: E.dma_start(
                B["ubt"][:], w["ub"][ee].rearrange("(c p) -> c p", p=128)),
                waits=[ubt_user], inc=sem_ub, k=16)
            t_ubm = p.on("tensor", lambda E: E.matmul(
                ps[:, 5, 0:16], B["ubt"][0:16, :], B["ident_f"][0:16, 0:16],
                start=True, stop=True), waits=[t_ub] + pst.write_waits([5]), inc=sem_te)
            pst.wrote([5], t_ubm)
            t_ubc = p.on("scalar", lambda E: E.copy(B["ubcol"][:], ps[:, 5, 0:16]),
                         waits=[t_ubm], inc=sem_sc)
            pst.read([5], t_ubc)
            gbt_user = t_gbm
            ubt_user = t_ubm

            slice_tok = {}
            for isl in range(2):
                slice_tok[isl] = (
                    p.on("gpsimd", lambda E, ee=e, ii=isl: E.dma_start(
                        B["gate"][:, ii % 2, :, :],
                        w["gw"][ee][:, ts(ii, 512)].rearrange("(c p) i -> p c i", p=128, c=8)),
                        waits=[gu_user[isl % 2]], inc=sem_gs[isl % 2], k=16),
                    p.on("gpsimd", lambda E, ee=e, ii=isl: E.dma_start(
                        B["up"][:, ii % 2, :, :],
                        w["uw"][ee][:, ts(ii, 512)].rearrange("(c p) i -> p c i", p=128, c=8)),
                        inc=sem_us[isl % 2], k=16),
                )

            t_dw = p.on("gpsimd", lambda E, ee=e: E.dma_start(
                B["down"][:], w["dw"][ee].rearrange("(c p) n -> p c n", p=128, c=16)),
                waits=[down_user], inc=sem_dn, k=16)

            for ic in range(16):
                isl = ic // 4
                il = ic % 4
                slot = isl % 2
                gb_banks = (0, 1) if ic % 2 == 0 else (4, 5)
                ub_banks = (2, 3) if ic % 2 == 0 else (6, 7)
                gtok, utok = slice_tok[isl]
                for dinc in range(8):
                    for tg in range(2):
                        def mk(E, b=gb_banks[tg], di=dinc, sl=slot, ill=il, tgg=tg):
                            return E.matmul(ps[:, b, :],
                                            B["gate"][:, sl, di, ts(ill, 128)][:, 0:128],
                                            B["x1b"][:, di, ts(tgg, 512)],
                                            start=(di == 0), stop=(di == 7))
                        wts = []
                        if dinc == 0:
                            wts = pst.write_waits([gb_banks[tg]])
                            if tg == 0:
                                wts = wts + [gtok, x1b_ready]
                        t_g = p.on("tensor", mk, waits=wts,
                                   inc=sem_te if (dinc == 7 and tg == 1) else None)
                pst.wrote(list(gb_banks), t_g)
                for tg in range(2):
                    def mksilu(E, b=gb_banks[tg], icc=ic, tgg=tg):
                        return E.activation(B["sg"][:, icc % 2, ts(tgg, 512)],
                                            ps[:, b, :], AF.Silu,
                                            bias=B["gbcol"][:, icc:icc + 1])
                    t_sg = p.on("scalar", mksilu,
                                waits=[t_g, t_gbc] + ([sg_user[ic % 2]] if tg == 0 else []),
                                inc=sem_sc)
                    pst.read([gb_banks[tg]], t_sg)
                for dinc in range(8):
                    for tg in range(2):
                        def mk(E, b=ub_banks[tg], di=dinc, sl=slot, ill=il, tgg=tg):
                            return E.matmul(ps[:, b, :],
                                            B["up"][:, sl, di, ts(ill, 128)][:, 0:128],
                                            B["x1b"][:, di, ts(tgg, 512)],
                                            start=(di == 0), stop=(di == 7))
                        wts = []
                        if dinc == 0:
                            wts = pst.write_waits([ub_banks[tg]])
                            if tg == 0:
                                wts = wts + [utok]
                        t_u = p.on("tensor", mk, waits=wts,
                                   inc=sem_te if (dinc == 7 and tg == 1) else None)
                pst.wrote(list(ub_banks), t_u)
                sg_done = None
                for tg in range(2):
                    def mkh(E, b=ub_banks[tg], icc=ic, tgg=tg):
                        return E.scalar_tensor_tensor(
                            B["hh"][:, icc, ts(tgg, 512)], ps[:, b, :],
                            B["ubcol"][:, icc:icc + 1], B["sg"][:, icc % 2, ts(tgg, 512)],
                            ALU.add, ALU.mult)
                    t_h = p.on("vector", mkh, waits=[t_u, t_ubc, t_sg], inc=sem_v)
                    pst.read([ub_banks[tg]], t_h)
                sg_user[ic % 2] = t_h
                if il == 3 and isl + 2 < 4:
                    te_now = (sem_te, sem_te.n)
                    gu_user[slot] = te_now
                    slice_tok[isl + 2] = (
                        p.on("gpsimd", lambda E, ee=e, ii=isl + 2: E.dma_start(
                            B["gate"][:, ii % 2, :, :],
                            w["gw"][ee][:, ts(ii, 512)].rearrange("(c p) i -> p c i", p=128, c=8)),
                            waits=[te_now], inc=sem_gs[(isl + 2) % 2], k=16),
                        p.on("gpsimd", lambda E, ee=e, ii=isl + 2: E.dma_start(
                            B["up"][:, ii % 2, :, :],
                            w["uw"][ee][:, ts(ii, 512)].rearrange("(c p) i -> p c i", p=128, c=8)),
                            inc=sem_us[(isl + 2) % 2], k=16),
                    )
                elif il == 3:
                    gu_user[slot] = (sem_te, sem_te.n)
            h_ready = (sem_v, sem_v.n)

            for tt in range(8):
                bk = [(0, 1), (2, 3), (4, 5), (6, 7)][tt % 4]
                for ic in range(16):
                    for half in range(2):
                        def mk(E, b=bk[half], icc=ic, hh=half, ttt=tt):
                            return E.matmul(ps[:, b, :], B["hh"][:, icc, ts(ttt, 128)],
                                            B["down"][:, icc, ts(hh, 512)],
                                            start=(icc == 0), stop=(icc == 15))
                        wts = []
                        if ic == 0:
                            wts = pst.write_waits([bk[half]])
                            if half == 0:
                                wts = wts + ([h_ready, t_dw] if tt == 0 else [])
                        t_dn = p.on("tensor", mk, waits=wts,
                                    inc=sem_te if (ic == 15 and half == 1) else None)
                pst.wrote(list(bk), t_dn)
                t_acc = p.on("vector", lambda E, ttt=tt, ee=e, b0=bk[0]:
                             E.scalar_tensor_tensor(
                                 B["yacc"][:, ttt, :],
                                 ps[:, b0:b0 + 2, :].rearrange("p c n -> p (c n)"),
                                 B["wall"][:, ttt, ee:ee + 1],
                                 B["yacc"][:, ttt, :], ALU.mult, ALU.add),
                             waits=[t_dn, acc_tok[tt]], inc=sem_v)
                pst.read(list(bk), t_acc)
                acc_tok[tt] = t_acc
            down_user = (sem_te, sem_te.n)

        for tt in range(8):
            gt = swp * 8 + tt
            t_o, t_st = emit_ln_sbuf(p, sm, B["yacc"][:, tt, :], xout[ts(gt, 128), :],
                                     (sem_v, sem_sc, sem_st[tt % 2]), acc_tok[tt],
                                     slot=tt % 2, store_user=list(store_user))
            store_user[tt % 2] = t_st
            yacc_user[tt] = t_o

    for ss in sem_st:
        p.raw("sync", lambda E, s2=ss.h, v=ss.n: E.wait_ge(s2, v))



def alloc_ffn1_bufs(nc, stack, ps, common):
    sb = lambda name, shape, dt: stack.enter_context(nc.sbuf_tensor(name, shape, dt))
    B = {
        "ps": ps,
        "ident_b": common["ident_b"],
        "ones_row": common["ones_row"],
        "x3b": sb("D_x3b", [128, 8, 1024], BF16),
        "yacc": sb("D_yacc", [128, 8, 1024], F32),
        "hh": sb("D_h", [128, 32, 1024], BF16),
        "w1": sb("D_w1", [128, 2, 8, 256], BF16),
        "w2": sb("D_w2", [128, 4, 4, 1024], BF16),
        "b1t": sb("D_b1t", [32, 128], BF16),
        "b1col": sb("D_b1col", [128, 32], F32),
        "b2b": sb("D_b2b", [8, 1024], BF16),
        "sall": sb("D_sall", [128, 8], F32),
        "habs": sb("D_habs", [128, 8], F32),
        "hint": sb("D_hint", [128, 8], I32),
        "hcf": sb("D_hcf", [128, 8], F32),
        "hfl": sb("D_hfl", [128, 8], F32),
        "hsgn": sb("D_hsgn", [128, 8], F32),
        "htr": sb("D_htr", [128, 8], F32),
        "hall": sb("D_hall", [128, 8], F32),
        "eqt": sb("D_eqt", [128, 8], F32),
        "eq2": sb("D_eq2", [128, 8], F32),
        "wall": sb("D_wall", [128, 8, 8], F32),
        "wallb": sb("D_wallb", [128, 8, 8], BF16),
        "wT": sb("D_wT", [8, 8, 128], BF16),
    }
    B["sm"] = alloc_ln_smalls(nc, stack, "D", common, with_pretile=False, otile_slots=2)
    B["sm"]["cast_b"] = sb("D_cast", [128, 2 * 1024], BF16)
    return B


def build_ffn1_phase(p, B, xin, xout, w, tag):
    ps = B["ps"]
    sm = B["sm"]
    sem_te = p.sem(tag + "_te")
    sem_v = p.sem(tag + "_v")
    sem_sc = p.sem(tag + "_sc")
    sem_yl = [p.sem(tag + "_yl%d" % i) for i in range(8)]
    sem_w1 = [p.sem(tag + "_w1s%d" % i) for i in range(4)]
    sem_w2 = [p.sem(tag + "_w2s%d" % i) for i in range(4)]
    sem_b1 = p.sem(tag + "_b1")
    sem_b2 = p.sem(tag + "_b2")
    sem_st = [p.sem(tag + "_st0"), p.sem(tag + "_st1")]
    pst = PsumTracker()

    t_b2 = p.on("gpsimd", lambda E: E.dma_start(B["b2b"][:], w["b2"][:, :]),
                inc=sem_b2, k=16)

    store_user = [None, None]
    yacc_user = [None] * 8
    wallb_user = [None] * 8
    w1_user = [None, None, None, None]
    w2s_user = [None, None, None, None]
    b1t_user = None
    cast_user = [None, None]

    for swp in range(4):
        acc_tok = [None] * 8
        ld_toks = [None] * 8
        for tt in range(8):
            gt = swp * 8 + tt
            t_ld = p.on("sync",
                        lambda E, d=B["yacc"][:, tt, :], sv=xin[ts(gt, 128), :]:
                        E.dma_start(d, sv),
                        waits=[yacc_user[tt]], inc=sem_yl[tt], k=16)
            ld_toks[tt] = t_ld
            cs = sm["cast_b"][:, ts(tt % 2, D)]
            t_c = p.on("vector", lambda E, d=cs, s2=B["yacc"][:, tt, :]:
                       E.tensor_copy(d, s2),
                       waits=[t_ld, cast_user[tt % 2]], inc=sem_v)
            banks = (0, 1) if tt % 2 == 0 else (2, 3)
            tok_mm, t_ev = emit_tile_transpose(
                p, pst, ps, cs, B["x3b"][:, 0:8, ts(tt, 128)], banks,
                B["ident_b"], sem_te, sem_sc, [t_c])
            cast_user[tt % 2] = tok_mm
            t_s = p.on("vector", lambda E, ttt=tt: E.tensor_tensor(
                B["sall"][:, ttt:ttt + 1], B["yacc"][:, ttt, 0:1],
                B["yacc"][:, ttt, 1:2], ALU.add), waits=[t_ld], inc=sem_v)
        x3b_ready = (sem_sc, sem_sc.n)

        t = p.on("vector", lambda E: E.tensor_scalar(
            B["habs"][:], B["sall"][:], -1.0, None, ALU.mult), waits=[t_s], inc=sem_v)
        t = p.on("vector", lambda E: E.tensor_tensor(
            B["habs"][:], B["habs"][:], B["sall"][:], ALU.max), waits=[t], inc=sem_v)
        t = p.on("vector", lambda E: E.tensor_copy(B["hint"][:], B["habs"][:]),
                 waits=[t], inc=sem_v)
        t = p.on("vector", lambda E: E.tensor_copy(B["hcf"][:], B["hint"][:]),
                 waits=[t], inc=sem_v)
        t = p.on("vector", lambda E: E.tensor_tensor(
            B["hfl"][:], B["hcf"][:], B["habs"][:], ALU.is_gt), waits=[t], inc=sem_v)
        t = p.on("vector", lambda E: E.tensor_tensor(
            B["hfl"][:], B["hcf"][:], B["hfl"][:], ALU.subtract), waits=[t], inc=sem_v)
        t = p.on("vector", lambda E: E.tensor_scalar(
            B["hsgn"][:], B["sall"][:], 0.0, None, ALU.is_ge), waits=[t], inc=sem_v)
        t = p.on("vector", lambda E: E.tensor_scalar(
            B["hsgn"][:], B["hsgn"][:], 2.0, -1.0, ALU.mult, ALU.add), waits=[t], inc=sem_v)
        t = p.on("vector", lambda E: E.tensor_tensor(
            B["htr"][:], B["hfl"][:], B["hsgn"][:], ALU.mult), waits=[t], inc=sem_v)
        t = p.on("vector", lambda E: E.tensor_scalar(
            B["htr"][:], B["htr"][:], 800.0, None, ALU.add), waits=[t], inc=sem_v)
        t = p.on("vector", lambda E: E.tensor_copy(B["hint"][:], B["htr"][:]),
                 waits=[t], inc=sem_v)
        t = p.on("vector", lambda E: E.tensor_scalar(
            B["hint"][:], B["hint"][:], 7, None, ALU.bitwise_and), waits=[t], inc=sem_v)
        t_hall = p.on("vector", lambda E: E.tensor_copy(B["hall"][:], B["hint"][:]),
                      waits=[t], inc=sem_v)
        prev_mask_tok = None
        for e in range(8):
            t1 = p.on("vector", lambda E, ee=e: E.tensor_scalar(
                B["eqt"][:], B["hall"][:], float(ee), None, ALU.is_equal),
                waits=[t_hall, prev_mask_tok], inc=sem_v)
            t2 = p.on("vector", lambda E, ee=e: E.tensor_scalar(
                B["eq2"][:], B["hall"][:], float((ee + 7) % 8), None, ALU.is_equal),
                waits=[t1], inc=sem_v)
            t3 = p.on("vector", lambda E: E.tensor_tensor(
                B["eqt"][:], B["eqt"][:], B["eq2"][:], ALU.add), waits=[t2], inc=sem_v)
            prev_mask_tok = p.on("vector", lambda E, ee=e: E.tensor_scalar(
                B["wall"][:, 0:8, ee], B["eqt"][:], 0.5, None, ALU.mult),
                waits=[t3], inc=sem_v)
        w_ready_v = (sem_v, sem_v.n)

        for tt in range(8):
            t_wb = p.on("vector", lambda E, ttt=tt: E.tensor_copy(
                B["wallb"][:, ttt, :], B["wall"][:, ttt, :]),
                waits=[w_ready_v, wallb_user[tt]], inc=sem_v)
            t_wt = p.on("tensor", lambda E, ttt=tt: E.matmul(
                ps[0:8, 4, 0:128], B["wallb"][:, ttt, :], B["ident_b"][:],
                start=True, stop=True),
                waits=[t_wb] + pst.write_waits([4]), inc=sem_te)
            pst.wrote([4], t_wt)
            t_wte = p.on("scalar", lambda E, ttt=tt: E.copy(
                B["wT"][0:8, ttt, :], ps[0:8, 4, 0:128]), waits=[t_wt], inc=sem_sc)
            pst.read([4], t_wte)
            for half in range(2):
                def mk(E, hh=half, ttt=tt):
                    return E.matmul(ps[:, 6 + hh, :], B["wT"][0:8, ttt, :],
                                    B["b2b"][0:8, ts(hh, 512)], start=True, stop=True)
                wts = [t_wte] + pst.write_waits([6 + half]) + ([t_b2] if tt == 0 and swp == 0 else [])
                t_ini = p.on("tensor", mk, waits=wts, inc=sem_te if half == 1 else None)
            pst.wrote([6, 7], t_ini)
            t_acc0 = p.on("vector", lambda E, ttt=tt: E.tensor_tensor(
                B["yacc"][:, ttt, :],
                ps[:, 6:8, :].rearrange("p c n -> p (c n)"),
                B["yacc"][:, ttt, :], ALU.add),
                waits=[t_ini, ld_toks[tt]], inc=sem_v)
            pst.read([6, 7], t_acc0)
            acc_tok[tt] = t_acc0
            wallb_user[tt] = t_wt

        for e in range(8):
            t_b1 = p.on("gpsimd", lambda E, ee=e: E.dma_start(
                B["b1t"][:], w["b1"][ee].rearrange("(c p) -> c p", p=128)),
                waits=[b1t_user], inc=sem_b1, k=16)
            t_b1m = p.on("tensor", lambda E: E.matmul(
                ps[:, 4, 0:32], B["b1t"][0:32, :], B["ident_b"][0:32, 0:32],
                start=True, stop=True), waits=[t_b1] + pst.write_waits([4]), inc=sem_te)
            pst.wrote([4], t_b1m)
            t_b1c = p.on("scalar", lambda E: E.copy(B["b1col"][:], ps[:, 4, 0:32]),
                         waits=[t_b1m], inc=sem_sc)
            pst.read([4], t_b1c)
            b1t_user = t_b1m

            slice_tok = {}
            for isl in range(2):
                slice_tok[isl] = p.on("gpsimd", lambda E, ee=e, ii=isl: E.dma_start(
                    B["w1"][:, ii % 2, :, :],
                    w["w1"][ee][:, ts(ii, 256)].rearrange("(c p) i -> p c i", p=128, c=8)),
                    waits=[w1_user[isl % 2]], inc=sem_w1[isl % 2], k=16)

            for ic in range(32):
                isl = ic // 2
                il = ic % 2
                slot = isl % 2
                bk = [(0, 1), (2, 3), (4, 5), (6, 7)][ic % 4]
                gtok = slice_tok[isl]
                for dinc in range(8):
                    for tg in range(2):
                        def mk(E, b=bk[tg], di=dinc, sl=slot, ill=il, tgg=tg):
                            return E.matmul(ps[:, b, :],
                                            B["w1"][:, sl, di, ts(ill, 128)],
                                            B["x3b"][:, di, ts(tgg, 512)],
                                            start=(di == 0), stop=(di == 7))
                        wts = []
                        if dinc == 0:
                            wts = pst.write_waits([bk[tg]])
                            if tg == 0:
                                wts = wts + [gtok, x3b_ready]
                        t_g = p.on("tensor", mk, waits=wts,
                                   inc=sem_te if (dinc == 7 and tg == 1) else None)
                pst.wrote(list(bk), t_g)
                for tg in range(2):
                    def mkrelu(E, b=bk[tg], icc=ic, tgg=tg):
                        return E.activation(B["hh"][:, icc, ts(tgg, 512)],
                                            ps[:, b, :], AF.Relu,
                                            bias=B["b1col"][:, icc:icc + 1])
                    t_r = p.on("scalar", mkrelu, waits=[t_g, t_b1c], inc=sem_sc)
                    pst.read([bk[tg]], t_r)
                if il == 1 and isl + 2 < 16:
                    te_now = (sem_te, sem_te.n)
                    w1_user[slot] = te_now
                    slice_tok[isl + 2] = p.on("gpsimd", lambda E, ee=e, ii=isl + 2: E.dma_start(
                        B["w1"][:, ii % 2, :, :],
                        w["w1"][ee][:, ts(ii, 256)].rearrange("(c p) i -> p c i", p=128, c=8)),
                        waits=[te_now], inc=sem_w1[(isl + 2) % 2], k=16)
                elif il == 1:
                    w1_user[slot] = (sem_te, sem_te.n)
            h_ready = (sem_sc, sem_sc.n)

            for ttg in range(2):
                w2tok = {}
                for icg in range(4):
                    w2tok[icg] = p.on("gpsimd", lambda E, ee=e, ig=icg: E.dma_start(
                        B["w2"][:, ig % 4, :, :],
                        w["w2"][ee][ts(ig, 512), :].rearrange("(c p) n -> p c n", p=128, c=4)),
                        waits=[w2s_user[icg % 4]], inc=sem_w2[icg % 4], k=16)
                dn_tok = [None] * 4
                for icg in range(8):
                    for ic in range(icg * 4, icg * 4 + 4):
                        for ti in range(4):
                            tt = ttg * 4 + ti
                            bkp = [(0, 1), (2, 3), (4, 5), (6, 7)][ti]
                            for half in range(2):
                                def mk(E, b=bkp[half], icc=ic, hh=half, ttt=tt, sl=icg % 4):
                                    return E.matmul(ps[:, b, :],
                                                    B["hh"][:, icc, ts(ttt, 128)],
                                                    B["w2"][:, sl, icc % 4, ts(hh, 512)],
                                                    start=(icc == 0), stop=(icc == 31))
                                wts = []
                                if ic == 0:
                                    wts = pst.write_waits([bkp[half]])
                                    if half == 0 and ti == 0:
                                        wts = wts + [h_ready, w2tok[0]]
                                elif ic % 4 == 0 and ti == 0 and half == 0:
                                    wts = [w2tok[icg]]
                                do_inc = (ic == 31 and half == 1) or (
                                    ic % 4 == 3 and ti == 3 and half == 1)
                                t_dn = p.on("tensor", mk, waits=wts,
                                            inc=sem_te if do_inc else None)
                            if ic == 31:
                                dn_tok[ti] = t_dn
                    if icg + 4 < 8:
                        te_now = (sem_te, sem_te.n)
                        w2s_user[icg % 4] = te_now
                        w2tok[icg + 4] = p.on("gpsimd", lambda E, ee=e, ig=icg + 4: E.dma_start(
                            B["w2"][:, ig % 4, :, :],
                            w["w2"][ee][ts(ig, 512), :].rearrange("(c p) n -> p c n", p=128, c=4)),
                            waits=[te_now], inc=sem_w2[(icg + 4) % 4], k=16)
                    else:
                        w2s_user[icg % 4] = (sem_te, sem_te.n)
                for ti in range(4):
                    tt = ttg * 4 + ti
                    bkp = [(0, 1), (2, 3), (4, 5), (6, 7)][ti]
                    pst.wrote(list(bkp), dn_tok[ti])
                    t_acc = p.on("vector", lambda E, ttt=tt, ee=e, b0=bkp[0]:
                                 E.scalar_tensor_tensor(
                                     B["yacc"][:, ttt, :],
                                     ps[:, b0:b0 + 2, :].rearrange("p c n -> p (c n)"),
                                     B["wall"][:, ttt, ee:ee + 1],
                                     B["yacc"][:, ttt, :], ALU.mult, ALU.add),
                                 waits=[dn_tok[ti], acc_tok[tt]], inc=sem_v)
                    pst.read(list(bkp), t_acc)
                    acc_tok[tt] = t_acc

        for tt in range(8):
            gt = swp * 8 + tt
            t_o, t_st = emit_ln_sbuf(p, sm, B["yacc"][:, tt, :], xout[ts(gt, 128), :],
                                     (sem_v, sem_sc, sem_st[tt % 2]), acc_tok[tt],
                                     slot=tt % 2, store_user=list(store_user))
            store_user[tt % 2] = t_st
            yacc_user[tt] = t_o

    for ss in sem_st:
        p.raw("sync", lambda E, s2=ss.h, v=ss.n: E.wait_ge(s2, v))



def build_program(phases=("A", "B", "C", "D")):
    nc = bass.Bass("TRN2", target_bir_lowering=False)
    stack = contextlib.ExitStack()
    p = Prog(nc, stack)
    have = list(phases)

    def chain(name, prev_ph, next_ph):
        if prev_ph not in have and next_ph not in have:
            return None
        if prev_ph in have and next_ph in have:
            kind = "Internal"
        elif prev_ph in have:
            kind = "ExternalOutput"
        else:
            kind = "ExternalInput"
        return nc.dram_tensor(name, [T, D], F32, kind=kind).ap()

    x0 = (nc.dram_tensor("x0", [T, D], F32, kind="ExternalInput").ap()
          if "A" in have else None)
    x1 = chain("x1", "A", "B")
    x2 = chain("x2", "B", "C")
    x3 = chain("x3", "C", "D")
    y = (nc.dram_tensor("y", [T, D], F32, kind="ExternalOutput").ap()
         if "D" in have else None)

    consts_id = nc.dram_tensor("c_ident", [128, 128], F32, kind="ExternalInput").ap()
    consts_mask = nc.dram_tensor("c_maskut", [128, 128], F32, kind="ExternalInput").ap()

    ps = stack.enter_context(nc.psum_tensor("ps", [128, 8, 512], F32))

    with stack:
        common = alloc_common(nc, stack)
        sem_cg = p.sem("constsg")
        sem_cs = p.sem("constss")
        sem_c = p.sem("consts")
        p.on("gpsimd", lambda E: E.dma_start(common["ident_b"][:], consts_id[:]),
             inc=sem_cg, k=16)
        p.on("sync", lambda E: E.dma_start(common["ident_f"][:], consts_id[:]),
             inc=sem_cs, k=16)
        p.on("sync", lambda E: E.dma_start(common["mask_ut"][:], consts_mask[:]),
             inc=sem_cs, k=16)
        p.on("vector", lambda E: E.memset(common["ones_row"][:], 1.0), inc=sem_c)
        p.on("vector", lambda E: E.memset(common["eps"][:], LN_EPS), inc=sem_c)
        for e in ENGS:
            p._wait(e, (sem_cg, sem_cg.n))
            p._wait(e, (sem_cs, sem_cs.n))
            p._wait(e, (sem_c, sem_c.n))

        if "A" in have:
            wq = nc.dram_tensor("L0_WQT", [D, D], F32, kind="ExternalInput").ap()
            wk = nc.dram_tensor("L0_WKT", [D, D], F32, kind="ExternalInput").ap()
            wv = nc.dram_tensor("L0_WVT", [D, D], F32, kind="ExternalInput").ap()
            wo = nc.dram_tensor("L0_WOT", [D, D], F32, kind="ExternalInput").ap()
            with contextlib.ExitStack() as phs:
                bufs = alloc_attn_bufs(nc, phs, ps, common, "A")
                build_attn_phase(p, bufs, x0, wq, wk, wv, wo, None, x1, "A")
                p.barrier("ph_A")
        if "B" in have:
            wB = {
                "rw": nc.dram_tensor("L0_rw", [D, 8], F32, kind="ExternalInput").ap(),
                "rb": nc.dram_tensor("L0_rb", [8], F32, kind="ExternalInput").ap(),
                "gw": nc.dram_tensor("L0_gw", [8, D, 2048], F32, kind="ExternalInput").ap(),
                "gb": nc.dram_tensor("L0_gb", [8, 2048], F32, kind="ExternalInput").ap(),
                "uw": nc.dram_tensor("L0_uw", [8, D, 2048], F32, kind="ExternalInput").ap(),
                "ub": nc.dram_tensor("L0_ub", [8, 2048], F32, kind="ExternalInput").ap(),
                "dw": nc.dram_tensor("L0_dw", [8, 2048, D], F32, kind="ExternalInput").ap(),
                "db": nc.dram_tensor("L0_db", [8, D], F32, kind="ExternalInput").ap(),
            }
            with contextlib.ExitStack() as phs:
                bufs = alloc_moe0_bufs(nc, phs, ps, common)
                build_moe0_phase(p, bufs, x1, x2, wB, "B")
                p.barrier("ph_B")
        if "C" in have:
            wqkv = nc.dram_tensor("L1_WqkvT", [D, 3 * D], F32, kind="ExternalInput").ap()
            wout = nc.dram_tensor("L1_WoutT", [D, D], F32, kind="ExternalInput").ap()
            bo = nc.dram_tensor("L1_bout", [D], F32, kind="ExternalInput").ap()
            with contextlib.ExitStack() as phs:
                bufs = alloc_attn_bufs(nc, phs, ps, common, "C")
                build_attn_phase(p, bufs, x2, wqkv[:, 0:D], wqkv[:, D:2 * D],
                                 wqkv[:, 2 * D:3 * D], wout, bo, x3, "C")
                p.barrier("ph_C")
        if "D" in have:
            wD = {
                "w1": nc.dram_tensor("L1_w1", [8, D, 4096], F32, kind="ExternalInput").ap(),
                "b1": nc.dram_tensor("L1_b1", [8, 4096], F32, kind="ExternalInput").ap(),
                "w2": nc.dram_tensor("L1_w2", [8, 4096, D], F32, kind="ExternalInput").ap(),
                "b2": nc.dram_tensor("L1_b2", [8, D], F32, kind="ExternalInput").ap(),
            }
            with contextlib.ExitStack() as phs:
                bufs = alloc_ffn1_bufs(nc, phs, ps, common)
                build_ffn1_phase(p, bufs, x3, y, wD, "D")
                p.barrier("ph_D")

        p.finalize()
    return nc



def _consts():
    return {
        "c_ident": np.eye(128, dtype=np.float32),
        "c_maskut": np.triu(np.full((128, 128), NEG_BIG, dtype=np.float32), 1),
    }


def _prep_attn_weights(inputs):
    f = np.float32
    return {
        "L0_WQT": np.ascontiguousarray(np.asarray(inputs["L0_attn_W_Q"], dtype=f).T),
        "L0_WKT": np.ascontiguousarray(np.asarray(inputs["L0_attn_W_K"], dtype=f).T),
        "L0_WVT": np.ascontiguousarray(np.asarray(inputs["L0_attn_W_V"], dtype=f).T),
        "L0_WOT": np.ascontiguousarray(np.asarray(inputs["L0_attn_W_O"], dtype=f).T),
        "L1_WqkvT": np.ascontiguousarray(np.asarray(inputs["L1_attn_Wqkv"], dtype=f).T),
        "L1_WoutT": np.ascontiguousarray(np.asarray(inputs["L1_attn_Wout"], dtype=f).T),
        "L1_bout": np.ascontiguousarray(np.asarray(inputs["L1_attn_bout"], dtype=f)),
    }


def _f32(a):
    return np.ascontiguousarray(np.asarray(a, dtype=np.float32))


def kernel(**inputs):
    x = _f32(inputs["L0_attn_x"])
    n_cores = 8
    shards = [np.ascontiguousarray(x[2 * c:2 * c + 2].reshape(T, D))
              for c in range(n_cores)]
    consts = _consts()
    aw = _prep_attn_weights(inputs)

    wA = {k: aw[k] for k in ("L0_WQT", "L0_WKT", "L0_WVT", "L0_WOT")}
    wB = {
        "L0_rw": _f32(inputs["L0_ffn_router_w"]),
        "L0_rb": _f32(inputs["L0_ffn_router_b"]),
        "L0_gw": _f32(inputs["L0_ffn_gate_w"]),
        "L0_gb": _f32(inputs["L0_ffn_gate_b"]),
        "L0_uw": _f32(inputs["L0_ffn_up_w"]),
        "L0_ub": _f32(inputs["L0_ffn_up_b"]),
        "L0_dw": _f32(inputs["L0_ffn_down_w"]),
        "L0_db": _f32(inputs["L0_ffn_down_b"]),
    }
    wC = {k: aw[k] for k in ("L1_WqkvT", "L1_WoutT", "L1_bout")}
    wD = {
        "L1_w1": _f32(inputs["L1_ffn_W1"]),
        "L1_b1": _f32(inputs["L1_ffn_b1"]),
        "L1_w2": _f32(inputs["L1_ffn_W2"]),
        "L1_b2": _f32(inputs["L1_ffn_b2"]),
    }

    cores = list(range(n_cores))

    nc = build_program(("A", "B", "C", "D"))
    res = run_bass_kernel_spmd(
        nc,
        [dict(x0=shards[c], **wA, **wB, **wC, **wD, **consts) for c in cores],
        core_ids=cores)

    out = np.concatenate(
        [r["y"].reshape(2, S_LEN, D) for r in res.results], axis=0)
    return np.ascontiguousarray(out, dtype=np.float32)


# version 22
# speedup vs baseline: 1.1953x; 1.0763x over previous
import contextlib
import numpy as np
import concourse.bass as bass
import concourse.mybir as mybir
from concourse.bass_utils import run_bass_kernel_spmd

F32 = mybir.dt.float32
BF16 = mybir.dt.bfloat16
I32 = mybir.dt.int32
AF = mybir.ActivationFunctionType
ALU = mybir.AluOpType
AX = mybir.AxisListType

ENGS = ("tensor", "vector", "scalar", "sync", "gpsimd")

S_LEN = 2048
D = 1024
NSEQ = 2
T = NSEQ * S_LEN
H = 16
DH = 64
NEG_BIG = -1.0e30
LN_EPS = 1.0e-5



class Sem:
    def __init__(self, handle):
        self.h = handle
        self.n = 0

    def bump(self, k):
        self.n += k
        return (self, self.n)


class Prog:

    def __init__(self, nc, stack):
        self.nc = nc
        self.stack = stack
        self.ops = {e: [] for e in ENGS}
        self._lastwait = {e: {} for e in ENGS}
        self._sems = {}

    def sem(self, name):
        if name not in self._sems:
            self._sems[name] = Sem(self.stack.enter_context(self.nc.semaphore(name)))
        return self._sems[name]

    def _wait(self, eng, tok):
        if tok is None:
            return
        sem, val = tok
        if val <= self._lastwait[eng].get(sem, 0):
            return
        self._lastwait[eng][sem] = val
        self.ops[eng].append(lambda E, s=sem.h, v=val: E.wait_ge(s, v))

    def on(self, eng, build, waits=(), inc=None, k=1):
        for t in waits:
            self._wait(eng, t)
        tok = None
        if inc is not None:
            tok = inc.bump(k)

            def fn(E, b=build, s=inc.h, kk=k):
                b(E).then_inc(s, kk)
        else:
            def fn(E, b=build):
                b(E)
        self.ops[eng].append(fn)
        return tok

    def raw(self, eng, fn):
        self.ops[eng].append(fn)

    def barrier(self, name):
        s = self.sem(name)
        target = s.n + len(ENGS)
        for e in ENGS:
            self.ops[e].append(lambda E, ss=s.h: E.sem_inc(ss, 1))
        s.n = target
        for e in ENGS:
            self._wait(e, (s, target))

    def finalize(self):
        with self.nc.Block() as block:
            for eng in ENGS:
                getattr(block, eng)(
                    lambda E, ops=self.ops[eng]: [fn(E) for fn in ops])


class PsumTracker:
    def __init__(self, nbanks=8):
        self.writer = [None] * nbanks
        self.readers = [[] for _ in range(nbanks)]

    def write_waits(self, banks):
        w = []
        for b in banks:
            w.extend(self.readers[b])
            if self.writer[b] is not None:
                w.append(self.writer[b])
        return w

    def wrote(self, banks, tok):
        for b in banks:
            self.writer[b] = tok
            self.readers[b] = []

    def read_waits(self, banks):
        return [self.writer[b] for b in banks if self.writer[b] is not None]

    def read(self, banks, tok):
        for b in banks:
            self.readers[b].append(tok)


def ts(i, n):
    return bass.ts(i, n)


def load_w_bf16(p, dst, src, sem, waits=()):
    c = dst.shape[1]
    srcv = src.rearrange("(c p) n -> p c n", p=128, c=c)
    return p.on("gpsimd", lambda E, d=dst, s=srcv: E.dma_start(d[:], s),
                waits=waits, inc=sem, k=16)



def emit_tile_transpose(p, pst, ps, cast_slot_ap, dst_ap, banks, ident_b,
                        sem_te, sem_sc, first_wait):
    tok = None
    for dc in range(8):
        bank = banks[dc // 4]
        col = (dc % 4) * 128
        def mk(E, b=bank, c=col, dcc=dc):
            return E.matmul(ps[:, b, c:c + 128], cast_slot_ap[:, ts(dcc, 128)],
                            ident_b[:], start=True, stop=True)
        w = []
        if dc == 0:
            w = list(first_wait) + pst.write_waits([banks[0]])
        elif dc == 4:
            w = pst.write_waits([banks[1]])
        tok = p.on("tensor", mk, waits=w, inc=sem_te if dc == 7 else None)
    pst.wrote(list(banks), tok)
    def ev(E):
        return E.copy(dst_ap,
                      ps[:, banks[0]:banks[0] + 2, :].rearrange(
                          "p c (d n) -> p (c d) n", d=4, n=128))
    t_ev = p.on("scalar", ev, waits=[tok], inc=sem_sc)
    pst.read(list(banks), t_ev)
    return tok, t_ev



def emit_residual_ln(p, pst, ps, banks, res_tile, sm, out_hbm_ap, sems,
                     mm_tok, res_tok, extra_add=None, extra_tok=None, slot=0,
                     store_user=None):
    sem_v, sem_sc, sem_st = sems
    def mkpre(E, b0=banks[0], s2=res_tile):
        return E.tensor_tensor(sm["pretile"][:],
                               ps[:, b0:b0 + 2, :].rearrange("p c n -> p (c n)"),
                               s2, ALU.add)
    t_pre = p.on("vector", mkpre, waits=[mm_tok, res_tok], inc=sem_v)
    pst.read(banks, t_pre)
    if extra_add is not None:
        t_pre = p.on("vector",
                     lambda E: E.tensor_tensor(sm["pretile"][:], sm["pretile"][:],
                                               extra_add[:], ALU.add),
                     waits=[t_pre, extra_tok], inc=sem_v)
    t_s1 = p.on("vector",
                lambda E: E.reduce_sum(sm["negs1"][:], sm["pretile"][:], AX.X, negate=True),
                waits=[t_pre], inc=sem_v)
    t_mu = p.on("vector",
                lambda E: E.tensor_scalar(sm["negmu"][:], sm["negs1"][:],
                                          1.0 / D, None, ALU.mult),
                waits=[t_s1], inc=sem_v)
    t_d = p.on("vector",
               lambda E: E.tensor_scalar(sm["pretile"][:], sm["pretile"][:],
                                         sm["negmu"][:], None, ALU.add),
               waits=[t_mu], inc=sem_v)
    t_sq = p.on("scalar",
                lambda E: E.activation(sm["sqtile"][:], sm["pretile"][:],
                                       AF.Square, accum_out=sm["s2"][:]),
                waits=[t_d], inc=sem_sc)
    t_ln = p.on("scalar",
                lambda E: E.activation(sm["lnv"][:], sm["s2"][:],
                                       AF.Ln, bias=sm["eps"][:], scale=1.0 / D),
                waits=[t_sq], inc=sem_sc)
    t_rs = p.on("scalar",
                lambda E: E.activation(sm["rstd"][:], sm["lnv"][:],
                                       AF.Exp, scale=-0.5),
                waits=[t_ln], inc=sem_sc)
    ot = sm["otile"][:, 0, :]
    w = [t_rs]
    if store_user is not None:
        w.extend(t for t in (store_user if isinstance(store_user, list) else [store_user])
                 if t is not None)
    t_o = p.on("vector",
               lambda E, o=ot: E.tensor_scalar(o, sm["pretile"][:],
                                               sm["rstd"][:], None, ALU.mult),
               waits=w, inc=sem_v)
    t_st = p.on("sync", lambda E, d=out_hbm_ap, o=ot: E.dma_start(d, o),
                waits=[t_o], inc=sem_st, k=16)
    return t_st


def emit_ln_sbuf(p, sm, src_ap, out_hbm_ap, sems, src_tok, slot=0,
                 store_user=None):
    sem_v, sem_sc, sem_st = sems
    t_s1 = p.on("vector",
                lambda E: E.reduce_sum(sm["negs1"][:], src_ap, AX.X, negate=True),
                waits=[src_tok], inc=sem_v)
    t_mu = p.on("vector",
                lambda E: E.tensor_scalar(sm["negmu"][:], sm["negs1"][:],
                                          1.0 / D, None, ALU.mult),
                waits=[t_s1], inc=sem_v)
    t_d = p.on("vector",
               lambda E: E.tensor_scalar(src_ap, src_ap,
                                         sm["negmu"][:], None, ALU.add),
               waits=[t_mu], inc=sem_v)
    t_sq = p.on("scalar",
                lambda E: E.activation(sm["sqtile"][:], src_ap,
                                       AF.Square, accum_out=sm["s2"][:]),
                waits=[t_d], inc=sem_sc)
    t_ln = p.on("scalar",
                lambda E: E.activation(sm["lnv"][:], sm["s2"][:],
                                       AF.Ln, bias=sm["eps"][:], scale=1.0 / D),
                waits=[t_sq], inc=sem_sc)
    t_rs = p.on("scalar",
                lambda E: E.activation(sm["rstd"][:], sm["lnv"][:],
                                       AF.Exp, scale=-0.5),
                waits=[t_ln], inc=sem_sc)
    ot = sm["otile"][:, slot % sm["otile"].shape[1], :]
    w = [t_rs]
    if store_user is not None:
        w.extend(t for t in (store_user if isinstance(store_user, list) else [store_user])
                 if t is not None)
    t_o = p.on("vector",
               lambda E, o=ot: E.tensor_scalar(o, src_ap,
                                               sm["rstd"][:], None, ALU.mult),
               waits=w, inc=sem_v)
    t_st = p.on("sync", lambda E, d=out_hbm_ap, o=ot: E.dma_start(d, o),
                waits=[t_o], inc=sem_st, k=16)
    return t_o, t_st



def build_attn_phase(p, bufs, xin, wqt, wkt, wvt, wot, bout, xout, tag):
    nc = p.nc
    B = bufs
    ps = B["ps"]
    sm = B["sm"]
    sem_te = p.sem(tag + "_te")
    sem_v = p.sem(tag + "_v")
    sem_sc = p.sem(tag + "_sc")
    sem_wA = p.sem(tag + "_wA")
    sem_wB = p.sem(tag + "_wB")
    sem_x = [p.sem(tag + "_x0"), p.sem(tag + "_x1")]
    sem_xb = p.sem(tag + "_xb")
    sem_st = [p.sem(tag + "_st0"), p.sem(tag + "_st1")]
    pst = PsumTracker()

    xtile_user = [None, None, None, None]
    cast_user = [None, None, None, None]
    store_user = [None, None]

    bout_tok = None
    if bout is not None:
        brow = sm["otile"][0:1, 0, :]
        browb = sm["cast_b"][0:1, 0:1024]
        t_ld = p.on("sync", lambda E: E.dma_start(brow, bout[None, :]),
                    inc=sem_xb, k=16)
        t_c = p.on("vector", lambda E: E.tensor_copy(browb, brow),
                   waits=[t_ld], inc=sem_v)
        toks = []
        for half in range(2):
            def mk(E, hh=half):
                return E.matmul(ps[:, hh, :], sm["ones_row"][:],
                                sm["cast_b"][0:1, hh * 512:hh * 512 + 512],
                                start=True, stop=True)
            t_mm = p.on("tensor", mk, waits=[t_c] + pst.write_waits([half]), inc=sem_te)
            pst.wrote([half], t_mm)
            t_ev = p.on("scalar",
                        lambda E, hh=half, t=None: E.copy(B["boutbc"][:, ts(hh, 512)], ps[:, hh, :]),
                        waits=[t_mm], inc=sem_sc)
            pst.read([half], t_ev)
            toks.append(t_ev)
            last_mm = t_mm
        bout_tok = toks[-1]
        cast_user[0] = last_mm
        xtile_user[0] = last_mm

    for s in range(NSEQ):
        xin_s = xin[s * S_LEN:(s + 1) * S_LEN, :]
        xout_s = xout[s * S_LEN:(s + 1) * S_LEN, :]

        te_now = (sem_te, sem_te.n)
        tok_wk = load_w_bf16(p, B["wbufA"], wkt, sem_wA, waits=[te_now])
        tok_wv = load_w_bf16(p, B["wbufB"], wvt, sem_wB)

        xb_tok = [None] * 16
        for tt in range(16):
            sl = B["xtile"][:, tt % 2, :]
            t_ld = p.on("sync", lambda E, dd=sl, sv=xin_s[ts(tt, 128), :]: E.dma_start(dd, sv),
                        waits=[xtile_user[tt % 2]], inc=sem_x[tt % 2], k=16)
            cs = sm["cast_b"][:, ts(tt % 2, D)]
            t_c = p.on("vector", lambda E, dd=cs, s2=sl: E.tensor_copy(dd, s2),
                       waits=[t_ld, cast_user[tt % 2]], inc=sem_v)
            xtile_user[tt % 2] = t_c
            banks = (0, 1) if tt % 2 == 0 else (2, 3)
            tok_mm, t_ev = emit_tile_transpose(
                p, pst, ps, cs, B["xb"][:, 0:8, ts(tt, 128)], banks,
                B["ident_b"], sem_te, sem_sc, [t_c])
            cast_user[tt % 2] = tok_mm
            xb_tok[tt] = t_ev
        xb_ready = (sem_sc, sem_sc.n)

        for dc in range(8):
            banks = [0, 1, 2, 3] if dc % 2 == 0 else [4, 5, 6, 7]
            for dinc in range(8):
                for tg in range(4):
                    def mk(E, b=banks[tg], dcc=dc, di=dinc, tgg=tg):
                        return E.matmul(ps[:, b, :], B["wbufA"][:, di, ts(dcc, 128)],
                                        B["xb"][:, di, ts(tgg, 512)],
                                        start=(di == 0), stop=(di == 7))
                    w = []
                    if dinc == 0:
                        w = pst.write_waits([banks[tg]])
                        if dc == 0:
                            w = w + [tok_wk, xb_tok[4 * tg + 3]]
                    tok = p.on("tensor", mk, waits=w,
                               inc=sem_te if (dinc == 7 and tg == 3) else None)
            pst.wrote(banks, tok)
            for tg in range(4):
                t_ev = p.on("scalar",
                            lambda E, dcc=dc, tgg=tg, b=banks[tg]:
                            E.copy(B["kt"][:, dcc, ts(tgg, 512)], ps[:, b, :]),
                            waits=[tok], inc=sem_sc)
                pst.read([banks[tg]], t_ev)
        kt_ready = (sem_sc, sem_sc.n)

        for tt in range(16):
            bk = [(0, 1), (2, 3), (4, 5), (6, 7)][tt % 4]
            for dinc in range(8):
                for half in range(2):
                    def mk(E, b=bk[half], di=dinc, hh=half, ttt=tt):
                        return E.matmul(ps[:, b, :], B["xb"][:, di, ts(ttt, 128)],
                                        B["wbufB"][:, di, ts(hh, 512)],
                                        start=(di == 0), stop=(di == 7))
                    w = []
                    if dinc == 0:
                        w = pst.write_waits([bk[half]])
                        if half == 0:
                            w = w + [tok_wv, xb_tok[tt]]
                    tok = p.on("tensor", mk, waits=w,
                               inc=sem_te if (dinc == 7 and half == 1) else None)
            pst.wrote(list(bk), tok)
            t_ev = p.on("scalar",
                        lambda E, ttt=tt, b0=bk[0]:
                        E.copy(B["vv"][:, ttt, :],
                               ps[:, b0:b0 + 2, :].rearrange("p c n -> p (c n)")),
                        waits=[tok], inc=sem_sc)
            pst.read(list(bk), t_ev)
        v_ready = (sem_sc, sem_sc.n)

        te_now = (sem_te, sem_te.n)
        tok_wq = load_w_bf16(p, B["wbufA"], wqt, sem_wA, waits=[te_now])
        tok_wo = load_w_bf16(p, B["wbufB"], wot, sem_wB, waits=[te_now])

        pq_free = [None, None, None, None]
        ao_prev = None
        for qg in range(4):
            for dc in range(8):
                bank = dc
                for dinc in range(8):
                    def mk(E, b=bank, di=dinc, dcc=dc, q=qg):
                        return E.matmul(ps[:, b, :], B["wbufA"][:, di, ts(dcc, 128)],
                                        B["xb"][:, di, ts(q, 512)],
                                        start=(di == 0), stop=(di == 7))
                    w = []
                    if dinc == 0:
                        w = pst.write_waits([bank]) + ([tok_wq] if dc == 0 else [])
                    tok = p.on("tensor", mk, waits=w, inc=sem_te if dinc == 7 else None)
                pst.wrote([bank], tok)
                t_ev = p.on("scalar",
                            lambda E, dcc=dc, b=bank: E.activation(
                                B["qt"][:, dcc, :], ps[:, b, :], AF.Copy, scale=0.125),
                            waits=[tok], inc=sem_sc)
                pst.read([bank], t_ev)
            qt_ready = (sem_sc, sem_sc.n)

            def u_id(qt, h, qg=qg):
                return (qt % 2) * 2 + (h % 2 if qg <= 1 else 0)

            def pq_slot(qt, h, qg=qg):
                if qg <= 1:
                    return ("pq", qt % 2, (h % 2) * 1024, [(qt % 2) * 2 + h % 2])
                return ("pq", qt % 2, 0, [(qt % 2) * 2, (qt % 2) * 2 + 1])

            def s_banks(qt, h, qg=qg):
                if qg == 0:
                    return [(0 if qt % 2 == 0 else 4) + h % 2]
                if qg == 1:
                    base = 0 if qt % 2 == 0 else 4
                    return [base + 2 * (h % 2), base + 2 * (h % 2) + 1]
                return [0, 1, 2, 3] if qt % 2 == 0 else [4, 5, 6, 7]

            qt_state = {}

            def emit_S(qt, h):
                c = h // 2
                off = (h % 2) * 64
                k_hi = qg * 512 + qt * 128 + 128
                nb = (k_hi + 511) // 512
                sbanks = s_banks(qt, h)
                sb_used = sbanks[:nb]
                u = u_id(qt, h)
                kb_toks = []
                for kb in range(nb):
                    n = min(512, k_hi - kb * 512)
                    def mk(E, b=sbanks[kb], kbb=kb, nn=n, cc=c, oo=off, qtt=qt):
                        return E.matmul(ps[:, b, 0:nn],
                                        B["qt"][oo:oo + 64, cc, ts(qtt, 128)],
                                        B["kt"][oo:oo + 64, cc, kbb * 512:kbb * 512 + nn],
                                        start=True, stop=True)
                    w = pst.write_waits([sbanks[kb]])
                    if h == 0 and qt == 0 and kb == 0:
                        w = w + [qt_ready, kt_ready]
                    kb_toks.append(p.on("tensor", mk, waits=w, inc=sem_te))
                db = sbanks[(k_hi - 128) // 512]
                dcol = (k_hi - 128) % 512
                t_m = p.on("vector",
                           lambda E, b=db, col=dcol: E.tensor_tensor(
                               ps[:, b, col:col + 128], ps[:, b, col:col + 128],
                               B["mask_ut"][:], ALU.add),
                           waits=[kb_toks[-1]], inc=sem_v)
                pst.wrote(sb_used, t_m)
                t_pm = None
                for kb in range(nb):
                    n = min(512, k_hi - kb * 512)
                    wv = [kb_toks[kb]] if kb < nb - 1 else [t_m]
                    t_pm = p.on("vector",
                                lambda E, b=sbanks[kb], nn=n, kbb=kb, uu=u:
                                E.reduce_max(sm["pmax"][:, uu * 4 + kbb:uu * 4 + kbb + 1],
                                             ps[:, b, 0:nn], AX.X),
                                waits=wv, inc=sem_v)
                    pst.read([sbanks[kb]], t_pm)
                t_nm = p.on("vector",
                            lambda E, uu=u, nbb=nb: E.reduce_max(
                                sm["negm"][:, uu:uu + 1],
                                sm["pmax"][:, uu * 4:uu * 4 + nbb],
                                AX.X, negate=True),
                            waits=[t_pm], inc=sem_v)
                qt_state[(h, qt)] = (k_hi, sbanks, t_nm)

            def emit_expinv(qt, h):
                k_hi, sbanks, t_nm = qt_state[(h, qt)]
                nb = (k_hi + 511) // 512
                u = u_id(qt, h)
                buf, sid, poff, ids = pq_slot(qt, h)
                pfw = [pq_free[i] for i in ids if pq_free[i] is not None]
                if nb >= 3:
                    t_exp = None
                    for kb in range(nb):
                        n = min(512, k_hi - kb * 512)
                        def mkexp(E, b=sbanks[kb], kbb=kb, nn=n, uu=u, bu=buf, si=sid, po=poff):
                            return E.activation(
                                B[bu][:, si, po + kbb * 512:po + kbb * 512 + nn],
                                ps[:, b, 0:nn],
                                AF.Exp, bias=sm["negm"][:, uu:uu + 1], scale=1.0,
                                accum_out=sm["lpart"][:, uu * 4 + kbb:uu * 4 + kbb + 1])
                        w = [t_nm] + (pfw if kb == 0 else [])
                        t_exp = p.on("scalar", mkexp, waits=w, inc=sem_sc)
                        pst.read([sbanks[kb]], t_exp)
                    t_l = p.on("vector",
                               lambda E, uu=u, nbb=nb: E.reduce_sum(
                                   sm["l"][:, uu:uu + 1], sm["lpart"][:, uu * 4:uu * 4 + nbb],
                                   AX.X),
                               waits=[t_exp], inc=sem_v)
                    t_inv = p.on("vector",
                                 lambda E, uu=u: E.reciprocal(sm["invl"][:, uu:uu + 1],
                                                              sm["l"][:, uu:uu + 1]),
                                 waits=[t_l], inc=sem_v)
                else:
                    def mkexp(E, b0=sbanks[0], kh=k_hi, uu=u, bu=buf, si=sid, po=poff, nbb=nb):
                        return E.activation(
                            B[bu][:, si, po:po + kh],
                            ps[:, b0:b0 + nbb, :].rearrange("p c n -> p (c n)")[:, 0:kh]
                            if nbb > 1 else ps[:, b0, 0:kh],
                            AF.Exp, bias=sm["negm"][:, uu:uu + 1], scale=1.0,
                            accum_out=sm["l"][:, uu:uu + 1])
                    t_exp = p.on("scalar", mkexp, waits=[t_nm] + pfw, inc=sem_sc)
                    pst.read(sbanks[:nb], t_exp)
                    t_inv = p.on("vector",
                                 lambda E, uu=u: E.reciprocal(sm["invl"][:, uu:uu + 1],
                                                              sm["l"][:, uu:uu + 1]),
                                 waits=[t_exp], inc=sem_v)
                t_dg = p.on("vector",
                            lambda E, uu=u: E.tensor_scalar(
                                sm["diag_b"][:, uu, :], B["ident_b"][:],
                                sm["invl"][:, uu:uu + 1], None, ALU.mult),
                            waits=[t_inv], inc=sem_v)
                qt_state[(h, qt)] = (k_hi, sbanks, t_dg)

            def emit_PT(qt, h):
                k_hi, sbanks, t_dg = qt_state[(h, qt)]
                kc_n = k_hi // 128
                u = u_id(qt, h)
                buf, sid, poff, ids = pq_slot(qt, h)
                pt_base = (h % 2) * 8 if qg <= 1 else 0
                ev_banks = []
                tok = None
                for kc in range(kc_n):
                    bank = sbanks[min(kc // 4, len(sbanks) - 1)]
                    col = (kc % 4) * 128
                    def mk(E, b=bank, c2=col, kcc=kc, uu=u, bu=buf, si=sid, po=poff):
                        return E.matmul(ps[:, b, c2:c2 + 128],
                                        B[bu][:, si, po + kcc * 128:po + kcc * 128 + 128],
                                        sm["diag_b"][:, uu, :],
                                        start=True, stop=True)
                    w = []
                    if kc == 0:
                        w = [t_dg]
                    if kc % 4 == 0:
                        w = w + pst.write_waits([bank])
                        ev_banks.append(bank)
                    tok = p.on("tensor", mk, waits=w,
                               inc=sem_te if kc == kc_n - 1 else None)
                pst.wrote(ev_banks, tok)
                for i in ids:
                    pq_free[i] = tok
                for bi, bank in enumerate(ev_banks):
                    n_kc = min(4, kc_n - bi * 4)
                    eng = "vector" if (h + bi) % 2 == 0 else "scalar"
                    isem = sem_v if eng == "vector" else sem_sc
                    if eng == "vector":
                        def ev(E, b=bank, bi2=bi, nk=n_kc, q=qt, pb=pt_base):
                            return E.tensor_copy(
                                B["pt"][:, pb + bi2 * 4:pb + bi2 * 4 + nk, ts(q, 128)],
                                ps[:, b, 0:nk * 128].rearrange("p (c n) -> p c n", n=128))
                    else:
                        def ev(E, b=bank, bi2=bi, nk=n_kc, q=qt, pb=pt_base):
                            return E.copy(
                                B["pt"][:, pb + bi2 * 4:pb + bi2 * 4 + nk, ts(q, 128)],
                                ps[:, b, 0:nk * 128].rearrange("p (c n) -> p c n", n=128))
                    t_ev = p.on(eng, ev, waits=[tok], inc=isem)
                    pst.read([bank], t_ev)

            def emit_PV(h):
                c = h // 2
                off = (h % 2) * 64
                pt_ready = (sem_v, sem_v.n)
                pt_ready_sc = (sem_sc, sem_sc.n)
                kc_all = (qg + 1) * 4
                pt_base = (h % 2) * 8 if qg <= 1 else 0
                pv_bank = {0: 6, 1: 6, 2: 3, 3: 3}[qg]
                if qg == 1:
                    pv_bank = 2 * (h % 2)
                tok = None
                for kc in range(kc_all):
                    qs = max(0, kc * 128 - qg * 512)
                    def mk(E, kcc=kc, qss=qs, hh=h, ka=kc_all, pvb=pv_bank, pb=pt_base):
                        return E.matmul(ps[0:64, pvb, qss:512],
                                        B["vv"][:, kcc, hh * 64:hh * 64 + 64],
                                        B["pt"][:, pb + kcc, qss:512],
                                        start=(kcc == 0), stop=(kcc == ka - 1))
                    w = []
                    if kc == 0:
                        w = [pt_ready, pt_ready_sc] + pst.write_waits([pv_bank])
                        if h == 0 and qg == 0:
                            w = w + [v_ready]
                    tok = p.on("tensor", mk, waits=w, inc=sem_te if kc == kc_all - 1 else None)
                pst.wrote([pv_bank], tok)
                t_ev = p.on("scalar",
                            lambda E, cc=c, oo=off, pvb=pv_bank: E.copy(
                                B["obuf"][oo:oo + 64, cc, :], ps[0:64, pvb, :]),
                            waits=[tok], inc=sem_sc)
                pst.read([pv_bank], t_ev)

            for hp in range(H // 2):
                h0 = 2 * hp
                h1 = h0 + 1
                if qg <= 1:
                    emit_S(0, h0); emit_S(0, h1); emit_S(1, h0); emit_S(1, h1)
                    emit_expinv(0, h0); emit_expinv(0, h1)
                    emit_PT(0, h0); emit_PT(0, h1)
                    emit_expinv(1, h0); emit_expinv(1, h1)
                    emit_S(2, h0); emit_S(2, h1)
                    emit_PT(1, h0); emit_PT(1, h1)
                    emit_expinv(2, h0); emit_expinv(2, h1)
                    emit_S(3, h0); emit_S(3, h1)
                    emit_PT(2, h0); emit_PT(2, h1)
                    emit_expinv(3, h0); emit_expinv(3, h1)
                    emit_PT(3, h0); emit_PT(3, h1)
                    emit_PV(h0)
                    emit_PV(h1)
                else:
                    for h in (h0, h1):
                        emit_S(0, h)
                        emit_S(1, h)
                        emit_expinv(0, h)
                        emit_PT(0, h)
                        emit_expinv(1, h)
                        emit_S(2, h)
                        emit_PT(1, h)
                        emit_expinv(2, h)
                        emit_S(3, h)
                        emit_PT(2, h)
                        emit_expinv(3, h)
                        emit_PT(3, h)
                        emit_PV(h)
            obuf_ready = (sem_sc, sem_sc.n)

            for tq in range(4):
                tt = qg * 4 + tq
                banks = [2, 3] if tq % 2 == 0 else [4, 5]
                for dinc in range(8):
                    for half in range(2):
                        def mk(E, b=banks[half], di=dinc, hh=half, tqq=tq):
                            return E.matmul(ps[:, b, :], B["obuf"][:, di, ts(tqq, 128)],
                                            B["wbufB"][:, di, ts(hh, 512)],
                                            start=(di == 0), stop=(di == 7))
                        w = []
                        if dinc == 0:
                            w = pst.write_waits([banks[half]])
                            if half == 0 and tq == 0:
                                w = w + [obuf_ready]
                                if qg == 0:
                                    w = w + [tok_wo]
                        tok = p.on("tensor", mk, waits=w,
                                   inc=sem_te if (dinc == 7 and half == 1) else None)
                pst.wrote(banks, tok)
                sl = B["xtile"][:, tt % 2, :]
                t_ld = p.on("sync", lambda E, dd=sl, sv=xin_s[ts(tt, 128), :]: E.dma_start(dd, sv),
                            waits=[xtile_user[tt % 2]], inc=sem_x[tt % 2], k=16)
                t_st = emit_residual_ln(
                    p, pst, ps, banks, sl, sm, xout_s[ts(tt, 128), :],
                    (sem_v, sem_sc, sem_st[tt % 2]), tok, t_ld,
                    extra_add=B["boutbc"] if bout is not None else None,
                    extra_tok=bout_tok, slot=tt % 2, store_user=list(store_user))
                store_user[tt % 2] = t_st
                xtile_user[tt % 2] = t_st

    for ss in sem_st:
        p.raw("sync", lambda E, s2=ss.h, v=ss.n: E.wait_ge(s2, v))


def alloc_common(nc, stack):
    sb = lambda name, shape, dt: stack.enter_context(nc.sbuf_tensor(name, shape, dt))
    return {
        "ident_b": sb("c_identb", [128, 128], BF16),
        "ident_f": sb("c_identf", [128, 128], F32),
        "mask_ut": sb("c_mask", [128, 128], F32),
        "ones_row": sb("c_ones", [1, 128], BF16),
        "eps": sb("c_eps", [128, 1], F32),
    }


def alloc_ln_smalls(nc, stack, pre, common, with_pretile=True, otile_slots=1):
    sb = lambda name, shape, dt: stack.enter_context(nc.sbuf_tensor(name, shape, dt))
    d = {
        "pretile": sb(pre + "_pre", [128, 1024], F32) if with_pretile else None,
        "negs1": sb(pre + "_negs1", [128, 1], F32),
        "negmu": sb(pre + "_negmu", [128, 1], F32),
        "sqtile": sb(pre + "_sq", [128, 1024], F32),
        "s2": sb(pre + "_s2", [128, 1], F32),
        "lnv": sb(pre + "_lnv", [128, 1], F32),
        "rstd": sb(pre + "_rstd", [128, 1], F32),
        "otile": sb(pre + "_otile", [128, otile_slots, 1024], F32),
        "eps": common["eps"],
    }
    return d


def alloc_attn_bufs(nc, stack, ps, common, pre="A"):
    sb = lambda name, shape, dt: stack.enter_context(nc.sbuf_tensor(name, shape, dt))
    B = {
        "ps": ps,
        "ident_b": common["ident_b"],
        "mask_ut": common["mask_ut"],
        "xb": sb(pre + "_xb", [128, 8, 2048], BF16),
        "kt": sb(pre + "_kt", [128, 8, 2048], BF16),
        "vv": sb(pre + "_v", [128, 16, 1024], BF16),
        "qt": sb(pre + "_qt", [128, 8, 512], BF16),
        "pt": sb(pre + "_pt", [128, 16, 512], BF16),
        "pq": sb(pre + "_pq", [128, 2, 2048], BF16),
        "pqs": sb(pre + "_pqs", [128, 2, 512], BF16),
        "obuf": sb(pre + "_obuf", [128, 8, 512], BF16),
        "wbufA": sb(pre + "_wA", [128, 8, 1024], BF16),
        "wbufB": sb(pre + "_wB", [128, 8, 1024], BF16),
        "xtile": sb(pre + "_xtile", [128, 2, 1024], F32),
        "boutbc": sb(pre + "_boutbc", [128, 1024], F32),
    }
    B["sm"] = alloc_ln_smalls(nc, stack, pre + "ln", common, otile_slots=2)
    B["sm"].update({
        "cast_b": sb(pre + "_cast", [128, 2 * 1024], BF16),
        "negm": sb(pre + "_negm", [128, 4], F32),
        "pmax": sb(pre + "_pmax", [128, 16], F32),
        "lpart": sb(pre + "_lpart", [128, 16], F32),
        "l": sb(pre + "_l", [128, 4], F32),
        "invl": sb(pre + "_invl", [128, 4], F32),
        "diag_b": sb(pre + "_diagb", [128, 4, 128], BF16),
        "ones_row": common["ones_row"],
    })
    return B



def alloc_moe0_bufs(nc, stack, ps, common):
    sb = lambda name, shape, dt: stack.enter_context(nc.sbuf_tensor(name, shape, dt))
    B = {
        "ps": ps,
        "ident_b": common["ident_b"],
        "ident_f": common["ident_f"],
        "ones_row": common["ones_row"],
        "x1b": sb("B_x1b", [128, 8, 1024], BF16),
        "yacc": sb("B_yacc", [128, 8, 1024], F32),
        "hh": sb("B_h", [128, 16, 1024], BF16),
        "sg": sb("B_sg", [128, 2, 1024], BF16),
        "gate": sb("B_gate", [128, 2, 8, 512], BF16),
        "up": sb("B_up", [128, 2, 8, 512], BF16),
        "down": sb("B_down", [128, 16, 1024], BF16),
        "rwb": sb("B_rwb", [128, 8, 8], BF16),
        "rbbc": sb("B_rbbc", [128, 8], F32),
        "rbrow": sb("B_rbrow", [1, 8], F32),
        "rbrowb": sb("B_rbrowb", [1, 8], BF16),
        "dbb": sb("B_dbb", [8, 1024], BF16),
        "lg": sb("B_lg", [128, 8], F32),
        "p8": sb("B_p8", [128, 8], F32),
        "probs": sb("B_probs", [128, 8], F32),
        "eqm": sb("B_eqm", [128, 8], F32),
        "masked": sb("B_masked", [128, 8], F32),
        "keep": sb("B_keep", [128, 8], F32),
        "m1": sb("B_m1", [128, 1], F32),
        "m2": sb("B_m2", [128, 1], F32),
        "negm8": sb("B_negm8", [128, 1], F32),
        "se": sb("B_se", [128, 1], F32),
        "inv": sb("B_inv", [128, 1], F32),
        "s12": sb("B_s12", [128, 1], F32),
        "inv12": sb("B_inv12", [128, 1], F32),
        "wall": sb("B_wall", [128, 8, 8], F32),
        "wallb": sb("B_wallb", [128, 8, 8], BF16),
        "wT": sb("B_wT", [8, 8, 128], BF16),
        "gbt": sb("B_gbt", [16, 128], F32),
        "gbcol": sb("B_gbcol", [128, 16], F32),
        "ubt": sb("B_ubt", [16, 128], F32),
        "ubcol": sb("B_ubcol", [128, 16], F32),
    }
    B["sm"] = alloc_ln_smalls(nc, stack, "B", common, with_pretile=False, otile_slots=2)
    B["sm"]["cast_b"] = sb("B_cast", [128, 2 * 1024], BF16)
    return B


def build_moe0_phase(p, B, xin, xout, w, tag):
    ps = B["ps"]
    sm = B["sm"]
    sem_te = p.sem(tag + "_te")
    sem_v = p.sem(tag + "_v")
    sem_sc = p.sem(tag + "_sc")
    sem_yl = [p.sem(tag + "_yl%d" % i) for i in range(8)]
    sem_gs = [p.sem(tag + "_gs%d" % i) for i in range(2)]
    sem_us = [p.sem(tag + "_us%d" % i) for i in range(2)]
    sem_dn = p.sem(tag + "_dn")
    sem_gb = p.sem(tag + "_gb")
    sem_ub = p.sem(tag + "_ub")
    sem_rw = p.sem(tag + "_rw")
    sem_rb = p.sem(tag + "_rb")
    sem_db = p.sem(tag + "_db")
    sem_st = [p.sem(tag + "_st0"), p.sem(tag + "_st1")]
    pst = PsumTracker()

    t_rw = p.on("gpsimd", lambda E: E.dma_start(
        B["rwb"][:], w["rw"].rearrange("(c p) e -> p c e", p=128, c=8)),
        inc=sem_rw, k=16)
    t_rb = p.on("sync", lambda E: E.dma_start(B["rbrow"][:], w["rb"][None, :]),
                inc=sem_rb, k=16)
    t_rbb = p.on("vector", lambda E: E.tensor_copy(B["rbrowb"][:], B["rbrow"][:]),
                 waits=[t_rb], inc=sem_v)
    t_mm = p.on("tensor", lambda E: E.matmul(ps[:, 0, 0:8], B["ones_row"][:],
                                             B["rbrowb"][:], start=True, stop=True),
                waits=[t_rbb] + pst.write_waits([0]), inc=sem_te)
    pst.wrote([0], t_mm)
    t_rbbc = p.on("scalar", lambda E: E.copy(B["rbbc"][:], ps[:, 0, 0:8]),
                  waits=[t_mm], inc=sem_sc)
    pst.read([0], t_rbbc)
    t_db = p.on("gpsimd", lambda E: E.dma_start(B["dbb"][:], w["db"][:, :]),
                inc=sem_db, k=16)

    store_user = [None, None]
    yacc_user = [None] * 8
    wallb_user = [None] * 8
    sg_user = [None, None]
    gu_user = [None, None]
    down_user = None
    gbt_user = None
    ubt_user = None
    cast_user = [None, None]

    for swp in range(4):
        acc_tok = [None] * 8
        for tt in range(8):
            gt = swp * 8 + tt
            t_ld = p.on("sync",
                        lambda E, d=B["yacc"][:, tt, :], sv=xin[ts(gt, 128), :]:
                        E.dma_start(d, sv),
                        waits=[yacc_user[tt]], inc=sem_yl[tt], k=16)
            cs = sm["cast_b"][:, ts(tt % 2, D)]
            t_c = p.on("vector", lambda E, d=cs, s2=B["yacc"][:, tt, :]:
                       E.tensor_copy(d, s2),
                       waits=[t_ld, cast_user[tt % 2]], inc=sem_v)
            banks = (0, 1) if tt % 2 == 0 else (2, 3)
            tok_mm, t_ev = emit_tile_transpose(
                p, pst, ps, cs, B["x1b"][:, 0:8, ts(tt, 128)], banks,
                B["ident_b"], sem_te, sem_sc, [t_c])
            cast_user[tt % 2] = tok_mm
            for dinc in range(8):
                def mk(E, di=dinc, ttt=tt):
                    return E.matmul(ps[:, 4, 0:8], B["x1b"][:, di, ts(ttt, 128)],
                                    B["rwb"][:, di, :], start=(di == 0), stop=(di == 7))
                wts = []
                if dinc == 0:
                    wts = pst.write_waits([4]) + [t_ev, t_rw]
                t_r = p.on("tensor", mk, waits=wts, inc=sem_te if dinc == 7 else None)
            pst.wrote([4], t_r)
            t_lg = p.on("vector", lambda E: E.tensor_tensor(
                B["lg"][:], ps[:, 4, 0:8], B["rbbc"][:], ALU.add),
                waits=[t_r, t_rbbc], inc=sem_v)
            pst.read([4], t_lg)
            t_nm = p.on("vector", lambda E: E.reduce_max(
                B["negm8"][:], B["lg"][:], AX.X, negate=True), waits=[t_lg], inc=sem_v)
            t_p8 = p.on("scalar", lambda E: E.activation(
                B["p8"][:], B["lg"][:], AF.Exp, bias=B["negm8"][:],
                accum_out=B["se"][:]), waits=[t_nm], inc=sem_sc)
            t_inv = p.on("vector", lambda E: E.reciprocal(B["inv"][:], B["se"][:]),
                         waits=[t_p8], inc=sem_v)
            t_pr = p.on("vector", lambda E: E.tensor_scalar(
                B["probs"][:], B["p8"][:], B["inv"][:], None, ALU.mult),
                waits=[t_inv], inc=sem_v)
            t_m1 = p.on("vector", lambda E: E.reduce_max(B["m1"][:], B["probs"][:], AX.X),
                        waits=[t_pr], inc=sem_v)
            t_eq = p.on("vector", lambda E: E.tensor_scalar(
                B["eqm"][:], B["probs"][:], B["m1"][:], None, ALU.is_ge),
                waits=[t_m1], inc=sem_v)
            t_ms = p.on("vector", lambda E: E.scalar_tensor_tensor(
                B["masked"][:], B["eqm"][:], NEG_BIG, B["probs"][:],
                ALU.mult, ALU.add), waits=[t_eq], inc=sem_v)
            t_m2 = p.on("vector", lambda E: E.reduce_max(B["m2"][:], B["masked"][:], AX.X),
                        waits=[t_ms], inc=sem_v)
            t_kp = p.on("vector", lambda E: E.tensor_scalar(
                B["keep"][:], B["probs"][:], B["m2"][:], None, ALU.is_ge),
                waits=[t_m2], inc=sem_v)
            t_s12 = p.on("vector", lambda E: E.tensor_tensor(
                B["s12"][:], B["m1"][:], B["m2"][:], ALU.add), waits=[t_kp], inc=sem_v)
            t_i12 = p.on("vector", lambda E: E.reciprocal(B["inv12"][:], B["s12"][:]),
                         waits=[t_s12], inc=sem_v)
            t_w = p.on("vector", lambda E, ttt=tt: E.scalar_tensor_tensor(
                B["wall"][:, ttt, :], B["probs"][:], B["inv12"][:], B["keep"][:],
                ALU.mult, ALU.mult), waits=[t_i12], inc=sem_v)
            t_wb = p.on("vector", lambda E, ttt=tt: E.tensor_copy(
                B["wallb"][:, ttt, :], B["wall"][:, ttt, :]),
                waits=[t_w, wallb_user[tt]], inc=sem_v)
            t_wt = p.on("tensor", lambda E, ttt=tt: E.matmul(
                ps[0:8, 5, 0:128], B["wallb"][:, ttt, :], B["ident_b"][:],
                start=True, stop=True),
                waits=[t_wb] + pst.write_waits([5]), inc=sem_te)
            pst.wrote([5], t_wt)
            t_wte = p.on("scalar", lambda E, ttt=tt: E.copy(
                B["wT"][0:8, ttt, :], ps[0:8, 5, 0:128]), waits=[t_wt], inc=sem_sc)
            pst.read([5], t_wte)
            for half in range(2):
                def mk(E, hh=half, ttt=tt):
                    return E.matmul(ps[:, 6 + hh, :], B["wT"][0:8, ttt, :],
                                    B["dbb"][0:8, ts(hh, 512)], start=True, stop=True)
                wts = [t_wte] + pst.write_waits([6 + half]) + ([t_db] if tt == 0 and swp == 0 else [])
                t_ini = p.on("tensor", mk, waits=wts, inc=sem_te if half == 1 else None)
            pst.wrote([6, 7], t_ini)
            t_acc0 = p.on("vector", lambda E, ttt=tt: E.tensor_tensor(
                B["yacc"][:, ttt, :],
                ps[:, 6:8, :].rearrange("p c n -> p (c n)"),
                B["yacc"][:, ttt, :], ALU.add),
                waits=[t_ini, t_ld], inc=sem_v)
            pst.read([6, 7], t_acc0)
            acc_tok[tt] = t_acc0
            wallb_user[tt] = t_wt
        x1b_ready = (sem_sc, sem_sc.n)

        for e in range(8):
            t_gb = p.on("sync", lambda E, ee=e: E.dma_start(
                B["gbt"][:], w["gb"][ee].rearrange("(c p) -> c p", p=128)),
                waits=[gbt_user], inc=sem_gb, k=16)
            t_gbm = p.on("tensor", lambda E: E.matmul(
                ps[:, 4, 0:16], B["gbt"][0:16, :], B["ident_f"][0:16, 0:16],
                start=True, stop=True), waits=[t_gb] + pst.write_waits([4]), inc=sem_te)
            pst.wrote([4], t_gbm)
            t_gbc = p.on("scalar", lambda E: E.copy(B["gbcol"][:], ps[:, 4, 0:16]),
                         waits=[t_gbm], inc=sem_sc)
            pst.read([4], t_gbc)
            t_ub = p.on("sync", lambda E, ee=e: E.dma_start(
                B["ubt"][:], w["ub"][ee].rearrange("(c p) -> c p", p=128)),
                waits=[ubt_user], inc=sem_ub, k=16)
            t_ubm = p.on("tensor", lambda E: E.matmul(
                ps[:, 5, 0:16], B["ubt"][0:16, :], B["ident_f"][0:16, 0:16],
                start=True, stop=True), waits=[t_ub] + pst.write_waits([5]), inc=sem_te)
            pst.wrote([5], t_ubm)
            t_ubc = p.on("scalar", lambda E: E.copy(B["ubcol"][:], ps[:, 5, 0:16]),
                         waits=[t_ubm], inc=sem_sc)
            pst.read([5], t_ubc)
            gbt_user = t_gbm
            ubt_user = t_ubm

            slice_tok = {}
            for isl in range(2):
                slice_tok[isl] = (
                    p.on("gpsimd", lambda E, ee=e, ii=isl: E.dma_start(
                        B["gate"][:, ii % 2, :, :],
                        w["gw"][ee][:, ts(ii, 512)].rearrange("(c p) i -> p c i", p=128, c=8)),
                        waits=[gu_user[isl % 2]], inc=sem_gs[isl % 2], k=16),
                    p.on("gpsimd", lambda E, ee=e, ii=isl: E.dma_start(
                        B["up"][:, ii % 2, :, :],
                        w["uw"][ee][:, ts(ii, 512)].rearrange("(c p) i -> p c i", p=128, c=8)),
                        inc=sem_us[isl % 2], k=16),
                )

            t_dw = p.on("gpsimd", lambda E, ee=e: E.dma_start(
                B["down"][:], w["dw"][ee].rearrange("(c p) n -> p c n", p=128, c=16)),
                waits=[down_user], inc=sem_dn, k=16)

            for ic in range(16):
                isl = ic // 4
                il = ic % 4
                slot = isl % 2
                gb_banks = (0, 1) if ic % 2 == 0 else (4, 5)
                ub_banks = (2, 3) if ic % 2 == 0 else (6, 7)
                gtok, utok = slice_tok[isl]
                for dinc in range(8):
                    for tg in range(2):
                        def mk(E, b=gb_banks[tg], di=dinc, sl=slot, ill=il, tgg=tg):
                            return E.matmul(ps[:, b, :],
                                            B["gate"][:, sl, di, ts(ill, 128)][:, 0:128],
                                            B["x1b"][:, di, ts(tgg, 512)],
                                            start=(di == 0), stop=(di == 7))
                        wts = []
                        if dinc == 0:
                            wts = pst.write_waits([gb_banks[tg]])
                            if tg == 0:
                                wts = wts + [gtok, x1b_ready]
                        t_g = p.on("tensor", mk, waits=wts,
                                   inc=sem_te if (dinc == 7 and tg == 1) else None)
                pst.wrote(list(gb_banks), t_g)
                for tg in range(2):
                    def mksilu(E, b=gb_banks[tg], icc=ic, tgg=tg):
                        return E.activation(B["sg"][:, icc % 2, ts(tgg, 512)],
                                            ps[:, b, :], AF.Silu,
                                            bias=B["gbcol"][:, icc:icc + 1])
                    t_sg = p.on("scalar", mksilu,
                                waits=[t_g, t_gbc] + ([sg_user[ic % 2]] if tg == 0 else []),
                                inc=sem_sc)
                    pst.read([gb_banks[tg]], t_sg)
                for dinc in range(8):
                    for tg in range(2):
                        def mk(E, b=ub_banks[tg], di=dinc, sl=slot, ill=il, tgg=tg):
                            return E.matmul(ps[:, b, :],
                                            B["up"][:, sl, di, ts(ill, 128)][:, 0:128],
                                            B["x1b"][:, di, ts(tgg, 512)],
                                            start=(di == 0), stop=(di == 7))
                        wts = []
                        if dinc == 0:
                            wts = pst.write_waits([ub_banks[tg]])
                            if tg == 0:
                                wts = wts + [utok]
                        t_u = p.on("tensor", mk, waits=wts,
                                   inc=sem_te if (dinc == 7 and tg == 1) else None)
                pst.wrote(list(ub_banks), t_u)
                sg_done = None
                for tg in range(2):
                    def mkh(E, b=ub_banks[tg], icc=ic, tgg=tg):
                        return E.scalar_tensor_tensor(
                            B["hh"][:, icc, ts(tgg, 512)], ps[:, b, :],
                            B["ubcol"][:, icc:icc + 1], B["sg"][:, icc % 2, ts(tgg, 512)],
                            ALU.add, ALU.mult)
                    t_h = p.on("vector", mkh, waits=[t_u, t_ubc, t_sg], inc=sem_v)
                    pst.read([ub_banks[tg]], t_h)
                sg_user[ic % 2] = t_h
                if il == 3 and isl + 2 < 4:
                    te_now = (sem_te, sem_te.n)
                    gu_user[slot] = te_now
                    slice_tok[isl + 2] = (
                        p.on("gpsimd", lambda E, ee=e, ii=isl + 2: E.dma_start(
                            B["gate"][:, ii % 2, :, :],
                            w["gw"][ee][:, ts(ii, 512)].rearrange("(c p) i -> p c i", p=128, c=8)),
                            waits=[te_now], inc=sem_gs[(isl + 2) % 2], k=16),
                        p.on("gpsimd", lambda E, ee=e, ii=isl + 2: E.dma_start(
                            B["up"][:, ii % 2, :, :],
                            w["uw"][ee][:, ts(ii, 512)].rearrange("(c p) i -> p c i", p=128, c=8)),
                            inc=sem_us[(isl + 2) % 2], k=16),
                    )
                elif il == 3:
                    gu_user[slot] = (sem_te, sem_te.n)
            h_ready = (sem_v, sem_v.n)

            for tt in range(8):
                bk = [(0, 1), (2, 3), (4, 5), (6, 7)][tt % 4]
                for ic in range(16):
                    for half in range(2):
                        def mk(E, b=bk[half], icc=ic, hh=half, ttt=tt):
                            return E.matmul(ps[:, b, :], B["hh"][:, icc, ts(ttt, 128)],
                                            B["down"][:, icc, ts(hh, 512)],
                                            start=(icc == 0), stop=(icc == 15))
                        wts = []
                        if ic == 0:
                            wts = pst.write_waits([bk[half]])
                            if half == 0:
                                wts = wts + ([h_ready, t_dw] if tt == 0 else [])
                        t_dn = p.on("tensor", mk, waits=wts,
                                    inc=sem_te if (ic == 15 and half == 1) else None)
                pst.wrote(list(bk), t_dn)
                t_acc = p.on("vector", lambda E, ttt=tt, ee=e, b0=bk[0]:
                             E.scalar_tensor_tensor(
                                 B["yacc"][:, ttt, :],
                                 ps[:, b0:b0 + 2, :].rearrange("p c n -> p (c n)"),
                                 B["wall"][:, ttt, ee:ee + 1],
                                 B["yacc"][:, ttt, :], ALU.mult, ALU.add),
                             waits=[t_dn, acc_tok[tt]], inc=sem_v)
                pst.read(list(bk), t_acc)
                acc_tok[tt] = t_acc
            down_user = (sem_te, sem_te.n)

        for tt in range(8):
            gt = swp * 8 + tt
            t_o, t_st = emit_ln_sbuf(p, sm, B["yacc"][:, tt, :], xout[ts(gt, 128), :],
                                     (sem_v, sem_sc, sem_st[tt % 2]), acc_tok[tt],
                                     slot=tt % 2, store_user=list(store_user))
            store_user[tt % 2] = t_st
            yacc_user[tt] = t_o

    for ss in sem_st:
        p.raw("sync", lambda E, s2=ss.h, v=ss.n: E.wait_ge(s2, v))



def alloc_ffn1_bufs(nc, stack, ps, common):
    sb = lambda name, shape, dt: stack.enter_context(nc.sbuf_tensor(name, shape, dt))
    B = {
        "ps": ps,
        "ident_b": common["ident_b"],
        "ones_row": common["ones_row"],
        "x3b": sb("D_x3b", [128, 8, 1024], BF16),
        "yacc": sb("D_yacc", [128, 8, 1024], F32),
        "hh": sb("D_h", [128, 32, 1024], BF16),
        "w1": sb("D_w1", [128, 2, 8, 256], BF16),
        "w2": sb("D_w2", [128, 4, 4, 1024], BF16),
        "b1t": sb("D_b1t", [32, 128], BF16),
        "b1col": sb("D_b1col", [128, 32], F32),
        "b2b": sb("D_b2b", [8, 1024], BF16),
        "sall": sb("D_sall", [128, 8], F32),
        "habs": sb("D_habs", [128, 8], F32),
        "hint": sb("D_hint", [128, 8], I32),
        "hcf": sb("D_hcf", [128, 8], F32),
        "hfl": sb("D_hfl", [128, 8], F32),
        "hsgn": sb("D_hsgn", [128, 8], F32),
        "htr": sb("D_htr", [128, 8], F32),
        "hall": sb("D_hall", [128, 8], F32),
        "eqt": sb("D_eqt", [128, 8], F32),
        "eq2": sb("D_eq2", [128, 8], F32),
        "wall": sb("D_wall", [128, 8, 8], F32),
        "wallb": sb("D_wallb", [128, 8, 8], BF16),
        "wT": sb("D_wT", [8, 8, 128], BF16),
    }
    B["sm"] = alloc_ln_smalls(nc, stack, "D", common, with_pretile=False, otile_slots=2)
    B["sm"]["cast_b"] = sb("D_cast", [128, 2 * 1024], BF16)
    return B


def build_ffn1_phase(p, B, xin, xout, w, tag):
    ps = B["ps"]
    sm = B["sm"]
    sem_te = p.sem(tag + "_te")
    sem_v = p.sem(tag + "_v")
    sem_sc = p.sem(tag + "_sc")
    sem_yl = [p.sem(tag + "_yl%d" % i) for i in range(8)]
    sem_w1 = [p.sem(tag + "_w1s%d" % i) for i in range(4)]
    sem_w2 = [p.sem(tag + "_w2s%d" % i) for i in range(4)]
    sem_b1 = p.sem(tag + "_b1")
    sem_b2 = p.sem(tag + "_b2")
    sem_st = [p.sem(tag + "_st0"), p.sem(tag + "_st1")]
    pst = PsumTracker()

    t_b2 = p.on("gpsimd", lambda E: E.dma_start(B["b2b"][:], w["b2"][:, :]),
                inc=sem_b2, k=16)

    store_user = [None, None]
    yacc_user = [None] * 8
    wallb_user = [None] * 8
    w1_user = [None, None, None, None]
    w2s_user = [None, None, None, None]
    b1t_user = None
    cast_user = [None, None]

    for swp in range(4):
        acc_tok = [None] * 8
        ld_toks = [None] * 8
        for tt in range(8):
            gt = swp * 8 + tt
            t_ld = p.on("sync",
                        lambda E, d=B["yacc"][:, tt, :], sv=xin[ts(gt, 128), :]:
                        E.dma_start(d, sv),
                        waits=[yacc_user[tt]], inc=sem_yl[tt], k=16)
            ld_toks[tt] = t_ld
            cs = sm["cast_b"][:, ts(tt % 2, D)]
            t_c = p.on("vector", lambda E, d=cs, s2=B["yacc"][:, tt, :]:
                       E.tensor_copy(d, s2),
                       waits=[t_ld, cast_user[tt % 2]], inc=sem_v)
            banks = (0, 1) if tt % 2 == 0 else (2, 3)
            tok_mm, t_ev = emit_tile_transpose(
                p, pst, ps, cs, B["x3b"][:, 0:8, ts(tt, 128)], banks,
                B["ident_b"], sem_te, sem_sc, [t_c])
            cast_user[tt % 2] = tok_mm
            t_s = p.on("vector", lambda E, ttt=tt: E.tensor_tensor(
                B["sall"][:, ttt:ttt + 1], B["yacc"][:, ttt, 0:1],
                B["yacc"][:, ttt, 1:2], ALU.add), waits=[t_ld], inc=sem_v)
        x3b_ready = (sem_sc, sem_sc.n)

        t = p.on("vector", lambda E: E.tensor_scalar(
            B["habs"][:], B["sall"][:], -1.0, None, ALU.mult), waits=[t_s], inc=sem_v)
        t = p.on("vector", lambda E: E.tensor_tensor(
            B["habs"][:], B["habs"][:], B["sall"][:], ALU.max), waits=[t], inc=sem_v)
        t = p.on("vector", lambda E: E.tensor_copy(B["hint"][:], B["habs"][:]),
                 waits=[t], inc=sem_v)
        t = p.on("vector", lambda E: E.tensor_copy(B["hcf"][:], B["hint"][:]),
                 waits=[t], inc=sem_v)
        t = p.on("vector", lambda E: E.tensor_tensor(
            B["hfl"][:], B["hcf"][:], B["habs"][:], ALU.is_gt), waits=[t], inc=sem_v)
        t = p.on("vector", lambda E: E.tensor_tensor(
            B["hfl"][:], B["hcf"][:], B["hfl"][:], ALU.subtract), waits=[t], inc=sem_v)
        t = p.on("vector", lambda E: E.tensor_scalar(
            B["hsgn"][:], B["sall"][:], 0.0, None, ALU.is_ge), waits=[t], inc=sem_v)
        t = p.on("vector", lambda E: E.tensor_scalar(
            B["hsgn"][:], B["hsgn"][:], 2.0, -1.0, ALU.mult, ALU.add), waits=[t], inc=sem_v)
        t = p.on("vector", lambda E: E.tensor_tensor(
            B["htr"][:], B["hfl"][:], B["hsgn"][:], ALU.mult), waits=[t], inc=sem_v)
        t = p.on("vector", lambda E: E.tensor_scalar(
            B["htr"][:], B["htr"][:], 800.0, None, ALU.add), waits=[t], inc=sem_v)
        t = p.on("vector", lambda E: E.tensor_copy(B["hint"][:], B["htr"][:]),
                 waits=[t], inc=sem_v)
        t = p.on("vector", lambda E: E.tensor_scalar(
            B["hint"][:], B["hint"][:], 7, None, ALU.bitwise_and), waits=[t], inc=sem_v)
        t_hall = p.on("vector", lambda E: E.tensor_copy(B["hall"][:], B["hint"][:]),
                      waits=[t], inc=sem_v)
        prev_mask_tok = None
        for e in range(8):
            t1 = p.on("vector", lambda E, ee=e: E.tensor_scalar(
                B["eqt"][:], B["hall"][:], float(ee), None, ALU.is_equal),
                waits=[t_hall, prev_mask_tok], inc=sem_v)
            t2 = p.on("vector", lambda E, ee=e: E.tensor_scalar(
                B["eq2"][:], B["hall"][:], float((ee + 7) % 8), None, ALU.is_equal),
                waits=[t1], inc=sem_v)
            t3 = p.on("vector", lambda E: E.tensor_tensor(
                B["eqt"][:], B["eqt"][:], B["eq2"][:], ALU.add), waits=[t2], inc=sem_v)
            prev_mask_tok = p.on("vector", lambda E, ee=e: E.tensor_scalar(
                B["wall"][:, 0:8, ee], B["eqt"][:], 0.5, None, ALU.mult),
                waits=[t3], inc=sem_v)
        w_ready_v = (sem_v, sem_v.n)

        for tt in range(8):
            t_wb = p.on("vector", lambda E, ttt=tt: E.tensor_copy(
                B["wallb"][:, ttt, :], B["wall"][:, ttt, :]),
                waits=[w_ready_v, wallb_user[tt]], inc=sem_v)
            t_wt = p.on("tensor", lambda E, ttt=tt: E.matmul(
                ps[0:8, 4, 0:128], B["wallb"][:, ttt, :], B["ident_b"][:],
                start=True, stop=True),
                waits=[t_wb] + pst.write_waits([4]), inc=sem_te)
            pst.wrote([4], t_wt)
            t_wte = p.on("scalar", lambda E, ttt=tt: E.copy(
                B["wT"][0:8, ttt, :], ps[0:8, 4, 0:128]), waits=[t_wt], inc=sem_sc)
            pst.read([4], t_wte)
            for half in range(2):
                def mk(E, hh=half, ttt=tt):
                    return E.matmul(ps[:, 6 + hh, :], B["wT"][0:8, ttt, :],
                                    B["b2b"][0:8, ts(hh, 512)], start=True, stop=True)
                wts = [t_wte] + pst.write_waits([6 + half]) + ([t_b2] if tt == 0 and swp == 0 else [])
                t_ini = p.on("tensor", mk, waits=wts, inc=sem_te if half == 1 else None)
            pst.wrote([6, 7], t_ini)
            t_acc0 = p.on("vector", lambda E, ttt=tt: E.tensor_tensor(
                B["yacc"][:, ttt, :],
                ps[:, 6:8, :].rearrange("p c n -> p (c n)"),
                B["yacc"][:, ttt, :], ALU.add),
                waits=[t_ini, ld_toks[tt]], inc=sem_v)
            pst.read([6, 7], t_acc0)
            acc_tok[tt] = t_acc0
            wallb_user[tt] = t_wt

        for e in range(8):
            t_b1 = p.on("gpsimd", lambda E, ee=e: E.dma_start(
                B["b1t"][:], w["b1"][ee].rearrange("(c p) -> c p", p=128)),
                waits=[b1t_user], inc=sem_b1, k=16)
            t_b1m = p.on("tensor", lambda E: E.matmul(
                ps[:, 4, 0:32], B["b1t"][0:32, :], B["ident_b"][0:32, 0:32],
                start=True, stop=True), waits=[t_b1] + pst.write_waits([4]), inc=sem_te)
            pst.wrote([4], t_b1m)
            t_b1c = p.on("scalar", lambda E: E.copy(B["b1col"][:], ps[:, 4, 0:32]),
                         waits=[t_b1m], inc=sem_sc)
            pst.read([4], t_b1c)
            b1t_user = t_b1m

            slice_tok = {}
            for isl in range(2):
                slice_tok[isl] = p.on("gpsimd", lambda E, ee=e, ii=isl: E.dma_start(
                    B["w1"][:, ii % 2, :, :],
                    w["w1"][ee][:, ts(ii, 256)].rearrange("(c p) i -> p c i", p=128, c=8)),
                    waits=[w1_user[isl % 2]], inc=sem_w1[isl % 2], k=16)

            for ic in range(32):
                isl = ic // 2
                il = ic % 2
                slot = isl % 2
                bk = [(0, 1), (2, 3), (4, 5), (6, 7)][ic % 4]
                gtok = slice_tok[isl]
                for dinc in range(8):
                    for tg in range(2):
                        def mk(E, b=bk[tg], di=dinc, sl=slot, ill=il, tgg=tg):
                            return E.matmul(ps[:, b, :],
                                            B["w1"][:, sl, di, ts(ill, 128)],
                                            B["x3b"][:, di, ts(tgg, 512)],
                                            start=(di == 0), stop=(di == 7))
                        wts = []
                        if dinc == 0:
                            wts = pst.write_waits([bk[tg]])
                            if tg == 0:
                                wts = wts + [gtok, x3b_ready]
                        t_g = p.on("tensor", mk, waits=wts,
                                   inc=sem_te if (dinc == 7 and tg == 1) else None)
                pst.wrote(list(bk), t_g)
                for tg in range(2):
                    def mkrelu(E, b=bk[tg], icc=ic, tgg=tg):
                        return E.activation(B["hh"][:, icc, ts(tgg, 512)],
                                            ps[:, b, :], AF.Relu,
                                            bias=B["b1col"][:, icc:icc + 1])
                    t_r = p.on("scalar", mkrelu, waits=[t_g, t_b1c], inc=sem_sc)
                    pst.read([bk[tg]], t_r)
                if il == 1 and isl + 2 < 16:
                    te_now = (sem_te, sem_te.n)
                    w1_user[slot] = te_now
                    slice_tok[isl + 2] = p.on("gpsimd", lambda E, ee=e, ii=isl + 2: E.dma_start(
                        B["w1"][:, ii % 2, :, :],
                        w["w1"][ee][:, ts(ii, 256)].rearrange("(c p) i -> p c i", p=128, c=8)),
                        waits=[te_now], inc=sem_w1[(isl + 2) % 2], k=16)
                elif il == 1:
                    w1_user[slot] = (sem_te, sem_te.n)
            h_ready = (sem_sc, sem_sc.n)

            for ttg in range(2):
                w2tok = {}
                for icg in range(4):
                    w2tok[icg] = p.on("gpsimd", lambda E, ee=e, ig=icg: E.dma_start(
                        B["w2"][:, ig % 4, :, :],
                        w["w2"][ee][ts(ig, 512), :].rearrange("(c p) n -> p c n", p=128, c=4)),
                        waits=[w2s_user[icg % 4]], inc=sem_w2[icg % 4], k=16)
                dn_tok = [None] * 4
                for icg in range(8):
                    for ic in range(icg * 4, icg * 4 + 4):
                        for ti in range(4):
                            tt = ttg * 4 + ti
                            bkp = [(0, 1), (2, 3), (4, 5), (6, 7)][ti]
                            for half in range(2):
                                def mk(E, b=bkp[half], icc=ic, hh=half, ttt=tt, sl=icg % 4):
                                    return E.matmul(ps[:, b, :],
                                                    B["hh"][:, icc, ts(ttt, 128)],
                                                    B["w2"][:, sl, icc % 4, ts(hh, 512)],
                                                    start=(icc == 0), stop=(icc == 31))
                                wts = []
                                if ic == 0:
                                    wts = pst.write_waits([bkp[half]])
                                    if half == 0 and ti == 0:
                                        wts = wts + [h_ready, w2tok[0]]
                                elif ic % 4 == 0 and ti == 0 and half == 0:
                                    wts = [w2tok[icg]]
                                do_inc = (ic == 31 and half == 1) or (
                                    ic % 4 == 3 and ti == 3 and half == 1)
                                t_dn = p.on("tensor", mk, waits=wts,
                                            inc=sem_te if do_inc else None)
                            if ic == 31:
                                dn_tok[ti] = t_dn
                    if icg + 4 < 8:
                        te_now = (sem_te, sem_te.n)
                        w2s_user[icg % 4] = te_now
                        w2tok[icg + 4] = p.on("gpsimd", lambda E, ee=e, ig=icg + 4: E.dma_start(
                            B["w2"][:, ig % 4, :, :],
                            w["w2"][ee][ts(ig, 512), :].rearrange("(c p) n -> p c n", p=128, c=4)),
                            waits=[te_now], inc=sem_w2[(icg + 4) % 4], k=16)
                    else:
                        w2s_user[icg % 4] = (sem_te, sem_te.n)
                for ti in range(4):
                    tt = ttg * 4 + ti
                    bkp = [(0, 1), (2, 3), (4, 5), (6, 7)][ti]
                    pst.wrote(list(bkp), dn_tok[ti])
                    t_acc = p.on("vector", lambda E, ttt=tt, ee=e, b0=bkp[0]:
                                 E.scalar_tensor_tensor(
                                     B["yacc"][:, ttt, :],
                                     ps[:, b0:b0 + 2, :].rearrange("p c n -> p (c n)"),
                                     B["wall"][:, ttt, ee:ee + 1],
                                     B["yacc"][:, ttt, :], ALU.mult, ALU.add),
                                 waits=[dn_tok[ti], acc_tok[tt]], inc=sem_v)
                    pst.read(list(bkp), t_acc)
                    acc_tok[tt] = t_acc

        for tt in range(8):
            gt = swp * 8 + tt
            t_o, t_st = emit_ln_sbuf(p, sm, B["yacc"][:, tt, :], xout[ts(gt, 128), :],
                                     (sem_v, sem_sc, sem_st[tt % 2]), acc_tok[tt],
                                     slot=tt % 2, store_user=list(store_user))
            store_user[tt % 2] = t_st
            yacc_user[tt] = t_o

    for ss in sem_st:
        p.raw("sync", lambda E, s2=ss.h, v=ss.n: E.wait_ge(s2, v))



def build_program(phases=("A", "B", "C", "D")):
    nc = bass.Bass("TRN2", target_bir_lowering=False)
    stack = contextlib.ExitStack()
    p = Prog(nc, stack)
    have = list(phases)

    def chain(name, prev_ph, next_ph):
        if prev_ph not in have and next_ph not in have:
            return None
        if prev_ph in have and next_ph in have:
            kind = "Internal"
        elif prev_ph in have:
            kind = "ExternalOutput"
        else:
            kind = "ExternalInput"
        return nc.dram_tensor(name, [T, D], F32, kind=kind).ap()

    x0 = (nc.dram_tensor("x0", [T, D], F32, kind="ExternalInput").ap()
          if "A" in have else None)
    x1 = chain("x1", "A", "B")
    x2 = chain("x2", "B", "C")
    x3 = chain("x3", "C", "D")
    y = (nc.dram_tensor("y", [T, D], F32, kind="ExternalOutput").ap()
         if "D" in have else None)

    consts_id = nc.dram_tensor("c_ident", [128, 128], F32, kind="ExternalInput").ap()
    consts_mask = nc.dram_tensor("c_maskut", [128, 128], F32, kind="ExternalInput").ap()

    ps = stack.enter_context(nc.psum_tensor("ps", [128, 8, 512], F32))

    with stack:
        common = alloc_common(nc, stack)
        sem_cg = p.sem("constsg")
        sem_cs = p.sem("constss")
        sem_c = p.sem("consts")
        p.on("gpsimd", lambda E: E.dma_start(common["ident_b"][:], consts_id[:]),
             inc=sem_cg, k=16)
        p.on("sync", lambda E: E.dma_start(common["ident_f"][:], consts_id[:]),
             inc=sem_cs, k=16)
        p.on("sync", lambda E: E.dma_start(common["mask_ut"][:], consts_mask[:]),
             inc=sem_cs, k=16)
        p.on("vector", lambda E: E.memset(common["ones_row"][:], 1.0), inc=sem_c)
        p.on("vector", lambda E: E.memset(common["eps"][:], LN_EPS), inc=sem_c)
        for e in ENGS:
            p._wait(e, (sem_cg, sem_cg.n))
            p._wait(e, (sem_cs, sem_cs.n))
            p._wait(e, (sem_c, sem_c.n))

        if "A" in have:
            wq = nc.dram_tensor("L0_WQT", [D, D], F32, kind="ExternalInput").ap()
            wk = nc.dram_tensor("L0_WKT", [D, D], F32, kind="ExternalInput").ap()
            wv = nc.dram_tensor("L0_WVT", [D, D], F32, kind="ExternalInput").ap()
            wo = nc.dram_tensor("L0_WOT", [D, D], F32, kind="ExternalInput").ap()
            with contextlib.ExitStack() as phs:
                bufs = alloc_attn_bufs(nc, phs, ps, common, "A")
                build_attn_phase(p, bufs, x0, wq, wk, wv, wo, None, x1, "A")
                p.barrier("ph_A")
        if "B" in have:
            wB = {
                "rw": nc.dram_tensor("L0_rw", [D, 8], F32, kind="ExternalInput").ap(),
                "rb": nc.dram_tensor("L0_rb", [8], F32, kind="ExternalInput").ap(),
                "gw": nc.dram_tensor("L0_gw", [8, D, 2048], F32, kind="ExternalInput").ap(),
                "gb": nc.dram_tensor("L0_gb", [8, 2048], F32, kind="ExternalInput").ap(),
                "uw": nc.dram_tensor("L0_uw", [8, D, 2048], F32, kind="ExternalInput").ap(),
                "ub": nc.dram_tensor("L0_ub", [8, 2048], F32, kind="ExternalInput").ap(),
                "dw": nc.dram_tensor("L0_dw", [8, 2048, D], F32, kind="ExternalInput").ap(),
                "db": nc.dram_tensor("L0_db", [8, D], F32, kind="ExternalInput").ap(),
            }
            with contextlib.ExitStack() as phs:
                bufs = alloc_moe0_bufs(nc, phs, ps, common)
                build_moe0_phase(p, bufs, x1, x2, wB, "B")
                p.barrier("ph_B")
        if "C" in have:
            wqkv = nc.dram_tensor("L1_WqkvT", [D, 3 * D], F32, kind="ExternalInput").ap()
            wout = nc.dram_tensor("L1_WoutT", [D, D], F32, kind="ExternalInput").ap()
            bo = nc.dram_tensor("L1_bout", [D], F32, kind="ExternalInput").ap()
            with contextlib.ExitStack() as phs:
                bufs = alloc_attn_bufs(nc, phs, ps, common, "C")
                build_attn_phase(p, bufs, x2, wqkv[:, 0:D], wqkv[:, D:2 * D],
                                 wqkv[:, 2 * D:3 * D], wout, bo, x3, "C")
                p.barrier("ph_C")
        if "D" in have:
            wD = {
                "w1": nc.dram_tensor("L1_w1", [8, D, 4096], F32, kind="ExternalInput").ap(),
                "b1": nc.dram_tensor("L1_b1", [8, 4096], F32, kind="ExternalInput").ap(),
                "w2": nc.dram_tensor("L1_w2", [8, 4096, D], F32, kind="ExternalInput").ap(),
                "b2": nc.dram_tensor("L1_b2", [8, D], F32, kind="ExternalInput").ap(),
            }
            with contextlib.ExitStack() as phs:
                bufs = alloc_ffn1_bufs(nc, phs, ps, common)
                build_ffn1_phase(p, bufs, x3, y, wD, "D")
                p.barrier("ph_D")

        p.finalize()
    return nc



def _consts():
    return {
        "c_ident": np.eye(128, dtype=np.float32),
        "c_maskut": np.triu(np.full((128, 128), NEG_BIG, dtype=np.float32), 1),
    }


def _prep_attn_weights(inputs):
    f = np.float32
    return {
        "L0_WQT": np.ascontiguousarray(np.asarray(inputs["L0_attn_W_Q"], dtype=f).T),
        "L0_WKT": np.ascontiguousarray(np.asarray(inputs["L0_attn_W_K"], dtype=f).T),
        "L0_WVT": np.ascontiguousarray(np.asarray(inputs["L0_attn_W_V"], dtype=f).T),
        "L0_WOT": np.ascontiguousarray(np.asarray(inputs["L0_attn_W_O"], dtype=f).T),
        "L1_WqkvT": np.ascontiguousarray(np.asarray(inputs["L1_attn_Wqkv"], dtype=f).T),
        "L1_WoutT": np.ascontiguousarray(np.asarray(inputs["L1_attn_Wout"], dtype=f).T),
        "L1_bout": np.ascontiguousarray(np.asarray(inputs["L1_attn_bout"], dtype=f)),
    }


def _f32(a):
    return np.ascontiguousarray(np.asarray(a, dtype=np.float32))


def kernel(**inputs):
    x = _f32(inputs["L0_attn_x"])
    n_cores = 8
    shards = [np.ascontiguousarray(x[2 * c:2 * c + 2].reshape(T, D))
              for c in range(n_cores)]
    consts = _consts()
    aw = _prep_attn_weights(inputs)

    wA = {k: aw[k] for k in ("L0_WQT", "L0_WKT", "L0_WVT", "L0_WOT")}
    wB = {
        "L0_rw": _f32(inputs["L0_ffn_router_w"]),
        "L0_rb": _f32(inputs["L0_ffn_router_b"]),
        "L0_gw": _f32(inputs["L0_ffn_gate_w"]),
        "L0_gb": _f32(inputs["L0_ffn_gate_b"]),
        "L0_uw": _f32(inputs["L0_ffn_up_w"]),
        "L0_ub": _f32(inputs["L0_ffn_up_b"]),
        "L0_dw": _f32(inputs["L0_ffn_down_w"]),
        "L0_db": _f32(inputs["L0_ffn_down_b"]),
    }
    wC = {k: aw[k] for k in ("L1_WqkvT", "L1_WoutT", "L1_bout")}
    wD = {
        "L1_w1": _f32(inputs["L1_ffn_W1"]),
        "L1_b1": _f32(inputs["L1_ffn_b1"]),
        "L1_w2": _f32(inputs["L1_ffn_W2"]),
        "L1_b2": _f32(inputs["L1_ffn_b2"]),
    }

    cores = list(range(n_cores))

    nc = build_program(("A", "B", "C", "D"))
    res = run_bass_kernel_spmd(
        nc,
        [dict(x0=shards[c], **wA, **wB, **wC, **wD, **consts) for c in cores],
        core_ids=cores)

    out = np.concatenate(
        [r["y"].reshape(2, S_LEN, D) for r in res.results], axis=0)
    return np.ascontiguousarray(out, dtype=np.float32)
